# version 08
# baseline (speedup 1.0000x reference)
import os
import numpy as np
from contextlib import ExitStack

KDBG = os.environ.get("KDBG", "0") == "1"
KSER = os.environ.get("KSER", "0") == "1"
KSERC = int(os.environ.get("KSERC", "1"))
KSERV = int(os.environ.get("KSERV", "2048"))

import concourse.bass as bass
import concourse.mybir as mybir
from concourse.bass_utils import run_bass_kernel_spmd

F32 = mybir.dt.float32
BF16 = mybir.dt.bfloat16
I8 = mybir.dt.int8
AF = mybir.ActivationFunctionType
ALU = mybir.AluOpType

MAGIC = 12582912.0
EPS_LN = 1e-5
EPS_FFN = 1e-5
ONE_OVER_D = 1.0 / 1024.0

L = 2048
D = 1024
DI = 2048
E = 4096
NST = 16
RNK = 64
NQ = 4
LQ = 512
KD = 8
GI = 16
NEB = 32


class Sched:

    def __init__(self):
        self.ops = []
        self.cnt = {}

    CLS = {"sync": 1, "tensor": 2, "vector": 4, "scalar": 8, "gpsimd": 16}
    stage = 32

    def op(self, eng, fn, waits=(), inc=None, amt=1):
        ser = KSER or (KSERC & self.CLS[eng]) or (
            eng == "vector" and (KSERV & self.stage))
        snap = dict(self.cnt) if ser else None
        ws = tuple(w for w in waits if w is not None)
        if inc == "s_in" and self.cnt.get("s_in"):
            ws = ws + (("s_in", self.cnt["s_in"]),)
        if inc == "s_gd" and self.cnt.get("s_gd"):
            ws = ws + (("s_gd", self.cnt["s_gd"]),)
        ev = None
        if inc is not None:
            self.cnt[inc] = self.cnt.get(inc, 0) + amt
            ev = (inc, self.cnt[inc])
        if snap:
            ws = ws + tuple(snap.items())
        self.ops.append((eng, fn, ws, inc, amt))
        return ev


def build_nc():
    nc = bass.Bass("TRN2", target_bir_lowering=False,
                   detect_race_conditions=KDBG)

    dx = nc.dram_tensor("x_fm", [D, L], F32, kind="ExternalInput")
    dwin = nc.dram_tensor("winT", [D, E], F32, kind="ExternalInput")
    dwout = nc.dram_tensor("woutT", [DI, D], F32, kind="ExternalInput")
    dwx = nc.dram_tensor("wxT_t", [128, GI * 96], F32, kind="ExternalInput")
    dwdt = nc.dram_tensor("wdtT", [128, DI], F32, kind="ExternalInput")
    dconvd = nc.dram_tensor("convd", [128, GI * 4 * 128], F32,
                            kind="ExternalInput")
    dsel = nc.dram_tensor("sel_bf", [128, 37 * 128], BF16,
                          kind="ExternalInput")
    didf = nc.dram_tensor("ident_f32", [128, 128], F32, kind="ExternalInput")
    dalog = nc.dram_tensor("alog_fm", [128, GI * NST], F32,
                           kind="ExternalInput")
    dcols = nc.dram_tensor("cols", [128, 48], F32, kind="ExternalInput")
    dfcols = nc.dram_tensor("fcols", [128, 16], F32, kind="ExternalInput")
    dfw8 = nc.dram_tensor("fwT8", [D, D], I8, kind="ExternalInput")
    dout = nc.dram_tensor("out", [D, L], F32, kind="ExternalOutput")
    ddbg = nc.dram_tensor("dbg", [128, 2 * GI * NST], F32,
                          kind="ExternalOutput")
    dwin_bf = nc.dram_tensor("win_bf", [D, E], BF16)
    dwout_bf = nc.dram_tensor("wout_bf", [DI, D], BF16)
    dfw_bf = nc.dram_tensor("fw_bf", [D, D], BF16)
    dcvd_bf = nc.dram_tensor("cvd_bf", [128, GI * 4 * 128], BF16)
    dxres = nc.dram_tensor("xres", [D, L], F32)
    dccin = nc.dram_tensor("cc_in", [1, 1], F32)
    dccout = nc.dram_tensor("cc_out", [1, 1], F32)

    es = ExitStack()

    def sbt(name, shape, dt=F32):
        return es.enter_context(nc.sbuf_tensor(name, shape, dt))

    x_sb = sbt("x_sb", [128, KD * LQ])
    t1_sb = sbt("t1_sb", [128, KD * LQ], BF16)
    xcr_sb = sbt("xcr_sb", [128, 4 * (LQ + 3)], BF16)
    halo_sb = sbt("halo_sb", [128, GI * 3], BF16)
    g_sb = sbt("g_sb", [128, GI * LQ], BF16)
    xcv_sb = sbt("xcv_sb", [128, GI * LQ], BF16)
    dt_sb = sbt("dt_sb", [128, GI * LQ], BF16)
    y_sb = sbt("y_sb", [128, GI * LQ], BF16)
    rep_sb = sbt("rep_sb", [128, 32 * LQ], BF16)
    xdbl_sb = sbt("xdbl_sb", [128, LQ], BF16)
    da_sb = sbt("da_sb", [128, 2 * LQ], BF16)
    dbx_sb = sbt("dbx_sb", [128, 2 * LQ], BF16)
    h_sb = sbt("h_sb", [128, 2 * LQ], BF16)
    z_sb = sbt("z_sb", [128, 2 * LQ], BF16)
    carry_sb = sbt("carry_sb", [128, 2 * GI * NST], BF16)
    inv_sb = sbt("inv_sb", [128, LQ])
    etmp_sb = sbt("etmp_sb", [128, 2 * LQ])
    lnt_sb = sbt("lnt_sb", [128, LQ])
    m2pad_sb = sbt("m2pad_sb", [128, L], BF16)
    padtmp_sb = sbt("padtmp_sb", [1, 2 * LQ], BF16)
    qrow_sb = sbt("qrow_sb", [1, LQ], BF16)
    qcol_sb = sbt("qcol_sb", [128, 2])
    smal_sb = sbt("smal_sb", [1, 16])
    gpad_sb = sbt("gpad_sb", [128, 4], BF16)
    iscol_sb = sbt("iscol_sb", [128, 1])
    wsx_sb = sbt("wsx_sb", [128, 8])
    fc_sb = sbt("fc_sb", [128, 16])
    a_sb = sbt("a_sb", [128, GI * NST])
    cols_sb = sbt("cols_sb", [128, 48])
    cc_sb = sbt("cc_sb", [128, 4])
    sel_sb = sbt("sel_sb", [128, 37 * 128], BF16)
    idf_sb = sbt("idf_sb", [128, 128])
    onesd_sb = sbt("onesd_sb", [128, 128], BF16)
    wx_sb = sbt("wx_sb", [128, GI * 96], BF16)
    wdt_sb = sbt("wdt_sb", [128, DI], BF16)
    cvrg_sb = sbt("cvrg_sb", [128, 2 * 1024], BF16)
    winrg_sb = sbt("winrg_sb", [128, 3 * 2048], BF16)
    wotrg_sb = sbt("wotrg_sb", [128, 2 * 2048], BF16)
    i8t_sb = sbt("i8t_sb", [128, 2 * 1024], I8)

    ps = [es.enter_context(nc.psum_tensor(f"pb{i}", [128, LQ], F32))
          for i in range(8)]

    sem_names = ["s_in", "s_gd", "s_t", "s_v", "s_s", "s_g", "s_cc"]
    sems = {n: es.enter_context(nc.semaphore(n)) for n in sem_names}

    S = Sched()
    AP = bass.AP

    bank_free = [None] * 8

    idb = lambda: sel_sb[:, 32 * 128:33 * 128]

    acol = lambda g, n: a_sb[:, g * NST + n:g * NST + n + 1]
    convb_c = lambda g: cols_sb[:, g:g + 1]
    bdt_c = lambda g: cols_sb[:, 16 + g:16 + g + 1]
    dp_c = lambda g: cols_sb[:, 32 + g:32 + g + 1]

    xk = lambda k: x_sb[:, k * LQ:(k + 1) * LQ]
    t1k = lambda k: t1_sb[:, k * LQ:(k + 1) * LQ]
    xcs = lambda g, a, b: xcr_sb[:, (g % 4) * (LQ + 3) + a:
                                 (g % 4) * (LQ + 3) + b]
    gg = lambda g: g_sb[:, g * LQ:(g + 1) * LQ]
    xcvg = lambda g: xcv_sb[:, g * LQ:(g + 1) * LQ]
    dtg = lambda g: dt_sb[:, g * LQ:(g + 1) * LQ]
    yg = lambda g: y_sb[:, g * LQ:(g + 1) * LQ]
    repB = lambda n: rep_sb[:, n * LQ:(n + 1) * LQ]
    repC = lambda n: rep_sb[:, (16 + n) * LQ:(17 + n) * LQ]
    selB = lambda n: sel_sb[:, n * 128:(n + 1) * 128]
    selC = lambda n: sel_sb[:, (16 + n) * 128:(17 + n) * 128]
    selR = lambda j: sel_sb[:, (33 + j) * 128:(34 + j) * 128]
    da_s = lambda i: da_sb[:, (i % 2) * LQ:(i % 2 + 1) * LQ]
    dbx_s = lambda i: dbx_sb[:, (i % 2) * LQ:(i % 2 + 1) * LQ]
    h_s = lambda i: h_sb[:, (i % 2) * LQ:(i % 2 + 1) * LQ]
    z_s = lambda i: z_sb[:, (i % 2) * LQ:(i % 2 + 1) * LQ]
    et_s = lambda i: etmp_sb[:, (i % 2) * LQ:(i % 2 + 1) * LQ]
    def wintile(eb, k):
        base = ((eb // 2) % 3) * 2048
        off = base + k * 256 + (eb % 2) * 128
        return winrg_sb[:, off:off + 128]
    wotslot = lambda s: wotrg_sb[:, (s % 2) * 2048:(s % 2 + 1) * 2048]
    cvslot = lambda g: cvrg_sb[:, ((g // 2) % 2) * 1024 + (g % 2) * 512:
                                ((g // 2) % 2) * 1024 + (g % 2 + 1) * 512]

    e_sel = S.op("sync", lambda e: e.dma_start(out=sel_sb[:], in_=dsel.ap()),
                 inc="s_in", amt=16)
    e_idf = S.op("sync", lambda e: e.dma_start(out=idf_sb[:], in_=didf.ap()),
                 inc="s_in", amt=16)
    e_alog = S.op("sync", lambda e: e.dma_start(
        out=x_sb[:, 0:GI * NST], in_=dalog.ap()), inc="s_in", amt=16)
    e_cols = S.op("sync", lambda e: e.dma_start(out=cols_sb[:],
                  in_=dcols.ap()), inc="s_in", amt=16)
    e_fcols = S.op("sync", lambda e: e.dma_start(out=fc_sb[:],
                   in_=dfcols.ap()), inc="s_in", amt=16)
    e_wx = S.op("sync", lambda e: e.dma_start(
        out=x_sb[:, 512:512 + GI * 96], in_=dwx.ap()), inc="s_in", amt=16)
    e_wdt = S.op("sync", lambda e: e.dma_start(
        out=AP(x_sb, 2048, [[KD * LQ, 128], [1, DI]]), in_=dwdt.ap()),
        inc="s_in", amt=16)
    e_cc0 = S.op("vector", lambda e: e.memset(cc_sb[:, 0:1], EPS_LN), inc="s_v")
    e_cc1 = S.op("vector", lambda e: e.memset(cc_sb[:, 1:2], 1.0), inc="s_v")
    e_cc2 = S.op("vector", lambda e: e.memset(cc_sb[:, 2:3], -0.5), inc="s_v")
    e_cc3 = S.op("vector", lambda e: e.memset(cc_sb[:, 3:4], -1.0),
                 inc="s_v")
    e_a1 = S.op("scalar", lambda e: e.activation(
        out=a_sb[:], in_=x_sb[:, 0:GI * NST], func=AF.Exp),
        waits=[e_alog], inc="s_s")
    e_a2 = S.op("scalar", lambda e: e.activation(
        out=a_sb[:], in_=a_sb[:], func=AF.Copy, scale=cc_sb[:, 3:4]),
        waits=[e_cc3], inc="s_s")
    e_wxc = S.op("vector", lambda e: e.tensor_copy(
        wx_sb[:], x_sb[:, 512:512 + GI * 96]), waits=[e_wx], inc="s_v")
    e_wdtc = S.op("vector", lambda e: e.tensor_copy(
        wdt_sb[:], AP(x_sb, 2048, [[KD * LQ, 128], [1, DI]])),
        waits=[e_wdt], inc="s_v")
    e_ms1 = S.op("vector", lambda e: e.memset(onesd_sb[:], ONE_OVER_D),
                 inc="s_v")
    e_ms2 = S.op("vector", lambda e: e.memset(halo_sb[:], 0.0), inc="s_v")
    e_ms3 = S.op("vector", lambda e: e.memset(xdbl_sb[:], 0.0), inc="s_v")
    e_ms4 = S.op("vector", lambda e: e.memset(qcol_sb[:], 0.0), inc="s_v")
    e_ms5 = S.op("vector", lambda e: e.memset(gpad_sb[:], 0.0), inc="s_v")
    e_ms6 = S.op("vector", lambda e: e.memset(m2pad_sb[:], 0.0), inc="s_v")

    stg_cast_ev = {}
    stg_store_ev = {}
    stg_idx = [0]
    win_piece_ev = {}

    def _dst(src_dram):
        return {id(dconvd): dcvd_bf, id(dwin): dwin_bf,
                id(dwout): dwout_bf}[id(src_dram)]

    def _rl(src_dram):
        return {id(dconvd): GI * 4 * 128, id(dwin): E,
                id(dwout): D}[id(src_dram)]

    def stage_x(src_dram, off):
        i = stg_idx[0]
        stg_idx[0] += 1
        sl = i % 4
        evl = S.op("sync", lambda e, sl=sl, off=off, src_dram=src_dram:
                   e.dma_start(out=x_sb[:, sl * 1024:(sl + 1) * 1024],
                               in_=AP(src_dram, off,
                                      [[_rl(src_dram), 128], [1, 1024]])),
                   waits=[stg_cast_ev.get(i - 4)], inc="s_in", amt=16)
        evc = S.op("gpsimd", lambda e, sl=sl: e.tensor_copy(
            rep_sb[:, sl * 1024:(sl + 1) * 1024],
            x_sb[:, sl * 1024:(sl + 1) * 1024]),
            waits=[evl, stg_store_ev.get(i - 4)], inc="s_g")
        stg_cast_ev[i] = evc
        evs = S.op("gpsimd", lambda e, sl=sl, off=off, src_dram=src_dram:
                   e.dma_start(out=AP(_dst(src_dram), off,
                                      [[_rl(src_dram), 128], [1, 1024]]),
                               in_=rep_sb[:, sl * 1024:(sl + 1) * 1024]),
                   waits=[evc], inc="s_gd", amt=16)
        stg_store_ev[i] = evs
        return evs

    for p in range(8):
        stage_x(dconvd, p * 1024)
    e_cvd_st = {1: stg_store_ev[7]}
    for c2 in range(2):
        for j in range(KD):
            ev = stage_x(dwin, j * 128 * E + c2 * 1024)
            win_piece_ev[(j, c2 * 2)] = ev
            win_piece_ev[(j, c2 * 2 + 1)] = ev
    ph1_cast_done = stg_cast_ev[stg_idx[0] - 1]

    e_x0 = S.op("gpsimd", lambda e: e.dma_start(
        out=AP(x_sb, 0, [[KD * LQ, 128], [LQ, KD], [1, LQ]]),
        in_=AP(dx, 0, [[L, 128], [128 * L, KD], [1, LQ]])),
        waits=[e_a1, e_wxc, e_wdtc, ph1_cast_done], inc="s_gd", amt=16)

    fstage = [lnt_sb[:, 0:LQ], inv_sb[:, 0:LQ],
              etmp_sb[:, 0:LQ], etmp_sb[:, LQ:2 * LQ]]

    def stage(src_dram, off):
        i = stg_idx[0]
        stg_idx[0] += 1
        sl = i % 4
        evl = S.op("sync", lambda e, sl=sl, off=off, src_dram=src_dram:
                   e.dma_start(out=fstage[sl],
                               in_=AP(src_dram, off,
                                      [[_rl(src_dram), 128], [1, LQ]])),
                   waits=[stg_cast_ev.get(i - 4)], inc="s_in", amt=16)
        evc = S.op("gpsimd", lambda e, sl=sl: e.tensor_copy(
            rep_sb[:, sl * LQ:(sl + 1) * LQ], fstage[sl]),
            waits=[evl, stg_store_ev.get(i - 4)], inc="s_g")
        stg_cast_ev[i] = evc
        evs = S.op("gpsimd", lambda e, sl=sl, off=off, src_dram=src_dram:
                   e.dma_start(out=AP(_dst(src_dram), off,
                                      [[_rl(src_dram), 128], [1, LQ]]),
                               in_=rep_sb[:, sl * LQ:(sl + 1) * LQ]),
                   waits=[evc], inc="s_gd", amt=16)
        stg_store_ev[i] = evs
        return evs

    for c in range(4, 8):
        for j in range(KD):
            win_piece_ev[(j, c)] = stage(dwin, j * 128 * E + c * LQ)
    win_store_last = win_piece_ev[(KD - 1, 7)]
    wo_ev = {}
    for j in range(16):
        for c in range(2):
            wo_ev[(j, c)] = stage(dwout, j * 128 * D + c * LQ)
    wo_store = {15: wo_ev[(15, 1)]}
    fw_cast = {}
    fw_store = {}
    for j in range(8):
        evl = S.op("sync", lambda e, j=j: e.dma_start(
            out=i8t_sb[:, (j % 2) * 1024:(j % 2 + 1) * 1024],
            in_=AP(dfw8, j * 128 * D, [[D, 128], [1, D]])),
            waits=[fw_cast.get(j - 2)], inc="s_in", amt=16)
        fw_cast[j] = S.op("gpsimd", lambda e, j=j: e.tensor_copy(
            rep_sb[:, 8192 + (j % 2) * 1024:8192 + (j % 2 + 1) * 1024],
            i8t_sb[:, (j % 2) * 1024:(j % 2 + 1) * 1024]),
            waits=[evl, fw_store.get(j - 2),
                   stg_cast_ev[stg_idx[0] - 1] if j == 0 else None],
            inc="s_g")
        fw_store[j] = S.op("gpsimd", lambda e, j=j: e.dma_start(
            out=AP(dfw_bf, j * 128 * D, [[D, 128], [1, D]]),
            in_=rep_sb[:, 8192 + (j % 2) * 1024:8192 + (j % 2 + 1) * 1024]),
            waits=[fw_cast[j]], inc="s_gd", amt=16)

    startup_x_free = []
    ev_fstage_done = stg_cast_ev[stg_idx[0] - 1]
    ev_stores_done = stg_store_ev[stg_idx[0] - 1]
    rep_startup_guard = [fw_store[7], fw_cast[7], ev_stores_done]
    startup_done = [e_sel, e_idf, e_a2, e_cols, e_fcols, e_ms1]

    last_win_rd = {}
    last_wot_rd = {}
    last_cv_rd = {}
    prev_conv_done = None
    prev_scan_slot = {}
    prev_rep_reader = None
    prev_da_last = None
    ev_xres_store = {}
    prev_xguard = startup_x_free + [e_ms2, e_ms3] + startup_done
    prev_rowdma = None
    prev_inproj_last = None
    qrow_tmp = [None]

    for q in range(NQ):
        qrow_tmp[0] = qrow_sb[0:1, 0:LQ]
        S.stage = 1
        if q == 0:
            e_x = e_x0
        else:
            e_x = S.op("gpsimd", lambda e, q=q: e.dma_start(
                out=AP(x_sb, 0, [[KD * LQ, 128], [LQ, KD], [1, LQ]]),
                in_=AP(dx, q * LQ, [[L, 128], [128 * L, KD], [1, LQ]])),
                waits=prev_xguard, inc="s_gd", amt=16)
        e_cast = None
        for k in range(KD):
            e_cast = S.op("vector", lambda e, k=k: e.tensor_copy(
                dtg(k), xk(k)), waits=[e_x, prev_da_last], inc="s_v")
        e_m = None
        for k in range(KD):
            e_m = S.op("tensor", lambda e, k=k: e.matmul(
                ps[0][:], onesd_sb[:], dtg(k),
                start=(k == 0), stop=(k == KD - 1)),
                waits=[e_cast, bank_free[0], e_ms1], inc="s_t")
        e_t1 = None
        for k in range(KD):
            e_t1 = S.op("vector", lambda e, k=k: e.tensor_tensor(
                out=t1k(k), in0=xk(k), in1=ps[0][:], op=ALU.subtract),
                waits=[e_m], inc="s_v")
        e_sq = None
        for k in range(KD):
            e_sq = S.op("scalar", lambda e, k=k: e.activation(
                out=dtg(k), in_=t1k(k), func=AF.Square),
                waits=[e_t1, e_m], inc="s_s")
        e_v = None
        for k in range(KD):
            e_v = S.op("tensor", lambda e, k=k: e.matmul(
                ps[1][:], onesd_sb[:], dtg(k),
                start=(k == 0), stop=(k == KD - 1)),
                waits=[e_sq, bank_free[1]], inc="s_t")
        e_ln1 = S.op("scalar", lambda e: e.activation(
            out=lnt_sb[:], in_=ps[1][:], func=AF.Ln, bias=cc_sb[:, 0:1]),
            waits=[e_v, e_cc0, e_cc2,
                   ev_fstage_done if q == 0 else None], inc="s_s")
        e_inv1 = S.op("scalar", lambda e: e.activation(
            out=inv_sb[:], in_=lnt_sb[:], func=AF.Exp, scale=cc_sb[:, 2:3]),
            inc="s_s")
        bank_free[1] = e_ln1
        e_xln = None
        for k in range(KD):
            e_xln = S.op("vector", lambda e, k=k: e.tensor_tensor(
                out=t1k(k), in0=t1k(k), in1=inv_sb[:], op=ALU.mult),
                waits=[e_inv1], inc="s_v")
        bank_free[0] = e_t1

        S.stage = 2
        xc_rdy = {}
        cv_last_ld = [None]
        cv_mm_ev = {}
        g_done = {}
        xconv_done = {}
        ip_free = {2: bank_free[2], 3: bank_free[3]}
        cv_free = {4: bank_free[4], 5: bank_free[5]}
        e_conv_mm_last = None

        def emit_conv(g, q=q):
            nonlocal e_conv_mm_last
            bk = 4 + g % 2
            if g % 2 == 0:
                cv_last_ld[0] = S.op("sync", lambda e, g=g: e.dma_start(
                    out=cvrg_sb[:, ((g // 2) % 2) * 1024:
                                ((g // 2) % 2 + 1) * 1024],
                    in_=AP(dcvd_bf, g * 4 * 128,
                           [[GI * 4 * 128, 128], [1, 1024]])),
                    waits=[last_cv_rd.get((g // 2) % 2),
                           e_cvd_st[1] if q == 0 and g < 2 else None],
                    inc="s_in", amt=16)
            e_cl = cv_last_ld[0]
            e_cm = None
            for kk in range(4):
                e_cm = S.op("tensor", lambda e, g=g, kk=kk, bk=bk: e.matmul(
                    ps[bk][:], cvslot(g)[:, kk * 128:(kk + 1) * 128],
                    xcs(g, kk, kk + LQ),
                    start=(kk == 0), stop=(kk == 3)),
                    waits=[xc_rdy[g], e_cl, cv_free.get(bk)], inc="s_t")
            last_cv_rd[(g // 2) % 2] = e_cm
            cv_mm_ev[g] = e_cm
            e_cs = S.op("scalar", lambda e, g=g, bk=bk: e.activation(
                out=xcvg(g), in_=ps[bk][:], func=AF.Silu,
                bias=convb_c(g)), waits=[e_cm, e_cols], inc="s_s")
            cv_free[bk] = e_cs
            xconv_done[g] = e_cs
            e_conv_mm_last = e_cm

        last_ld = [None]
        for eb in range(NEB):
            if eb >= 4 and (eb - 4) < GI:
                emit_conv(eb - 4)
            if eb % 2 == 0:
                last_ld[0] = S.op("sync", lambda e, eb=eb: e.dma_start(
                    out=AP(winrg_sb, ((eb // 2) % 3) * 2048,
                           [[3 * 2048, 128], [256, KD], [1, 256]]),
                    in_=AP(dwin_bf, eb * 128,
                           [[E, 128], [128 * E, KD], [1, 256]])),
                    waits=[last_win_rd.get((eb // 2) % 3),
                           win_piece_ev[(KD - 1, (eb + 1) // 4)]
                           if q == 0 else None],
                    inc="s_in", amt=16)
            e_ld = last_ld[0]
            bk = 2 + eb % 2
            e_mm = None
            for k in range(KD):
                e_mm = S.op("tensor", lambda e, eb=eb, k=k, bk=bk: e.matmul(
                    ps[bk][:], wintile(eb, k), t1k(k),
                    start=(k == 0), stop=(k == KD - 1)),
                    waits=[e_ld, e_xln, ip_free.get(bk)], inc="s_t")
            last_win_rd[(eb // 2) % 3] = e_mm
            if eb < 16:
                g = eb
                slot_guard = cv_mm_ev.get(g - 4)
                if q > 0 and g < 4:
                    slot_guard = prev_conv_done
                e_e = S.op("vector", lambda e, g=g, bk=bk: e.tensor_copy(
                    xcs(g, 3, LQ + 3), ps[bk][:]),
                    waits=[e_mm, slot_guard], inc="s_v")
                S.op("vector", lambda e, g=g: e.tensor_copy(
                    xcs(g, 0, 3), halo_sb[:, g * 3:g * 3 + 3]),
                    waits=[e_ms2], inc="s_v")
                if q < NQ - 1:
                    e_sv = S.op("vector", lambda e, g=g: e.tensor_copy(
                        halo_sb[:, g * 3:g * 3 + 3], xcs(g, LQ, LQ + 3)),
                        inc="s_v")
                else:
                    e_sv = S.op("vector", lambda e: e.engine_nop(),
                                inc="s_v")
                xc_rdy[g] = e_sv
                ip_free[bk] = e_e
            else:
                e_e = S.op("scalar", lambda e, eb=eb, bk=bk: e.activation(
                    out=gg(eb - 16), in_=ps[bk][:], func=AF.Silu),
                    waits=[e_mm], inc="s_s")
                g_done[eb - 16] = e_e
                ip_free[bk] = e_e
        for g in range(GI - 4, GI):
            emit_conv(g)
        bank_free[2] = ip_free[2]
        bank_free[3] = ip_free[3]
        bank_free[4] = cv_free[4]
        bank_free[5] = cv_free[5]
        prev_conv_done = e_conv_mm_last
        prev_inproj_last = last_win_rd[((NEB - 1) // 2) % 3]

        e_wxm = None
        for g in range(GI):
            e_wxm = S.op("tensor", lambda e, g=g: e.matmul(
                ps[6][0:96, :], wx_sb[:, g * 96:(g + 1) * 96], xcvg(g),
                start=(g == 0), stop=(g == GI - 1)),
                waits=[xconv_done[g], e_wxc, bank_free[6]], inc="s_t")
        e_xdbl = S.op("vector", lambda e: e.tensor_copy(
            xdbl_sb[0:96, :], ps[6][0:96, :]),
            waits=[e_wxm, e_ms3], inc="s_v")
        bank_free[6] = e_xdbl

        dt_done = {}
        dt_free = {7: bank_free[7], 0: bank_free[0]}
        for g in range(GI):
            bk = 7 if g % 2 == 0 else 0
            e_dm = S.op("tensor", lambda e, g=g, bk=bk: e.matmul(
                ps[bk][:], wdt_sb[:, g * 128:(g + 1) * 128],
                xdbl_sb[:], start=True, stop=True),
                waits=[e_xdbl, e_wdtc, dt_free.get(bk),
                       e_inv1 if bk == 0 else None], inc="s_t")
            e_de = S.op("scalar", lambda e, g=g, bk=bk: e.activation(
                out=et_s(g), in_=ps[bk][:], func=AF.Exp, bias=bdt_c(g)),
                waits=[e_dm, ev_fstage_done if q == 0 and g < 2 else None],
                inc="s_s")
            e_dl = S.op("scalar", lambda e, g=g: e.activation(
                out=dtg(g), in_=et_s(g), func=AF.Ln, bias=cc_sb[:, 1:2]),
                waits=[e_cc1], inc="s_s")
            dt_free[bk] = e_de
            dt_done[g] = e_dl
        bank_free[7] = dt_free[7]
        bank_free[0] = dt_free[0]

        xcd_done = {}
        u_done = {}
        for g in range(GI):
            xcd_done[g] = S.op("vector", lambda e, g=g: e.tensor_scalar(
                out=yg(g), in0=xcvg(g), scalar1=dp_c(g), scalar2=None,
                op0=ALU.mult), waits=[xconv_done[g], e_cols], inc="s_v")
            u_done[g] = S.op("vector", lambda e, g=g: e.tensor_tensor(
                out=xcvg(g), in0=dtg(g), in1=xcvg(g), op=ALU.mult),
                waits=[dt_done[g]], inc="s_v")

        repB_ev = {}
        repC_ev = {}
        rb_free = bank_free[6]
        rc_free = bank_free[7]
        for n in range(NST):
            e_rb = S.op("tensor", lambda e, n=n: e.matmul(
                ps[6][:], selB(n), xdbl_sb[:], start=True, stop=True),
                waits=[e_xdbl, e_sel, rb_free], inc="s_t")
            e_rbe = S.op("scalar", lambda e, n=n: e.copy(
                out=repB(n), in_=ps[6][:]),
                waits=[e_rb, prev_rep_reader] +
                      (rep_startup_guard if q == 0 else []), inc="s_s")
            e_rc = S.op("tensor", lambda e, n=n: e.matmul(
                ps[7][:], selC(n), xdbl_sb[:], start=True, stop=True),
                waits=[rc_free], inc="s_t")
            e_rce = S.op("scalar", lambda e, n=n: e.copy(
                out=repC(n), in_=ps[7][:]),
                waits=[e_rc, prev_rep_reader] +
                      (rep_startup_guard if q == 0 else []), inc="s_s")
            rb_free, rc_free = e_rbe, e_rce
            repB_ev[n] = e_rbe
            repC_ev[n] = e_rce
        bank_free[6] = rb_free
        bank_free[7] = rc_free

        S.stage = 4
        gate_ev = {}
        scan_ev = {}
        zmm_ev = {}
        da_ev = {}
        y_free = {2: bank_free[2], 3: bank_free[3]}

        def emit_da(j, q=q):
            if j >= GI * NST:
                return
            gj, nj = j // NST, j % NST
            da_ev[j] = S.op("scalar", lambda e, gj=gj, nj=nj, j=j:
                            e.activation(out=da_s(j), in_=dtg(gj),
                                         func=AF.Exp, scale=acol(gj, nj)),
                            waits=[dt_done[gj], e_a2,
                                   scan_ev.get(j - 2,
                                               prev_scan_slot.get(j % 2))],
                            inc="s_s")

        emit_da(0)
        emit_da(1)
        ii = 0
        for g in range(GI):
            bk = 2 + g % 2
            S.op("tensor", lambda e, g=g, bk=bk: e.matmul(
                ps[bk][:], idb(), yg(g), start=True, stop=False),
                waits=[xcd_done[g], y_free.get(bk)], inc="s_t")
            for n in range(NST):
                i = ii
                ii += 1
                e_da = da_ev[i]
                S.stage = 64
                S.op("vector", lambda e, g=g, n=n, i=i: e.tensor_tensor(
                    out=dbx_s(i), in0=xcvg(g), in1=repB(n), op=ALU.mult),
                    waits=[u_done[g], repB_ev[n]], inc="s_v")
                S.stage = 128
                if q == 0:
                    e_h = S.op("vector", lambda e, i=i:
                               e.tensor_tensor_scan(
                                   out=h_s(i), data0=da_s(i), data1=dbx_s(i),
                                   initial=0.0, op0=ALU.mult, op1=ALU.add),
                               waits=[e_da], inc="s_v")
                else:
                    cr = ((q - 1) % 2) * GI * NST + g * NST + n
                    e_h = S.op("vector", lambda e, cr=cr, i=i:
                               e.tensor_tensor_scan(
                                   out=h_s(i), data0=da_s(i), data1=dbx_s(i),
                                   initial=carry_sb[:, cr:cr + 1],
                                   op0=ALU.mult, op1=ALU.add),
                               waits=[e_da], inc="s_v")
                scan_ev[i] = e_h
                emit_da(i + 2)
                S.stage = 256
                if q < NQ - 1:
                    cw = (q % 2) * GI * NST + g * NST + n
                    S.op("scalar", lambda e, cw=cw, i=i: e.copy(
                        out=carry_sb[:, cw:cw + 1],
                        in_=h_s(i)[:, LQ - 1:LQ]),
                        waits=[scan_ev[i]], inc="s_s")
                S.stage = 512
                e_z = S.op("vector", lambda e, n=n, i=i: e.tensor_tensor(
                    out=z_s(i), in0=h_s(i), in1=repC(n), op=ALU.mult),
                    waits=[repC_ev[n], zmm_ev.get(i - 2)], inc="s_v")
                zmm_ev[i] = S.op("tensor",
                                 lambda e, i=i, n=n, bk=bk: e.matmul(
                                     ps[bk][:], idb(), z_s(i),
                                     start=False, stop=(n == NST - 1)),
                                 waits=[e_z], inc="s_t")
            S.stage = 1024
            e_gate = S.op("vector", lambda e, g=g, bk=bk: e.tensor_tensor(
                out=yg(g), in0=ps[bk][:], in1=gg(g), op=ALU.mult),
                waits=[zmm_ev[ii - 1], g_done[g]], inc="s_v")
            S.stage = 4
            y_free[bk] = e_gate
            gate_ev[g] = e_gate
        bank_free[2] = y_free[2]
        bank_free[3] = y_free[3]
        prev_scan_slot = {0: scan_ev[ii - 2], 1: scan_ev[ii - 1]}
        prev_rep_reader = gate_ev[GI - 1]
        prev_da_last = da_ev[GI * NST - 1]

        S.stage = 8
        op_evac = {}
        op_free = {4: bank_free[4], 5: bank_free[5]}
        for dm in range(KD):
            e_wol = S.op("gpsimd", lambda e, dm=dm: e.dma_start(
                out=AP(wotrg_sb, (dm % 2) * 2048,
                       [[2 * 2048, 128], [128, GI], [1, 128]]),
                in_=AP(dwout_bf, dm * 128,
                       [[D, 128], [128 * D, GI], [1, 128]])),
                waits=[last_wot_rd.get(dm % 2)] +
                      ([wo_store[15]] if q == 0 and dm < 2 else []),
                inc="s_gd", amt=16)
            bk = 4 + dm % 2
            e_om = None
            for g in range(GI):
                e_om = S.op("tensor", lambda e, dm=dm, g=g, bk=bk: e.matmul(
                    ps[bk][:], wotslot(dm)[:, g * 128:(g + 1) * 128], yg(g),
                    start=(g == 0), stop=(g == GI - 1)),
                    waits=[e_wol, gate_ev[g], op_free.get(bk)], inc="s_t")
            last_wot_rd[dm % 2] = e_om
            e_oe = S.op("vector", lambda e, dm=dm, bk=bk: e.tensor_tensor(
                out=xk(dm), in0=ps[bk][:], in1=xk(dm), op=ALU.add),
                waits=[e_om], inc="s_v")
            op_evac[dm] = e_oe
            op_free[bk] = e_oe
        bank_free[4] = op_free[4]
        bank_free[5] = op_free[5]

        e_cast2 = None
        for k in range(KD):
            e_cast2 = S.op("vector", lambda e, k=k: e.tensor_copy(
                dtg(k), xk(k)), waits=[op_evac[k], prev_da_last], inc="s_v")
        e_m2 = None
        for k in range(KD):
            e_m2 = S.op("tensor", lambda e, k=k: e.matmul(
                ps[0][:], onesd_sb[:], dtg(k),
                start=(k == 0), stop=(k == KD - 1)),
                waits=[e_cast2, bank_free[0]], inc="s_t")
        e_sq2 = None
        for k in range(KD):
            e_sq2 = S.op("scalar", lambda e, k=k: e.activation(
                out=xcvg(k), in_=dtg(k), func=AF.Square),
                waits=[e_cast2, prev_rep_reader], inc="s_s")
        e_v2 = None
        for k in range(KD):
            e_v2 = S.op("tensor", lambda e, k=k: e.matmul(
                ps[1][:], onesd_sb[:], xcvg(k),
                start=(k == 0), stop=(k == KD - 1)),
                waits=[e_sq2, bank_free[1]], inc="s_t")
        e_c12 = S.op("scalar", lambda e: e.activation(
            out=lnt_sb[:], in_=ps[0][:], func=AF.Square),
            waits=[e_m2], inc="s_s")
        e_c22 = S.op("vector", lambda e: e.tensor_tensor(
            out=lnt_sb[:], in0=ps[1][:], in1=lnt_sb[:], op=ALU.subtract),
            waits=[e_v2, e_c12], inc="s_v")
        e_ln2 = S.op("scalar", lambda e: e.activation(
            out=ps[1][:], in_=lnt_sb[:], func=AF.Ln, bias=cc_sb[:, 0:1]),
            waits=[e_c22], inc="s_s")
        e_inv2 = S.op("scalar", lambda e: e.activation(
            out=inv_sb[:], in_=ps[1][:], func=AF.Exp, scale=cc_sb[:, 2:3]),
            inc="s_s")
        bank_free[1] = e_inv2
        e_r1 = S.op("vector", lambda e, q=q: e.tensor_copy(
            m2pad_sb[0:1, q * LQ:(q + 1) * LQ], ps[0][0:1, :]),
            waits=[e_m2, e_ms6, prev_rowdma], inc="s_v")
        e_r2 = S.op("vector", lambda e, q=q: e.tensor_tensor(
            out=padtmp_sb[0:1, 0:LQ], in0=ps[0][0:1, :],
            in1=m2pad_sb[0:1, q * LQ:(q + 1) * LQ], op=ALU.subtract),
            inc="s_v")
        e_r3 = S.op("vector", lambda e, q=q: e.tensor_copy(
            padtmp_sb[0:1, LQ:2 * LQ], inv_sb[0:1, :]),
            waits=[e_inv2], inc="s_v")
        e_r4 = S.op("vector", lambda e, q=q: e.tensor_tensor(
            out=padtmp_sb[0:1, 0:LQ] if False else qrow_tmp[0],
            in0=inv_sb[0:1, :], in1=padtmp_sb[0:1, LQ:2 * LQ],
            op=ALU.subtract), inc="s_v")
        e_d1 = S.op("gpsimd", lambda e, q=q: e.dma_start(
            out=AP(m2pad_sb, 32 * L + q * LQ, [[L, 1], [1, LQ]]),
            in_=padtmp_sb[0:1, 0:LQ]), waits=[e_r2], inc="s_gd", amt=16)
        e_d2 = S.op("gpsimd", lambda e, q=q: e.dma_start(
            out=AP(m2pad_sb, 64 * L + q * LQ, [[L, 1], [1, LQ]]),
            in_=padtmp_sb[0:1, LQ:2 * LQ]), waits=[e_r3], inc="s_gd", amt=16)
        e_d3 = S.op("gpsimd", lambda e, q=q: e.dma_start(
            out=AP(m2pad_sb, 96 * L + q * LQ, [[L, 1], [1, LQ]]),
            in_=qrow_tmp[0]), waits=[e_r4], inc="s_gd", amt=16)
        prev_rowdma = e_d3
        e_qp = None
        for k in range(KD):
            S.op("vector", lambda e, k=k: e.tensor_tensor(
                out=et_s(k), in0=xk(k), in1=ps[0][:], op=ALU.subtract),
                waits=[e_inv2], inc="s_v")
            S.op("vector", lambda e, k=k: e.tensor_tensor(
                out=et_s(k), in0=et_s(k), in1=inv_sb[:], op=ALU.mult),
                inc="s_v")
            S.op("vector", lambda e, k=k: e.tensor_reduce(
                out=qcol_sb[:, 1:2], in_=et_s(k), axis=mybir.AxisListType.X,
                op=ALU.max, apply_absolute_value=True),
                waits=[e_ms4], inc="s_v")
            e_qp = S.op("vector", lambda e: e.tensor_tensor(
                out=qcol_sb[:, 0:1], in0=qcol_sb[:, 0:1],
                in1=qcol_sb[:, 1:2], op=ALU.max), inc="s_v")
        bank_free[0] = e_qp
        ev_xres_store[q] = S.op("gpsimd", lambda e, q=q: e.dma_start(
            out=AP(dxres, q * LQ, [[L, 128], [128 * L, KD], [1, LQ]]),
            in_=AP(x_sb, 0, [[KD * LQ, 128], [LQ, KD], [1, LQ]])),
            waits=[op_evac[KD - 1]], inc="s_gd", amt=16)
        prev_xguard = [ev_xres_store[q], e_qp, e_r1, e_d3]
        prev_da_last = e_sq2

    S.stage = 2048
    e_tr = S.op("tensor", lambda e: e.transpose(
        ps[6][0:1, 0:128], qcol_sb[:, 0:1], idf_sb[:]),
        waits=[e_qp, e_idf, bank_free[6]], inc="s_t")
    e_gm = S.op("vector", lambda e: e.tensor_reduce(
        out=smal_sb[0:1, 0:1], in_=ps[6][0:1, 0:128],
        axis=mybir.AxisListType.X, op=ALU.max), waits=[e_tr], inc="s_v")
    bank_free[6] = e_gm
    e_st = S.op("gpsimd", lambda e: e.dma_start(
        out=AP(dccin, 0, [[1, 1], [1, 1]]), in_=smal_sb[0:1, 0:1]),
        waits=[e_gm], inc="s_gd", amt=16)
    e_cc = S.op("gpsimd", lambda e: e.collective_compute(
        "AllReduce", ALU.max, replica_groups=[list(range(8))],
        ins=[dccin.ap().opt()], outs=[dccout.ap().opt()]),
        waits=[e_st], inc="s_cc")
    e_ccl = S.op("gpsimd", lambda e: e.dma_start(
        out=smal_sb[0:1, 2:3], in_=dccout.ap()), waits=[e_cc],
        inc="s_gd", amt=16)
    e_xs = S.op("vector", lambda e: e.tensor_scalar(
        out=smal_sb[0:1, 1:2], in0=smal_sb[0:1, 2:3], scalar1=1.0 / 127.0,
        scalar2=EPS_FFN, op0=ALU.mult, op1=ALU.max),
        waits=[e_ccl], inc="s_v")
    e_is = S.op("vector", lambda e: e.reciprocal(
        out=smal_sb[0:1, 3:4], in_=smal_sb[0:1, 1:2]), inc="s_v")
    e_gp1 = S.op("vector", lambda e: e.tensor_copy(
        gpad_sb[0:1, 0:1], smal_sb[0:1, 1:2]),
        waits=[e_ms5, e_xs], inc="s_v")
    e_gp2 = S.op("vector", lambda e: e.tensor_tensor(
        out=gpad_sb[0:1, 1:2], in0=smal_sb[0:1, 1:2], in1=gpad_sb[0:1, 0:1],
        op=ALU.subtract), inc="s_v")
    e_gp3 = S.op("vector", lambda e: e.tensor_copy(
        gpad_sb[0:1, 2:3], smal_sb[0:1, 3:4]), waits=[e_is], inc="s_v")
    e_gp4 = S.op("vector", lambda e: e.tensor_tensor(
        out=gpad_sb[0:1, 3:4], in0=smal_sb[0:1, 3:4], in1=gpad_sb[0:1, 2:3],
        op=ALU.subtract), inc="s_v")
    e_xsc = None
    for j in range(2):
        e_xsc = S.op("tensor", lambda e, j=j: e.matmul(
            ps[7][:, 0:1], selR(0), gpad_sb[:, j:j + 1],
            start=(j == 0), stop=(j == 1)),
            waits=[e_gp2, e_sel, bank_free[7]], inc="s_t")
    e_wsx = S.op("vector", lambda e: e.tensor_scalar(
        out=wsx_sb[:], in0=fc_sb[:, 0:8], scalar1=ps[7][:, 0:1],
        scalar2=None, op0=ALU.mult), waits=[e_xsc, e_fcols], inc="s_v")
    bank_free[7] = e_wsx
    e_isc = None
    for j in range(2):
        e_isc = S.op("tensor", lambda e, j=j: e.matmul(
            ps[6][:, 0:1], selR(0), gpad_sb[:, 2 + j:3 + j],
            start=(j == 0), stop=(j == 1)),
            waits=[e_gp4, bank_free[6]], inc="s_t")
    e_iscol = S.op("vector", lambda e: e.tensor_copy(
        iscol_sb[:], ps[6][:, 0:1]), waits=[e_isc], inc="s_v")
    bank_free[6] = e_iscol
    e_fw = S.op("sync", lambda e: e.dma_start(
        out=rep_sb[:, 0:8192],
        in_=AP(dfw_bf, 0, [[D, 128], [128 * D, 8], [1, D]])),
        waits=[prev_rep_reader, fw_store[7]], inc="s_in", amt=16)
    fwt = lambda k, nb: rep_sb[:, k * 1024 + nb * 128:
                               k * 1024 + (nb + 1) * 128]

    S.stage = 16
    prev_ffn_done = None
    ps0_free = bank_free[0]
    ps1_free = bank_free[1]
    ffn_free = {2: bank_free[2], 3: bank_free[3]}
    for q in range(NQ):
        e_xl = S.op("gpsimd", lambda e, q=q: e.dma_start(
            out=AP(x_sb, 0, [[KD * LQ, 128], [LQ, KD], [1, LQ]]),
            in_=AP(dxres, q * LQ, [[L, 128], [128 * L, KD], [1, LQ]])),
            waits=[e_iscol, e_wsx, prev_ffn_done, e_qp if q == 0 else None],
            inc="s_gd", amt=16)
        e_m2r = None
        for j in range(2):
            e_m2r = S.op("tensor", lambda e, q=q, j=j: e.matmul(
                ps[0][:], selR(j), m2pad_sb[:, q * LQ:(q + 1) * LQ],
                start=(j == 0), stop=(j == 1)),
                waits=[prev_rowdma, ps0_free], inc="s_t")
        e_sir = None
        for j in range(2):
            e_sir = S.op("tensor", lambda e, q=q, j=j: e.matmul(
                ps[1][:], selR(2 + j), m2pad_sb[:, q * LQ:(q + 1) * LQ],
                start=(j == 0), stop=(j == 1)),
                waits=[prev_rowdma, ps1_free], inc="s_t")
        e_sirs = S.op("vector", lambda e: e.tensor_scalar(
            out=inv_sb[:], in0=ps[1][:], scalar1=iscol_sb[:, 0:1],
            scalar2=None, op0=ALU.mult), waits=[e_sir, e_iscol], inc="s_v")
        e_xq = {}
        for k in range(KD):
            S.op("vector", lambda e, k=k: e.tensor_tensor(
                out=et_s(k), in0=xk(k), in1=ps[0][:], op=ALU.subtract),
                waits=[e_xl, e_m2r, prev_ffn_done], inc="s_v")
            S.op("vector", lambda e, k=k: e.tensor_tensor(
                out=et_s(k), in0=et_s(k), in1=inv_sb[:], op=ALU.mult),
                waits=[e_sirs], inc="s_v")
            e_xq[k] = S.op("vector", lambda e, k=k: e.tensor_scalar(
                out=dtg(k), in0=et_s(k), scalar1=MAGIC, scalar2=MAGIC,
                op0=ALU.add, op1=ALU.subtract), inc="s_v")
        ps0_free = e_xq[KD - 1]
        ps1_free = e_sirs
        e_fin = {}
        for nb in range(KD):
            bk = 2 + nb % 2
            e_fm = None
            for k in range(KD):
                e_fm = S.op("tensor", lambda e, nb=nb, k=k, bk=bk: e.matmul(
                    ps[bk][:], fwt(k, nb), dtg(k),
                    start=(k == 0), stop=(k == KD - 1)),
                    waits=[e_fw, e_xq[k], ffn_free.get(bk)], inc="s_t")
            e_fe = S.op("scalar", lambda e, nb=nb, bk=bk: e.activation(
                out=et_s(nb), in_=ps[bk][:], func=AF.Copy,
                scale=wsx_sb[:, nb:nb + 1]),
                waits=[e_fm, e_wsx, e_xq[KD - 1], e_fin.get(nb - 2)],
                inc="s_s")
            e_fa = S.op("vector", lambda e, nb=nb: e.scalar_tensor_tensor(
                out=xk(nb), in0=et_s(nb), scalar=fc_sb[:, 8 + nb:9 + nb],
                in1=xk(nb), op0=ALU.add, op1=ALU.add),
                waits=[e_fe, e_fcols], inc="s_v")
            ffn_free[bk] = e_fa
            e_fin[nb] = e_fa
        prev_ffn_done = e_fin[KD - 1]
        S.op("gpsimd", lambda e, q=q: e.dma_start(
            out=AP(dout, q * LQ, [[L, 128], [128 * L, KD], [1, LQ]]),
            in_=AP(x_sb, 0, [[KD * LQ, 128], [LQ, KD], [1, LQ]])),
            waits=[e_fin[KD - 1]], inc="s_gd", amt=16)

    S.op("vector", lambda e: e.tensor_copy(
        etmp_sb[:, 0:2 * GI * NST], carry_sb[:]), inc="s_v")
    S.op("gpsimd", lambda e: e.dma_start(
        out=ddbg.ap(), in_=etmp_sb[:, 0:2 * GI * NST]),
        waits=[("s_v", S.cnt["s_v"])], inc="s_gd", amt=16)
    final_gd = S.cnt["s_gd"]
    S.op("gpsimd", lambda e: e.nop(), waits=[("s_gd", final_gd)], inc=None)

    with es:
        block = es.enter_context(nc.Block())

        def replay(eng_name):
            def _(eng):
                hw = {}
                for (en, fn, waits, inc, amt) in S.ops:
                    if en != eng_name:
                        continue
                    for (sn, val) in waits:
                        if hw.get(sn, 0) < val:
                            eng.wait_ge(sems[sn], val)
                            hw[sn] = val
                    instr = fn(eng)
                    if inc is not None:
                        instr.then_inc(sems[inc], amt)
                    if KDBG and eng_name != "sync":
                        try:
                            eng.drain()
                        except Exception:
                            pass
            return _

        block.sync(replay("sync"))
        block.vector(replay("vector"))
        block.scalar(replay("scalar"))
        block.tensor(replay("tensor"))
        block.gpsimd(replay("gpsimd"))
    return nc


def kernel(L0_attn_x, L0_attn_W_in, L0_attn_conv_w, L0_attn_conv_b,
           L0_attn_W_x, L0_attn_W_dt, L0_attn_b_dt, L0_attn_A_log,
           L0_attn_Dp, L0_attn_W_out, L0_ffn_weight_int8,
           L0_ffn_weight_scale, L0_ffn_bias):
    import ml_dtypes
    x = np.asarray(L0_attn_x, dtype=np.float32)
    W_in = np.asarray(L0_attn_W_in, dtype=np.float32)
    conv_w = np.asarray(L0_attn_conv_w, dtype=np.float32)
    conv_b = np.asarray(L0_attn_conv_b, dtype=np.float32)
    W_x = np.asarray(L0_attn_W_x, dtype=np.float32)
    W_dt = np.asarray(L0_attn_W_dt, dtype=np.float32)
    b_dt = np.asarray(L0_attn_b_dt, dtype=np.float32)
    A_log = np.asarray(L0_attn_A_log, dtype=np.float32)
    Dp = np.asarray(L0_attn_Dp, dtype=np.float32)
    W_out = np.asarray(L0_attn_W_out, dtype=np.float32)
    fw8 = np.asarray(L0_ffn_weight_int8)
    fws = np.asarray(L0_ffn_weight_scale, dtype=np.float32)
    fb = np.asarray(L0_ffn_bias, dtype=np.float32)

    x_fm = np.ascontiguousarray(np.swapaxes(x, 1, 2))
    winT = np.ascontiguousarray(W_in.T)
    woutT = np.ascontiguousarray(W_out.T)
    wxT_t = np.ascontiguousarray(
        W_x.T.reshape(GI, 128, 96).transpose(1, 0, 2).reshape(128, GI * 96))
    wdtT = np.zeros((128, DI), np.float32)
    wdtT[0:RNK, :] = W_dt.T
    fwT8 = np.ascontiguousarray(fw8.T)
    convd = np.zeros((128, GI * 4 * 128), np.float32)
    for g in range(GI):
        for kk in range(4):
            blk = convd[:, (g * 4 + kk) * 128:(g * 4 + kk + 1) * 128]
            np.fill_diagonal(blk, conv_w[g * 128:(g + 1) * 128, kk])
    sel = np.zeros((128, 37 * 128), ml_dtypes.bfloat16)
    for n in range(NST):
        sel[64 + n, n * 128:(n + 1) * 128] = 1
        sel[80 + n, (16 + n) * 128:(17 + n) * 128] = 1
    sel[:, 32 * 128:33 * 128] = np.eye(128, dtype=ml_dtypes.bfloat16)
    for j, row in enumerate((0, 32, 64, 96)):
        sel[row, (33 + j) * 128:(34 + j) * 128] = 1
    identf = np.eye(128, dtype=np.float32)
    alog_fm = np.ascontiguousarray(
        A_log.reshape(GI, 128, NST).transpose(1, 0, 2).reshape(128, GI * NST))
    colify = lambda v: np.ascontiguousarray(v.reshape(GI, 128).T)
    cols = np.concatenate([colify(conv_b), colify(b_dt), colify(Dp)], axis=1)
    colify8 = lambda v: np.ascontiguousarray(v.reshape(8, 128).T)
    fcols = np.concatenate([colify8(fws[:, 0]), colify8(fb)], axis=1)

    nc = build_nc()
    in_maps = []
    for c in range(8):
        in_maps.append({
            "x_fm": np.ascontiguousarray(x_fm[c]),
            "winT": winT, "woutT": woutT, "wxT_t": wxT_t, "wdtT": wdtT,
            "convd": convd, "sel_bf": sel,
            "ident_f32": identf, "alog_fm": alog_fm, "cols": cols,
            "fcols": fcols, "fwT8": fwT8,
        })
    res = run_bass_kernel_spmd(nc, in_maps, core_ids=list(range(8)))
    out_fm = np.stack([np.asarray(r["out"]) for r in res.results])
    return np.ascontiguousarray(np.swapaxes(out_fm, 1, 2))


if __name__ == "__main__":
    pass


# version 09
# speedup vs baseline: 1.0515x; 1.0515x over previous
import os
import numpy as np
from contextlib import ExitStack

KDBG = os.environ.get("KDBG", "0") == "1"
KSER = os.environ.get("KSER", "0") == "1"
KSERC = int(os.environ.get("KSERC", "1"))
KSERV = int(os.environ.get("KSERV", "2048"))

import concourse.bass as bass
import concourse.mybir as mybir
from concourse.bass_utils import run_bass_kernel_spmd

F32 = mybir.dt.float32
BF16 = mybir.dt.bfloat16
I8 = mybir.dt.int8
AF = mybir.ActivationFunctionType
ALU = mybir.AluOpType

MAGIC = 12582912.0
EPS_LN = 1e-5
EPS_FFN = 1e-5
ONE_OVER_D = 1.0 / 1024.0

L = 2048
D = 1024
DI = 2048
E = 4096
NST = 16
RNK = 64
NQ = 4
LQ = 512
KD = 8
GI = 16
NEB = 32


class Sched:

    def __init__(self):
        self.ops = []
        self.cnt = {}

    CLS = {"sync": 1, "tensor": 2, "vector": 4, "scalar": 8, "gpsimd": 16}
    stage = 32

    def op(self, eng, fn, waits=(), inc=None, amt=1):
        ser = KSER or (KSERC & self.CLS[eng]) or (
            eng == "vector" and (KSERV & self.stage))
        snap = dict(self.cnt) if ser else None
        ws = tuple(w for w in waits if w is not None)
        if inc == "s_in" and self.cnt.get("s_in"):
            ws = ws + (("s_in", self.cnt["s_in"]),)
        if inc == "s_gd" and self.cnt.get("s_gd"):
            ws = ws + (("s_gd", self.cnt["s_gd"]),)
        ev = None
        if inc is not None:
            self.cnt[inc] = self.cnt.get(inc, 0) + amt
            ev = (inc, self.cnt[inc])
        if snap:
            ws = ws + tuple(snap.items())
        self.ops.append((eng, fn, ws, inc, amt))
        return ev


def build_nc():
    nc = bass.Bass("TRN2", target_bir_lowering=False,
                   detect_race_conditions=KDBG)

    dx = nc.dram_tensor("x_fm", [D, L], F32, kind="ExternalInput")
    dwin = nc.dram_tensor("winT", [D, E], F32, kind="ExternalInput")
    dwout = nc.dram_tensor("woutT", [DI, D], F32, kind="ExternalInput")
    dwx = nc.dram_tensor("wxT_t", [128, GI * 96], F32, kind="ExternalInput")
    dwdt = nc.dram_tensor("wdtT", [128, DI], F32, kind="ExternalInput")
    dconvd = nc.dram_tensor("convd", [128, GI * 4 * 128], F32,
                            kind="ExternalInput")
    dsel = nc.dram_tensor("sel_bf", [128, 37 * 128], BF16,
                          kind="ExternalInput")
    didf = nc.dram_tensor("ident_f32", [128, 128], F32, kind="ExternalInput")
    dalog = nc.dram_tensor("alog_fm", [128, GI * NST], F32,
                           kind="ExternalInput")
    dcols = nc.dram_tensor("cols", [128, 48], F32, kind="ExternalInput")
    dfcols = nc.dram_tensor("fcols", [128, 16], F32, kind="ExternalInput")
    dfw8 = nc.dram_tensor("fwT8", [D, D], I8, kind="ExternalInput")
    dout = nc.dram_tensor("out", [D, L], F32, kind="ExternalOutput")
    ddbg = nc.dram_tensor("dbg", [128, 2 * GI * NST], F32,
                          kind="ExternalOutput")
    dwin_bf = nc.dram_tensor("win_bf", [D, E], BF16)
    dwout_bf = nc.dram_tensor("wout_bf", [DI, D], BF16)
    dfw_bf = nc.dram_tensor("fw_bf", [D, D], BF16)
    dcvd_bf = nc.dram_tensor("cvd_bf", [128, GI * 4 * 128], BF16)
    dxres = nc.dram_tensor("xres", [D, L], F32)
    dccin = nc.dram_tensor("cc_in", [1, 1], F32)
    dccout = nc.dram_tensor("cc_out", [1, 1], F32)

    es = ExitStack()

    def sbt(name, shape, dt=F32):
        return es.enter_context(nc.sbuf_tensor(name, shape, dt))

    x_sb = sbt("x_sb", [128, KD * LQ])
    t1_sb = sbt("t1_sb", [128, KD * LQ], BF16)
    xcr_sb = sbt("xcr_sb", [128, 4 * (LQ + 3)], BF16)
    halo_sb = sbt("halo_sb", [128, GI * 3], BF16)
    g_sb = sbt("g_sb", [128, GI * LQ], BF16)
    xcv_sb = sbt("xcv_sb", [128, GI * LQ], BF16)
    dt_sb = sbt("dt_sb", [128, GI * LQ], BF16)
    y_sb = sbt("y_sb", [128, GI * LQ], BF16)
    rep_sb = sbt("rep_sb", [128, 32 * LQ], BF16)
    xdbl_sb = sbt("xdbl_sb", [128, LQ], BF16)
    da_sb = sbt("da_sb", [128, 2 * LQ], BF16)
    dbx_sb = sbt("dbx_sb", [128, 2 * LQ], BF16)
    h_sb = sbt("h_sb", [128, 2 * LQ], BF16)
    z_sb = sbt("z_sb", [128, 2 * LQ], BF16)
    carry_sb = sbt("carry_sb", [128, 2 * GI * NST], BF16)
    inv_sb = sbt("inv_sb", [128, LQ])
    etmp_sb = sbt("etmp_sb", [128, 2 * LQ])
    lnt_sb = sbt("lnt_sb", [128, LQ])
    m2pad_sb = sbt("m2pad_sb", [128, L], BF16)
    padtmp_sb = sbt("padtmp_sb", [1, 2 * LQ], BF16)
    qrow_sb = sbt("qrow_sb", [1, LQ], BF16)
    qcol_sb = sbt("qcol_sb", [128, 2])
    smal_sb = sbt("smal_sb", [1, 16])
    gpad_sb = sbt("gpad_sb", [128, 4], BF16)
    iscol_sb = sbt("iscol_sb", [128, 1])
    wsx_sb = sbt("wsx_sb", [128, 8])
    fc_sb = sbt("fc_sb", [128, 16])
    a_sb = sbt("a_sb", [128, GI * NST])
    cols_sb = sbt("cols_sb", [128, 48])
    cc_sb = sbt("cc_sb", [128, 4])
    sel_sb = sbt("sel_sb", [128, 37 * 128], BF16)
    idf_sb = sbt("idf_sb", [128, 128])
    onesd_sb = sbt("onesd_sb", [128, 128], BF16)
    wx_sb = sbt("wx_sb", [128, GI * 96], BF16)
    wdt_sb = sbt("wdt_sb", [128, DI], BF16)
    cvrg_sb = sbt("cvrg_sb", [128, 2 * 1024], BF16)
    winrg_sb = sbt("winrg_sb", [128, 3 * 2048], BF16)
    wotrg_sb = sbt("wotrg_sb", [128, 2 * 2048], BF16)
    i8t_sb = sbt("i8t_sb", [128, 2 * 1024], I8)

    ps = [es.enter_context(nc.psum_tensor(f"pb{i}", [128, LQ], F32))
          for i in range(8)]

    sem_names = ["s_in", "s_gd", "s_t", "s_v", "s_s", "s_g", "s_cc"]
    sems = {n: es.enter_context(nc.semaphore(n)) for n in sem_names}

    S = Sched()
    AP = bass.AP

    bank_free = [None] * 8

    idb = lambda: sel_sb[:, 32 * 128:33 * 128]

    acol = lambda g, n: a_sb[:, g * NST + n:g * NST + n + 1]
    convb_c = lambda g: cols_sb[:, g:g + 1]
    bdt_c = lambda g: cols_sb[:, 16 + g:16 + g + 1]
    dp_c = lambda g: cols_sb[:, 32 + g:32 + g + 1]

    xk = lambda k: x_sb[:, k * LQ:(k + 1) * LQ]
    t1k = lambda k: t1_sb[:, k * LQ:(k + 1) * LQ]
    xcs = lambda g, a, b: xcr_sb[:, (g % 4) * (LQ + 3) + a:
                                 (g % 4) * (LQ + 3) + b]
    gg = lambda g: g_sb[:, g * LQ:(g + 1) * LQ]
    xcvg = lambda g: xcv_sb[:, g * LQ:(g + 1) * LQ]
    dtg = lambda g: dt_sb[:, g * LQ:(g + 1) * LQ]
    yg = lambda g: y_sb[:, g * LQ:(g + 1) * LQ]
    repB = lambda n: rep_sb[:, n * LQ:(n + 1) * LQ]
    repC = lambda n: rep_sb[:, (16 + n) * LQ:(17 + n) * LQ]
    selB = lambda n: sel_sb[:, n * 128:(n + 1) * 128]
    selC = lambda n: sel_sb[:, (16 + n) * 128:(17 + n) * 128]
    selR = lambda j: sel_sb[:, (33 + j) * 128:(34 + j) * 128]
    da_s = lambda i: da_sb[:, (i % 2) * LQ:(i % 2 + 1) * LQ]
    dbx_s = lambda i: dbx_sb[:, (i % 2) * LQ:(i % 2 + 1) * LQ]
    h_s = lambda i: h_sb[:, (i % 2) * LQ:(i % 2 + 1) * LQ]
    z_s = lambda i: z_sb[:, (i % 2) * LQ:(i % 2 + 1) * LQ]
    et_s = lambda i: etmp_sb[:, (i % 2) * LQ:(i % 2 + 1) * LQ]
    def wintile(eb, k):
        base = ((eb // 2) % 3) * 2048
        off = base + k * 256 + (eb % 2) * 128
        return winrg_sb[:, off:off + 128]
    wotslot = lambda s: wotrg_sb[:, (s % 2) * 2048:(s % 2 + 1) * 2048]
    cvslot = lambda g: cvrg_sb[:, ((g // 2) % 2) * 1024 + (g % 2) * 512:
                                ((g // 2) % 2) * 1024 + (g % 2 + 1) * 512]

    e_sel = S.op("sync", lambda e: e.dma_start(out=sel_sb[:], in_=dsel.ap()),
                 inc="s_in", amt=16)
    e_idf = S.op("sync", lambda e: e.dma_start(out=idf_sb[:], in_=didf.ap()),
                 inc="s_in", amt=16)
    e_alog = S.op("sync", lambda e: e.dma_start(
        out=x_sb[:, 0:GI * NST], in_=dalog.ap()), inc="s_in", amt=16)
    e_cols = S.op("sync", lambda e: e.dma_start(out=cols_sb[:],
                  in_=dcols.ap()), inc="s_in", amt=16)
    e_fcols = S.op("sync", lambda e: e.dma_start(out=fc_sb[:],
                   in_=dfcols.ap()), inc="s_in", amt=16)
    e_wx = S.op("sync", lambda e: e.dma_start(
        out=x_sb[:, 512:512 + GI * 96], in_=dwx.ap()), inc="s_in", amt=16)
    e_wdt = S.op("sync", lambda e: e.dma_start(
        out=AP(x_sb, 2048, [[KD * LQ, 128], [1, DI]]), in_=dwdt.ap()),
        inc="s_in", amt=16)
    e_cc0 = S.op("vector", lambda e: e.memset(cc_sb[:, 0:1], EPS_LN), inc="s_v")
    e_cc1 = S.op("vector", lambda e: e.memset(cc_sb[:, 1:2], 1.0), inc="s_v")
    e_cc2 = S.op("vector", lambda e: e.memset(cc_sb[:, 2:3], -0.5), inc="s_v")
    e_cc3 = S.op("vector", lambda e: e.memset(cc_sb[:, 3:4], -1.0),
                 inc="s_v")
    e_a1 = S.op("scalar", lambda e: e.activation(
        out=a_sb[:], in_=x_sb[:, 0:GI * NST], func=AF.Exp),
        waits=[e_alog], inc="s_s")
    e_a2 = S.op("scalar", lambda e: e.activation(
        out=a_sb[:], in_=a_sb[:], func=AF.Copy, scale=cc_sb[:, 3:4]),
        waits=[e_cc3], inc="s_s")
    e_wxc = S.op("vector", lambda e: e.tensor_copy(
        wx_sb[:], x_sb[:, 512:512 + GI * 96]), waits=[e_wx], inc="s_v")
    e_wdtc = S.op("vector", lambda e: e.tensor_copy(
        wdt_sb[:], AP(x_sb, 2048, [[KD * LQ, 128], [1, DI]])),
        waits=[e_wdt], inc="s_v")
    e_ms1 = S.op("vector", lambda e: e.memset(onesd_sb[:], ONE_OVER_D),
                 inc="s_v")
    e_ms2 = S.op("vector", lambda e: e.memset(halo_sb[:], 0.0), inc="s_v")
    e_ms3 = S.op("vector", lambda e: e.memset(xdbl_sb[:], 0.0), inc="s_v")
    e_ms4 = S.op("vector", lambda e: e.memset(qcol_sb[:], 0.0), inc="s_v")
    e_ms5 = S.op("vector", lambda e: e.memset(gpad_sb[:], 0.0), inc="s_v")
    e_ms6 = S.op("vector", lambda e: e.memset(m2pad_sb[:], 0.0), inc="s_v")

    stg_cast_ev = {}
    stg_store_ev = {}
    stg_idx = [0]
    win_piece_ev = {}

    def _dst(src_dram):
        return {id(dconvd): dcvd_bf, id(dwin): dwin_bf,
                id(dwout): dwout_bf}[id(src_dram)]

    def _rl(src_dram):
        return {id(dconvd): GI * 4 * 128, id(dwin): E,
                id(dwout): D}[id(src_dram)]

    def stage_x(src_dram, off):
        i = stg_idx[0]
        stg_idx[0] += 1
        sl = i % 4
        evl = S.op("sync", lambda e, sl=sl, off=off, src_dram=src_dram:
                   e.dma_start(out=x_sb[:, sl * 1024:(sl + 1) * 1024],
                               in_=AP(src_dram, off,
                                      [[_rl(src_dram), 128], [1, 1024]])),
                   waits=[stg_cast_ev.get(i - 4)], inc="s_in", amt=16)
        evc = S.op("gpsimd", lambda e, sl=sl: e.tensor_copy(
            rep_sb[:, sl * 1024:(sl + 1) * 1024],
            x_sb[:, sl * 1024:(sl + 1) * 1024]),
            waits=[evl, stg_store_ev.get(i - 4)], inc="s_g")
        stg_cast_ev[i] = evc
        evs = S.op("gpsimd", lambda e, sl=sl, off=off, src_dram=src_dram:
                   e.dma_start(out=AP(_dst(src_dram), off,
                                      [[_rl(src_dram), 128], [1, 1024]]),
                               in_=rep_sb[:, sl * 1024:(sl + 1) * 1024]),
                   waits=[evc], inc="s_gd", amt=16)
        stg_store_ev[i] = evs
        return evs

    for p in range(8):
        stage_x(dconvd, p * 1024)
    e_cvd_st = {1: stg_store_ev[7]}
    for c2 in range(2):
        for j in range(KD):
            ev = stage_x(dwin, j * 128 * E + c2 * 1024)
            win_piece_ev[(j, c2 * 2)] = ev
            win_piece_ev[(j, c2 * 2 + 1)] = ev
    ph1_cast_done = stg_cast_ev[stg_idx[0] - 1]

    e_x0 = S.op("gpsimd", lambda e: e.dma_start(
        out=AP(x_sb, 0, [[KD * LQ, 128], [LQ, KD], [1, LQ]]),
        in_=AP(dx, 0, [[L, 128], [128 * L, KD], [1, LQ]])),
        waits=[e_a1, e_wxc, e_wdtc, ph1_cast_done], inc="s_gd", amt=16)

    fst2 = [etmp_sb[:, 0:LQ], etmp_sb[:, LQ:2 * LQ]]
    h1_cast = {}
    h1_store = {}
    h1_idx = [0]

    def stage_h1(off):
        i = h1_idx[0]
        h1_idx[0] += 1
        sl = i % 2
        evl = S.op("gpsimd", lambda e, sl=sl, off=off: e.dma_start(
            out=fst2[sl], in_=AP(dwin, off, [[E, 128], [1, LQ]])),
            waits=[h1_cast.get(i - 2)], inc="s_gd", amt=16)
        evc = S.op("gpsimd", lambda e, sl=sl: e.tensor_copy(
            rep_sb[:, sl * LQ:(sl + 1) * LQ], fst2[sl]),
            waits=[evl, h1_store.get(i - 2)], inc="s_g")
        h1_cast[i] = evc
        evs = S.op("gpsimd", lambda e, sl=sl, off=off: e.dma_start(
            out=AP(dwin_bf, off, [[E, 128], [1, LQ]]),
            in_=rep_sb[:, sl * LQ:(sl + 1) * LQ]),
            waits=[evc], inc="s_gd", amt=16)
        h1_store[i] = evs
        return evs

    for c in range(4, 8):
        for j in range(KD):
            win_piece_ev[(j, c)] = stage_h1(j * 128 * E + c * LQ)
    wo_ev = {}
    wo_store = {}
    fw_cast = {}
    fw_store = {}
    late_done = [None]

    def emit_late_staging(e_inv1, e_xln, e_t1_free):
        land = [lnt_sb[:, 0:LQ], inv_sb[:, 0:LQ]]
        lgu = [e_inv1, e_xln]
        lc = {}
        ls = {}
        li = [0]

        def st2(off):
            i = li[0]
            li[0] += 1
            sl = i % 2
            cs = i % 4
            evl = S.op("gpsimd", lambda e, sl=sl, off=off: e.dma_start(
                out=land[sl], in_=AP(dwout, off, [[D, 128], [1, LQ]])),
                waits=[lc.get(i - 2), lgu[sl] if i < 2 else None],
                inc="s_gd", amt=16)
            evc = S.op("gpsimd", lambda e, cs=cs, sl=sl: e.tensor_copy(
                t1_sb[:, cs * LQ:(cs + 1) * LQ], land[sl]),
                waits=[evl, ls.get(i - 4),
                       e_t1_free if i < 4 else None], inc="s_g")
            lc[i] = evc
            evs = S.op("gpsimd", lambda e, cs=cs, off=off: e.dma_start(
                out=AP(dwout_bf, off, [[D, 128], [1, LQ]]),
                in_=t1_sb[:, cs * LQ:(cs + 1) * LQ]),
                waits=[evc], inc="s_gd", amt=16)
            ls[i] = evs
            return evs

        for j in range(16):
            for c in range(2):
                wo_ev[(j, c)] = st2(j * 128 * D + c * LQ)
        wo_store[15] = wo_ev[(15, 1)]
        for j in range(8):
            evl = S.op("gpsimd", lambda e, j=j: e.dma_start(
                out=i8t_sb[:, (j % 2) * 1024:(j % 2 + 1) * 1024],
                in_=AP(dfw8, j * 128 * D, [[D, 128], [1, D]])),
                waits=[fw_cast.get(j - 2)], inc="s_gd", amt=16)
            fw_cast[j] = S.op("gpsimd", lambda e, j=j: e.tensor_copy(
                t1_sb[:, 2048 + (j % 2) * 1024:2048 + (j % 2 + 1) * 1024],
                i8t_sb[:, (j % 2) * 1024:(j % 2 + 1) * 1024]),
                waits=[evl, fw_store.get(j - 2),
                       e_t1_free if j < 2 else None],
                inc="s_g")
            fw_store[j] = S.op("gpsimd", lambda e, j=j: e.dma_start(
                out=AP(dfw_bf, j * 128 * D, [[D, 128], [1, D]]),
                in_=t1_sb[:, 2048 + (j % 2) * 1024:
                          2048 + (j % 2 + 1) * 1024]),
                waits=[fw_cast[j]], inc="s_gd", amt=16)
        late_done[0] = fw_store[7]

    startup_x_free = []
    ev_fstage_done = h1_cast[h1_idx[0] - 1]
    ev_stores_done = h1_store[h1_idx[0] - 1]
    rep_startup_guard = [ev_stores_done]
    startup_done = [e_sel, e_idf, e_a2, e_cols, e_fcols, e_ms1]

    last_win_rd = {}
    last_wot_rd = {}
    last_cv_rd = {}
    prev_conv_done = None
    prev_scan_slot = {}
    prev_rep_reader = None
    prev_da_last = None
    ev_xres_store = {}
    prev_xguard = startup_x_free + [e_ms2, e_ms3] + startup_done
    prev_rowdma = None
    prev_inproj_last = None
    qrow_tmp = [None]

    for q in range(NQ):
        qrow_tmp[0] = qrow_sb[0:1, 0:LQ]
        S.stage = 1
        if q == 0:
            e_x = e_x0
        else:
            e_x = S.op("gpsimd", lambda e, q=q: e.dma_start(
                out=AP(x_sb, 0, [[KD * LQ, 128], [LQ, KD], [1, LQ]]),
                in_=AP(dx, q * LQ, [[L, 128], [128 * L, KD], [1, LQ]])),
                waits=prev_xguard, inc="s_gd", amt=16)
        e_cast = None
        for k in range(KD):
            e_cast = S.op("vector", lambda e, k=k: e.tensor_copy(
                dtg(k), xk(k)), waits=[e_x, prev_da_last], inc="s_v")
        e_m = None
        for k in range(KD):
            e_m = S.op("tensor", lambda e, k=k: e.matmul(
                ps[0][:], onesd_sb[:], dtg(k),
                start=(k == 0), stop=(k == KD - 1)),
                waits=[e_cast, bank_free[0], e_ms1], inc="s_t")
        e_t1 = None
        for k in range(KD):
            e_t1 = S.op("vector", lambda e, k=k: e.tensor_tensor(
                out=t1k(k), in0=xk(k), in1=ps[0][:], op=ALU.subtract),
                waits=[e_m, late_done[0] if q == 1 else None], inc="s_v")
        e_sq = None
        for k in range(KD):
            e_sq = S.op("scalar", lambda e, k=k: e.activation(
                out=dtg(k), in_=t1k(k), func=AF.Square),
                waits=[e_t1, e_m], inc="s_s")
        e_v = None
        for k in range(KD):
            e_v = S.op("tensor", lambda e, k=k: e.matmul(
                ps[1][:], onesd_sb[:], dtg(k),
                start=(k == 0), stop=(k == KD - 1)),
                waits=[e_sq, bank_free[1]], inc="s_t")
        e_ln1 = S.op("scalar", lambda e: e.activation(
            out=lnt_sb[:], in_=ps[1][:], func=AF.Ln, bias=cc_sb[:, 0:1]),
            waits=[e_v, e_cc0, e_cc2], inc="s_s")
        e_inv1 = S.op("scalar", lambda e: e.activation(
            out=inv_sb[:], in_=lnt_sb[:], func=AF.Exp, scale=cc_sb[:, 2:3]),
            inc="s_s")
        bank_free[1] = e_ln1
        e_xln = None
        for k in range(KD):
            e_xln = S.op("vector", lambda e, k=k: e.tensor_tensor(
                out=t1k(k), in0=t1k(k), in1=inv_sb[:], op=ALU.mult),
                waits=[e_inv1], inc="s_v")
        bank_free[0] = e_t1

        S.stage = 2
        xc_rdy = {}
        cv_last_ld = [None]
        cv_mm_ev = {}
        g_done = {}
        xconv_done = {}
        ip_free = {2: bank_free[2], 3: bank_free[3]}
        cv_free = {4: bank_free[4], 5: bank_free[5]}
        e_conv_mm_last = None

        def emit_conv(g, q=q):
            nonlocal e_conv_mm_last
            bk = 4 + g % 2
            if g % 2 == 0:
                cv_last_ld[0] = S.op("sync", lambda e, g=g: e.dma_start(
                    out=cvrg_sb[:, ((g // 2) % 2) * 1024:
                                ((g // 2) % 2 + 1) * 1024],
                    in_=AP(dcvd_bf, g * 4 * 128,
                           [[GI * 4 * 128, 128], [1, 1024]])),
                    waits=[last_cv_rd.get((g // 2) % 2),
                           e_cvd_st[1] if q == 0 and g < 2 else None],
                    inc="s_in", amt=16)
            e_cl = cv_last_ld[0]
            e_cm = None
            for kk in range(4):
                e_cm = S.op("tensor", lambda e, g=g, kk=kk, bk=bk: e.matmul(
                    ps[bk][:], cvslot(g)[:, kk * 128:(kk + 1) * 128],
                    xcs(g, kk, kk + LQ),
                    start=(kk == 0), stop=(kk == 3)),
                    waits=[xc_rdy[g], e_cl, cv_free.get(bk)], inc="s_t")
            last_cv_rd[(g // 2) % 2] = e_cm
            cv_mm_ev[g] = e_cm
            e_cs = S.op("scalar", lambda e, g=g, bk=bk: e.activation(
                out=xcvg(g), in_=ps[bk][:], func=AF.Silu,
                bias=convb_c(g)), waits=[e_cm, e_cols], inc="s_s")
            cv_free[bk] = e_cs
            xconv_done[g] = e_cs
            e_conv_mm_last = e_cm

        last_ld = [None]
        for eb in range(NEB):
            if eb >= 4 and (eb - 4) < GI:
                emit_conv(eb - 4)
            if eb % 2 == 0:
                last_ld[0] = S.op("sync", lambda e, eb=eb: e.dma_start(
                    out=AP(winrg_sb, ((eb // 2) % 3) * 2048,
                           [[3 * 2048, 128], [256, KD], [1, 256]]),
                    in_=AP(dwin_bf, eb * 128,
                           [[E, 128], [128 * E, KD], [1, 256]])),
                    waits=[last_win_rd.get((eb // 2) % 3),
                           win_piece_ev[(KD - 1, (eb + 1) // 4)]
                           if q == 0 else None],
                    inc="s_in", amt=16)
            e_ld = last_ld[0]
            bk = 2 + eb % 2
            e_mm = None
            for k in range(KD):
                e_mm = S.op("tensor", lambda e, eb=eb, k=k, bk=bk: e.matmul(
                    ps[bk][:], wintile(eb, k), t1k(k),
                    start=(k == 0), stop=(k == KD - 1)),
                    waits=[e_ld, e_xln, ip_free.get(bk)], inc="s_t")
            last_win_rd[(eb // 2) % 3] = e_mm
            if eb < 16:
                g = eb
                slot_guard = cv_mm_ev.get(g - 4)
                if q > 0 and g < 4:
                    slot_guard = prev_conv_done
                e_e = S.op("vector", lambda e, g=g, bk=bk: e.tensor_copy(
                    xcs(g, 3, LQ + 3), ps[bk][:]),
                    waits=[e_mm, slot_guard], inc="s_v")
                S.op("vector", lambda e, g=g: e.tensor_copy(
                    xcs(g, 0, 3), halo_sb[:, g * 3:g * 3 + 3]),
                    waits=[e_ms2], inc="s_v")
                if q < NQ - 1:
                    e_sv = S.op("vector", lambda e, g=g: e.tensor_copy(
                        halo_sb[:, g * 3:g * 3 + 3], xcs(g, LQ, LQ + 3)),
                        inc="s_v")
                else:
                    e_sv = S.op("vector", lambda e: e.engine_nop(),
                                inc="s_v")
                xc_rdy[g] = e_sv
                ip_free[bk] = e_e
            else:
                e_e = S.op("scalar", lambda e, eb=eb, bk=bk: e.activation(
                    out=gg(eb - 16), in_=ps[bk][:], func=AF.Silu),
                    waits=[e_mm], inc="s_s")
                g_done[eb - 16] = e_e
                ip_free[bk] = e_e
        for g in range(GI - 4, GI):
            emit_conv(g)
        bank_free[2] = ip_free[2]
        bank_free[3] = ip_free[3]
        bank_free[4] = cv_free[4]
        bank_free[5] = cv_free[5]
        prev_conv_done = e_conv_mm_last
        prev_inproj_last = last_win_rd[((NEB - 1) // 2) % 3]

        e_wxm = None
        for g in range(GI):
            e_wxm = S.op("tensor", lambda e, g=g: e.matmul(
                ps[6][0:96, :], wx_sb[:, g * 96:(g + 1) * 96], xcvg(g),
                start=(g == 0), stop=(g == GI - 1)),
                waits=[xconv_done[g], e_wxc, bank_free[6]], inc="s_t")
        e_xdbl = S.op("vector", lambda e: e.tensor_copy(
            xdbl_sb[0:96, :], ps[6][0:96, :]),
            waits=[e_wxm, e_ms3], inc="s_v")
        bank_free[6] = e_xdbl

        dt_done = {}
        dt_free = {7: bank_free[7], 0: bank_free[0]}
        for g in range(GI):
            bk = 7 if g % 2 == 0 else 0
            e_dm = S.op("tensor", lambda e, g=g, bk=bk: e.matmul(
                ps[bk][:], wdt_sb[:, g * 128:(g + 1) * 128],
                xdbl_sb[:], start=True, stop=True),
                waits=[e_xdbl, e_wdtc, dt_free.get(bk),
                       e_inv1 if bk == 0 else None], inc="s_t")
            e_de = S.op("scalar", lambda e, g=g, bk=bk: e.activation(
                out=et_s(g), in_=ps[bk][:], func=AF.Exp, bias=bdt_c(g)),
                waits=[e_dm, ev_fstage_done if q == 0 and g < 2 else None],
                inc="s_s")
            e_dl = S.op("scalar", lambda e, g=g: e.activation(
                out=dtg(g), in_=et_s(g), func=AF.Ln, bias=cc_sb[:, 1:2]),
                waits=[e_cc1], inc="s_s")
            dt_free[bk] = e_de
            dt_done[g] = e_dl
        bank_free[7] = dt_free[7]
        bank_free[0] = dt_free[0]

        xcd_done = {}
        u_done = {}
        for g in range(GI):
            xcd_done[g] = S.op("vector", lambda e, g=g: e.tensor_scalar(
                out=yg(g), in0=xcvg(g), scalar1=dp_c(g), scalar2=None,
                op0=ALU.mult), waits=[xconv_done[g], e_cols], inc="s_v")
            u_done[g] = S.op("vector", lambda e, g=g: e.tensor_tensor(
                out=xcvg(g), in0=dtg(g), in1=xcvg(g), op=ALU.mult),
                waits=[dt_done[g]], inc="s_v")

        repB_ev = {}
        repC_ev = {}
        rb_free = bank_free[6]
        rc_free = bank_free[7]
        for n in range(NST):
            e_rb = S.op("tensor", lambda e, n=n: e.matmul(
                ps[6][:], selB(n), xdbl_sb[:], start=True, stop=True),
                waits=[e_xdbl, e_sel, rb_free], inc="s_t")
            e_rbe = S.op("scalar", lambda e, n=n: e.copy(
                out=repB(n), in_=ps[6][:]),
                waits=[e_rb, prev_rep_reader] +
                      (rep_startup_guard if q == 0 else []), inc="s_s")
            e_rc = S.op("tensor", lambda e, n=n: e.matmul(
                ps[7][:], selC(n), xdbl_sb[:], start=True, stop=True),
                waits=[rc_free], inc="s_t")
            e_rce = S.op("scalar", lambda e, n=n: e.copy(
                out=repC(n), in_=ps[7][:]),
                waits=[e_rc, prev_rep_reader] +
                      (rep_startup_guard if q == 0 else []), inc="s_s")
            rb_free, rc_free = e_rbe, e_rce
            repB_ev[n] = e_rbe
            repC_ev[n] = e_rce
        bank_free[6] = rb_free
        bank_free[7] = rc_free

        S.stage = 4
        gate_ev = {}
        scan_ev = {}
        zmm_ev = {}
        da_ev = {}
        y_free = {2: bank_free[2], 3: bank_free[3]}

        def emit_da(j, q=q):
            if j >= GI * NST:
                return
            gj, nj = j // NST, j % NST
            da_ev[j] = S.op("scalar", lambda e, gj=gj, nj=nj, j=j:
                            e.activation(out=da_s(j), in_=dtg(gj),
                                         func=AF.Exp, scale=acol(gj, nj)),
                            waits=[dt_done[gj], e_a2,
                                   scan_ev.get(j - 2,
                                               prev_scan_slot.get(j % 2))],
                            inc="s_s")

        emit_da(0)
        emit_da(1)
        ii = 0
        for g in range(GI):
            bk = 2 + g % 2
            S.op("tensor", lambda e, g=g, bk=bk: e.matmul(
                ps[bk][:], idb(), yg(g), start=True, stop=False),
                waits=[xcd_done[g], y_free.get(bk)], inc="s_t")
            for n in range(NST):
                i = ii
                ii += 1
                e_da = da_ev[i]
                S.stage = 64
                S.op("vector", lambda e, g=g, n=n, i=i: e.tensor_tensor(
                    out=dbx_s(i), in0=xcvg(g), in1=repB(n), op=ALU.mult),
                    waits=[u_done[g], repB_ev[n]], inc="s_v")
                S.stage = 128
                if q == 0:
                    e_h = S.op("vector", lambda e, i=i:
                               e.tensor_tensor_scan(
                                   out=h_s(i), data0=da_s(i), data1=dbx_s(i),
                                   initial=0.0, op0=ALU.mult, op1=ALU.add),
                               waits=[e_da], inc="s_v")
                else:
                    cr = ((q - 1) % 2) * GI * NST + g * NST + n
                    e_h = S.op("vector", lambda e, cr=cr, i=i:
                               e.tensor_tensor_scan(
                                   out=h_s(i), data0=da_s(i), data1=dbx_s(i),
                                   initial=carry_sb[:, cr:cr + 1],
                                   op0=ALU.mult, op1=ALU.add),
                               waits=[e_da], inc="s_v")
                scan_ev[i] = e_h
                emit_da(i + 2)
                S.stage = 256
                if q < NQ - 1:
                    cw = (q % 2) * GI * NST + g * NST + n
                    S.op("scalar", lambda e, cw=cw, i=i: e.copy(
                        out=carry_sb[:, cw:cw + 1],
                        in_=h_s(i)[:, LQ - 1:LQ]),
                        waits=[scan_ev[i]], inc="s_s")
                S.stage = 512
                e_z = S.op("vector", lambda e, n=n, i=i: e.tensor_tensor(
                    out=z_s(i), in0=h_s(i), in1=repC(n), op=ALU.mult),
                    waits=[repC_ev[n], zmm_ev.get(i - 2)], inc="s_v")
                zmm_ev[i] = S.op("tensor",
                                 lambda e, i=i, n=n, bk=bk: e.matmul(
                                     ps[bk][:], idb(), z_s(i),
                                     start=False, stop=(n == NST - 1)),
                                 waits=[e_z], inc="s_t")
            S.stage = 1024
            e_gate = S.op("vector", lambda e, g=g, bk=bk: e.tensor_tensor(
                out=yg(g), in0=ps[bk][:], in1=gg(g), op=ALU.mult),
                waits=[zmm_ev[ii - 1], g_done[g]], inc="s_v")
            S.stage = 4
            y_free[bk] = e_gate
            gate_ev[g] = e_gate
        bank_free[2] = y_free[2]
        bank_free[3] = y_free[3]
        prev_scan_slot = {0: scan_ev[ii - 2], 1: scan_ev[ii - 1]}
        prev_rep_reader = gate_ev[GI - 1]
        if q == 0:
            emit_late_staging(e_inv1, e_xln, prev_inproj_last)
        prev_da_last = da_ev[GI * NST - 1]

        S.stage = 8
        op_evac = {}
        op_free = {4: bank_free[4], 5: bank_free[5]}
        for dm in range(KD):
            e_wol = S.op("gpsimd", lambda e, dm=dm: e.dma_start(
                out=AP(wotrg_sb, (dm % 2) * 2048,
                       [[2 * 2048, 128], [128, GI], [1, 128]]),
                in_=AP(dwout_bf, dm * 128,
                       [[D, 128], [128 * D, GI], [1, 128]])),
                waits=[last_wot_rd.get(dm % 2)] +
                      ([wo_store[15]] if q == 0 and dm < 2 else []),
                inc="s_gd", amt=16)
            bk = 4 + dm % 2
            e_om = None
            for g in range(GI):
                e_om = S.op("tensor", lambda e, dm=dm, g=g, bk=bk: e.matmul(
                    ps[bk][:], wotslot(dm)[:, g * 128:(g + 1) * 128], yg(g),
                    start=(g == 0), stop=(g == GI - 1)),
                    waits=[e_wol, gate_ev[g], op_free.get(bk)], inc="s_t")
            last_wot_rd[dm % 2] = e_om
            e_oe = S.op("vector", lambda e, dm=dm, bk=bk: e.tensor_tensor(
                out=xk(dm), in0=ps[bk][:], in1=xk(dm), op=ALU.add),
                waits=[e_om], inc="s_v")
            op_evac[dm] = e_oe
            op_free[bk] = e_oe
        bank_free[4] = op_free[4]
        bank_free[5] = op_free[5]

        e_cast2 = None
        for k in range(KD):
            e_cast2 = S.op("vector", lambda e, k=k: e.tensor_copy(
                dtg(k), xk(k)), waits=[op_evac[k], prev_da_last], inc="s_v")
        e_m2 = None
        for k in range(KD):
            e_m2 = S.op("tensor", lambda e, k=k: e.matmul(
                ps[0][:], onesd_sb[:], dtg(k),
                start=(k == 0), stop=(k == KD - 1)),
                waits=[e_cast2, bank_free[0]], inc="s_t")
        e_sq2 = None
        for k in range(KD):
            e_sq2 = S.op("scalar", lambda e, k=k: e.activation(
                out=xcvg(k), in_=dtg(k), func=AF.Square),
                waits=[e_cast2, prev_rep_reader], inc="s_s")
        e_v2 = None
        for k in range(KD):
            e_v2 = S.op("tensor", lambda e, k=k: e.matmul(
                ps[1][:], onesd_sb[:], xcvg(k),
                start=(k == 0), stop=(k == KD - 1)),
                waits=[e_sq2, bank_free[1]], inc="s_t")
        e_c12 = S.op("scalar", lambda e: e.activation(
            out=lnt_sb[:], in_=ps[0][:], func=AF.Square),
            waits=[e_m2, late_done[0] if q == 0 else None], inc="s_s")
        e_c22 = S.op("vector", lambda e: e.tensor_tensor(
            out=lnt_sb[:], in0=ps[1][:], in1=lnt_sb[:], op=ALU.subtract),
            waits=[e_v2, e_c12], inc="s_v")
        e_ln2 = S.op("scalar", lambda e: e.activation(
            out=ps[1][:], in_=lnt_sb[:], func=AF.Ln, bias=cc_sb[:, 0:1]),
            waits=[e_c22], inc="s_s")
        e_inv2 = S.op("scalar", lambda e: e.activation(
            out=inv_sb[:], in_=ps[1][:], func=AF.Exp, scale=cc_sb[:, 2:3]),
            waits=[late_done[0] if q == 0 else None], inc="s_s")
        bank_free[1] = e_inv2
        e_r1 = S.op("vector", lambda e, q=q: e.tensor_copy(
            m2pad_sb[0:1, q * LQ:(q + 1) * LQ], ps[0][0:1, :]),
            waits=[e_m2, e_ms6, prev_rowdma], inc="s_v")
        e_r2 = S.op("vector", lambda e, q=q: e.tensor_tensor(
            out=padtmp_sb[0:1, 0:LQ], in0=ps[0][0:1, :],
            in1=m2pad_sb[0:1, q * LQ:(q + 1) * LQ], op=ALU.subtract),
            inc="s_v")
        e_r3 = S.op("vector", lambda e, q=q: e.tensor_copy(
            padtmp_sb[0:1, LQ:2 * LQ], inv_sb[0:1, :]),
            waits=[e_inv2], inc="s_v")
        e_r4 = S.op("vector", lambda e, q=q: e.tensor_tensor(
            out=padtmp_sb[0:1, 0:LQ] if False else qrow_tmp[0],
            in0=inv_sb[0:1, :], in1=padtmp_sb[0:1, LQ:2 * LQ],
            op=ALU.subtract), inc="s_v")
        e_d1 = S.op("gpsimd", lambda e, q=q: e.dma_start(
            out=AP(m2pad_sb, 32 * L + q * LQ, [[L, 1], [1, LQ]]),
            in_=padtmp_sb[0:1, 0:LQ]), waits=[e_r2], inc="s_gd", amt=16)
        e_d2 = S.op("gpsimd", lambda e, q=q: e.dma_start(
            out=AP(m2pad_sb, 64 * L + q * LQ, [[L, 1], [1, LQ]]),
            in_=padtmp_sb[0:1, LQ:2 * LQ]), waits=[e_r3], inc="s_gd", amt=16)
        e_d3 = S.op("gpsimd", lambda e, q=q: e.dma_start(
            out=AP(m2pad_sb, 96 * L + q * LQ, [[L, 1], [1, LQ]]),
            in_=qrow_tmp[0]), waits=[e_r4], inc="s_gd", amt=16)
        prev_rowdma = e_d3
        e_qp = None
        for k in range(KD):
            S.op("vector", lambda e, k=k: e.tensor_tensor(
                out=et_s(k), in0=xk(k), in1=ps[0][:], op=ALU.subtract),
                waits=[e_inv2], inc="s_v")
            S.op("vector", lambda e, k=k: e.tensor_tensor(
                out=et_s(k), in0=et_s(k), in1=inv_sb[:], op=ALU.mult),
                inc="s_v")
            S.op("vector", lambda e, k=k: e.tensor_reduce(
                out=qcol_sb[:, 1:2], in_=et_s(k), axis=mybir.AxisListType.X,
                op=ALU.max, apply_absolute_value=True),
                waits=[e_ms4], inc="s_v")
            e_qp = S.op("vector", lambda e: e.tensor_tensor(
                out=qcol_sb[:, 0:1], in0=qcol_sb[:, 0:1],
                in1=qcol_sb[:, 1:2], op=ALU.max), inc="s_v")
        bank_free[0] = e_qp
        ev_xres_store[q] = S.op("gpsimd", lambda e, q=q: e.dma_start(
            out=AP(dxres, q * LQ, [[L, 128], [128 * L, KD], [1, LQ]]),
            in_=AP(x_sb, 0, [[KD * LQ, 128], [LQ, KD], [1, LQ]])),
            waits=[op_evac[KD - 1]], inc="s_gd", amt=16)
        prev_xguard = [ev_xres_store[q], e_qp, e_r1, e_d3]
        prev_da_last = e_sq2

    S.stage = 2048
    e_tr = S.op("tensor", lambda e: e.transpose(
        ps[6][0:1, 0:128], qcol_sb[:, 0:1], idf_sb[:]),
        waits=[e_qp, e_idf, bank_free[6]], inc="s_t")
    e_gm = S.op("vector", lambda e: e.tensor_reduce(
        out=smal_sb[0:1, 0:1], in_=ps[6][0:1, 0:128],
        axis=mybir.AxisListType.X, op=ALU.max), waits=[e_tr], inc="s_v")
    bank_free[6] = e_gm
    e_st = S.op("gpsimd", lambda e: e.dma_start(
        out=AP(dccin, 0, [[1, 1], [1, 1]]), in_=smal_sb[0:1, 0:1]),
        waits=[e_gm], inc="s_gd", amt=16)
    e_cc = S.op("gpsimd", lambda e: e.collective_compute(
        "AllReduce", ALU.max, replica_groups=[list(range(8))],
        ins=[dccin.ap().opt()], outs=[dccout.ap().opt()]),
        waits=[e_st], inc="s_cc")
    e_ccl = S.op("gpsimd", lambda e: e.dma_start(
        out=smal_sb[0:1, 2:3], in_=dccout.ap()), waits=[e_cc],
        inc="s_gd", amt=16)
    e_xs = S.op("vector", lambda e: e.tensor_scalar(
        out=smal_sb[0:1, 1:2], in0=smal_sb[0:1, 2:3], scalar1=1.0 / 127.0,
        scalar2=EPS_FFN, op0=ALU.mult, op1=ALU.max),
        waits=[e_ccl], inc="s_v")
    e_is = S.op("vector", lambda e: e.reciprocal(
        out=smal_sb[0:1, 3:4], in_=smal_sb[0:1, 1:2]), inc="s_v")
    e_gp1 = S.op("vector", lambda e: e.tensor_copy(
        gpad_sb[0:1, 0:1], smal_sb[0:1, 1:2]),
        waits=[e_ms5, e_xs], inc="s_v")
    e_gp2 = S.op("vector", lambda e: e.tensor_tensor(
        out=gpad_sb[0:1, 1:2], in0=smal_sb[0:1, 1:2], in1=gpad_sb[0:1, 0:1],
        op=ALU.subtract), inc="s_v")
    e_gp3 = S.op("vector", lambda e: e.tensor_copy(
        gpad_sb[0:1, 2:3], smal_sb[0:1, 3:4]), waits=[e_is], inc="s_v")
    e_gp4 = S.op("vector", lambda e: e.tensor_tensor(
        out=gpad_sb[0:1, 3:4], in0=smal_sb[0:1, 3:4], in1=gpad_sb[0:1, 2:3],
        op=ALU.subtract), inc="s_v")
    e_xsc = None
    for j in range(2):
        e_xsc = S.op("tensor", lambda e, j=j: e.matmul(
            ps[7][:, 0:1], selR(0), gpad_sb[:, j:j + 1],
            start=(j == 0), stop=(j == 1)),
            waits=[e_gp2, e_sel, bank_free[7]], inc="s_t")
    e_wsx = S.op("vector", lambda e: e.tensor_scalar(
        out=wsx_sb[:], in0=fc_sb[:, 0:8], scalar1=ps[7][:, 0:1],
        scalar2=None, op0=ALU.mult), waits=[e_xsc, e_fcols], inc="s_v")
    bank_free[7] = e_wsx
    e_isc = None
    for j in range(2):
        e_isc = S.op("tensor", lambda e, j=j: e.matmul(
            ps[6][:, 0:1], selR(0), gpad_sb[:, 2 + j:3 + j],
            start=(j == 0), stop=(j == 1)),
            waits=[e_gp4, bank_free[6]], inc="s_t")
    e_iscol = S.op("vector", lambda e: e.tensor_copy(
        iscol_sb[:], ps[6][:, 0:1]), waits=[e_isc], inc="s_v")
    bank_free[6] = e_iscol
    e_fw = S.op("sync", lambda e: e.dma_start(
        out=rep_sb[:, 0:8192],
        in_=AP(dfw_bf, 0, [[D, 128], [128 * D, 8], [1, D]])),
        waits=[prev_rep_reader, fw_store[7]], inc="s_in", amt=16)
    fwt = lambda k, nb: rep_sb[:, k * 1024 + nb * 128:
                               k * 1024 + (nb + 1) * 128]

    S.stage = 16
    prev_ffn_done = None
    ps0_free = bank_free[0]
    ps1_free = bank_free[1]
    ffn_free = {2: bank_free[2], 3: bank_free[3]}
    for q in range(NQ):
        e_xl = S.op("gpsimd", lambda e, q=q: e.dma_start(
            out=AP(x_sb, 0, [[KD * LQ, 128], [LQ, KD], [1, LQ]]),
            in_=AP(dxres, q * LQ, [[L, 128], [128 * L, KD], [1, LQ]])),
            waits=[e_iscol, e_wsx, prev_ffn_done, e_qp if q == 0 else None],
            inc="s_gd", amt=16)
        e_m2r = None
        for j in range(2):
            e_m2r = S.op("tensor", lambda e, q=q, j=j: e.matmul(
                ps[0][:], selR(j), m2pad_sb[:, q * LQ:(q + 1) * LQ],
                start=(j == 0), stop=(j == 1)),
                waits=[prev_rowdma, ps0_free], inc="s_t")
        e_sir = None
        for j in range(2):
            e_sir = S.op("tensor", lambda e, q=q, j=j: e.matmul(
                ps[1][:], selR(2 + j), m2pad_sb[:, q * LQ:(q + 1) * LQ],
                start=(j == 0), stop=(j == 1)),
                waits=[prev_rowdma, ps1_free], inc="s_t")
        e_sirs = S.op("vector", lambda e: e.tensor_scalar(
            out=inv_sb[:], in0=ps[1][:], scalar1=iscol_sb[:, 0:1],
            scalar2=None, op0=ALU.mult), waits=[e_sir, e_iscol], inc="s_v")
        e_xq = {}
        for k in range(KD):
            S.op("vector", lambda e, k=k: e.tensor_tensor(
                out=et_s(k), in0=xk(k), in1=ps[0][:], op=ALU.subtract),
                waits=[e_xl, e_m2r, prev_ffn_done], inc="s_v")
            S.op("vector", lambda e, k=k: e.tensor_tensor(
                out=et_s(k), in0=et_s(k), in1=inv_sb[:], op=ALU.mult),
                waits=[e_sirs], inc="s_v")
            e_xq[k] = S.op("vector", lambda e, k=k: e.tensor_scalar(
                out=dtg(k), in0=et_s(k), scalar1=MAGIC, scalar2=MAGIC,
                op0=ALU.add, op1=ALU.subtract), inc="s_v")
        ps0_free = e_xq[KD - 1]
        ps1_free = e_sirs
        e_fin = {}
        for nb in range(KD):
            bk = 2 + nb % 2
            e_fm = None
            for k in range(KD):
                e_fm = S.op("tensor", lambda e, nb=nb, k=k, bk=bk: e.matmul(
                    ps[bk][:], fwt(k, nb), dtg(k),
                    start=(k == 0), stop=(k == KD - 1)),
                    waits=[e_fw, e_xq[k], ffn_free.get(bk)], inc="s_t")
            e_fe = S.op("scalar", lambda e, nb=nb, bk=bk: e.activation(
                out=et_s(nb), in_=ps[bk][:], func=AF.Copy,
                scale=wsx_sb[:, nb:nb + 1]),
                waits=[e_fm, e_wsx, e_xq[KD - 1], e_fin.get(nb - 2)],
                inc="s_s")
            e_fa = S.op("vector", lambda e, nb=nb: e.scalar_tensor_tensor(
                out=xk(nb), in0=et_s(nb), scalar=fc_sb[:, 8 + nb:9 + nb],
                in1=xk(nb), op0=ALU.add, op1=ALU.add),
                waits=[e_fe, e_fcols], inc="s_v")
            ffn_free[bk] = e_fa
            e_fin[nb] = e_fa
        prev_ffn_done = e_fin[KD - 1]
        S.op("gpsimd", lambda e, q=q: e.dma_start(
            out=AP(dout, q * LQ, [[L, 128], [128 * L, KD], [1, LQ]]),
            in_=AP(x_sb, 0, [[KD * LQ, 128], [LQ, KD], [1, LQ]])),
            waits=[e_fin[KD - 1]], inc="s_gd", amt=16)

    S.op("vector", lambda e: e.tensor_copy(
        etmp_sb[:, 0:2 * GI * NST], carry_sb[:]), inc="s_v")
    S.op("gpsimd", lambda e: e.dma_start(
        out=ddbg.ap(), in_=etmp_sb[:, 0:2 * GI * NST]),
        waits=[("s_v", S.cnt["s_v"])], inc="s_gd", amt=16)
    final_gd = S.cnt["s_gd"]
    S.op("gpsimd", lambda e: e.nop(), waits=[("s_gd", final_gd)], inc=None)

    with es:
        block = es.enter_context(nc.Block())

        def replay(eng_name):
            def _(eng):
                hw = {}
                for (en, fn, waits, inc, amt) in S.ops:
                    if en != eng_name:
                        continue
                    for (sn, val) in waits:
                        if hw.get(sn, 0) < val:
                            eng.wait_ge(sems[sn], val)
                            hw[sn] = val
                    instr = fn(eng)
                    if inc is not None:
                        instr.then_inc(sems[inc], amt)
                    if KDBG and eng_name != "sync":
                        try:
                            eng.drain()
                        except Exception:
                            pass
            return _

        block.sync(replay("sync"))
        block.vector(replay("vector"))
        block.scalar(replay("scalar"))
        block.tensor(replay("tensor"))
        block.gpsimd(replay("gpsimd"))
    return nc


def kernel(L0_attn_x, L0_attn_W_in, L0_attn_conv_w, L0_attn_conv_b,
           L0_attn_W_x, L0_attn_W_dt, L0_attn_b_dt, L0_attn_A_log,
           L0_attn_Dp, L0_attn_W_out, L0_ffn_weight_int8,
           L0_ffn_weight_scale, L0_ffn_bias):
    import ml_dtypes
    x = np.asarray(L0_attn_x, dtype=np.float32)
    W_in = np.asarray(L0_attn_W_in, dtype=np.float32)
    conv_w = np.asarray(L0_attn_conv_w, dtype=np.float32)
    conv_b = np.asarray(L0_attn_conv_b, dtype=np.float32)
    W_x = np.asarray(L0_attn_W_x, dtype=np.float32)
    W_dt = np.asarray(L0_attn_W_dt, dtype=np.float32)
    b_dt = np.asarray(L0_attn_b_dt, dtype=np.float32)
    A_log = np.asarray(L0_attn_A_log, dtype=np.float32)
    Dp = np.asarray(L0_attn_Dp, dtype=np.float32)
    W_out = np.asarray(L0_attn_W_out, dtype=np.float32)
    fw8 = np.asarray(L0_ffn_weight_int8)
    fws = np.asarray(L0_ffn_weight_scale, dtype=np.float32)
    fb = np.asarray(L0_ffn_bias, dtype=np.float32)

    x_fm = np.ascontiguousarray(np.swapaxes(x, 1, 2))
    winT = np.ascontiguousarray(W_in.T)
    woutT = np.ascontiguousarray(W_out.T)
    wxT_t = np.ascontiguousarray(
        W_x.T.reshape(GI, 128, 96).transpose(1, 0, 2).reshape(128, GI * 96))
    wdtT = np.zeros((128, DI), np.float32)
    wdtT[0:RNK, :] = W_dt.T
    fwT8 = np.ascontiguousarray(fw8.T)
    convd = np.zeros((128, GI * 4 * 128), np.float32)
    for g in range(GI):
        for kk in range(4):
            blk = convd[:, (g * 4 + kk) * 128:(g * 4 + kk + 1) * 128]
            np.fill_diagonal(blk, conv_w[g * 128:(g + 1) * 128, kk])
    sel = np.zeros((128, 37 * 128), ml_dtypes.bfloat16)
    for n in range(NST):
        sel[64 + n, n * 128:(n + 1) * 128] = 1
        sel[80 + n, (16 + n) * 128:(17 + n) * 128] = 1
    sel[:, 32 * 128:33 * 128] = np.eye(128, dtype=ml_dtypes.bfloat16)
    for j, row in enumerate((0, 32, 64, 96)):
        sel[row, (33 + j) * 128:(34 + j) * 128] = 1
    identf = np.eye(128, dtype=np.float32)
    alog_fm = np.ascontiguousarray(
        A_log.reshape(GI, 128, NST).transpose(1, 0, 2).reshape(128, GI * NST))
    colify = lambda v: np.ascontiguousarray(v.reshape(GI, 128).T)
    cols = np.concatenate([colify(conv_b), colify(b_dt), colify(Dp)], axis=1)
    colify8 = lambda v: np.ascontiguousarray(v.reshape(8, 128).T)
    fcols = np.concatenate([colify8(fws[:, 0]), colify8(fb)], axis=1)

    nc = build_nc()
    in_maps = []
    for c in range(8):
        in_maps.append({
            "x_fm": np.ascontiguousarray(x_fm[c]),
            "winT": winT, "woutT": woutT, "wxT_t": wxT_t, "wdtT": wdtT,
            "convd": convd, "sel_bf": sel,
            "ident_f32": identf, "alog_fm": alog_fm, "cols": cols,
            "fcols": fcols, "fwT8": fwT8,
        })
    res = run_bass_kernel_spmd(nc, in_maps, core_ids=list(range(8)))
    out_fm = np.stack([np.asarray(r["out"]) for r in res.results])
    return np.ascontiguousarray(np.swapaxes(out_fm, 1, 2))


if __name__ == "__main__":
    pass


# version 10
# speedup vs baseline: 1.2892x; 1.2261x over previous
import os
import numpy as np
from contextlib import ExitStack

KDBG = os.environ.get("KDBG", "0") == "1"
KSER = os.environ.get("KSER", "0") == "1"
KSERC = int(os.environ.get("KSERC", "0"))
KSERV = int(os.environ.get("KSERV", "2048"))

import concourse.bass as bass
import concourse.mybir as mybir
from concourse.bass_utils import run_bass_kernel_spmd

F32 = mybir.dt.float32
BF16 = mybir.dt.bfloat16
I8 = mybir.dt.int8
AF = mybir.ActivationFunctionType
ALU = mybir.AluOpType

MAGIC = 12582912.0
EPS_LN = 1e-5
EPS_FFN = 1e-5
ONE_OVER_D = 1.0 / 1024.0

L = 2048
D = 1024
DI = 2048
E = 4096
NST = 16
RNK = 64
NQ = 4
LQ = 512
KD = 8
GI = 16
NEB = 32


class Sched:

    def __init__(self):
        self.ops = []
        self.cnt = {}

    CLS = {"sync": 1, "tensor": 2, "vector": 4, "scalar": 8, "gpsimd": 16}
    stage = 32

    def op(self, eng, fn, waits=(), inc=None, amt=1):
        ser = KSER or (KSERC & self.CLS[eng]) or (
            eng == "vector" and (KSERV & self.stage))
        snap = dict(self.cnt) if ser else None
        ws = tuple(w for w in waits if w is not None)
        if inc == "s_in" and self.cnt.get("s_in"):
            ws = ws + (("s_in", self.cnt["s_in"]),)
        if inc == "s_gd" and self.cnt.get("s_gd"):
            ws = ws + (("s_gd", self.cnt["s_gd"]),)
        ev = None
        if inc is not None:
            self.cnt[inc] = self.cnt.get(inc, 0) + amt
            ev = (inc, self.cnt[inc])
        if snap:
            ws = ws + tuple(snap.items())
        self.ops.append((eng, fn, ws, inc, amt))
        return ev


def build_nc():
    nc = bass.Bass("TRN2", target_bir_lowering=False,
                   detect_race_conditions=KDBG)

    dx = nc.dram_tensor("x_fm", [D, L], F32, kind="ExternalInput")
    dwin = nc.dram_tensor("winT", [D, E], F32, kind="ExternalInput")
    dwout = nc.dram_tensor("woutT", [DI, D], F32, kind="ExternalInput")
    dwx = nc.dram_tensor("wxT_t", [128, GI * 96], F32, kind="ExternalInput")
    dwdt = nc.dram_tensor("wdtT", [128, DI], F32, kind="ExternalInput")
    dconvd = nc.dram_tensor("convd", [128, GI * 4 * 128], F32,
                            kind="ExternalInput")
    dsel = nc.dram_tensor("sel_bf", [128, 37 * 128], BF16,
                          kind="ExternalInput")
    didf = nc.dram_tensor("ident_f32", [128, 128], F32, kind="ExternalInput")
    dalog = nc.dram_tensor("alog_fm", [128, GI * NST], F32,
                           kind="ExternalInput")
    dcols = nc.dram_tensor("cols", [128, 48], F32, kind="ExternalInput")
    dfcols = nc.dram_tensor("fcols", [128, 16], F32, kind="ExternalInput")
    dfw8 = nc.dram_tensor("fwT8", [D, D], I8, kind="ExternalInput")
    dout = nc.dram_tensor("out", [D, L], F32, kind="ExternalOutput")
    ddbg = nc.dram_tensor("dbg", [128, 2 * GI * NST], F32,
                          kind="ExternalOutput")
    dwin_bf = nc.dram_tensor("win_bf", [D, E], BF16)
    dwout_bf = nc.dram_tensor("wout_bf", [DI, D], BF16)
    dfw_bf = nc.dram_tensor("fw_bf", [D, D], BF16)
    dcvd_bf = nc.dram_tensor("cvd_bf", [128, GI * 4 * 128], BF16)
    dxres = nc.dram_tensor("xres", [D, L], F32)
    dccin = nc.dram_tensor("cc_in", [1, 1], F32)
    dccout = nc.dram_tensor("cc_out", [1, 1], F32)

    es = ExitStack()

    def sbt(name, shape, dt=F32):
        return es.enter_context(nc.sbuf_tensor(name, shape, dt))

    x_sb = sbt("x_sb", [128, KD * LQ])
    t1_sb = sbt("t1_sb", [128, KD * LQ], BF16)
    xcr_sb = sbt("xcr_sb", [128, 4 * (LQ + 3)], BF16)
    halo_sb = sbt("halo_sb", [128, GI * 3], BF16)
    g_sb = sbt("g_sb", [128, GI * LQ], BF16)
    xcv_sb = sbt("xcv_sb", [128, GI * LQ], BF16)
    dt_sb = sbt("dt_sb", [128, GI * LQ], BF16)
    y_sb = sbt("y_sb", [128, GI * LQ], BF16)
    rep_sb = sbt("rep_sb", [128, 32 * LQ], BF16)
    xdbl_sb = sbt("xdbl_sb", [128, LQ], BF16)
    da_sb = sbt("da_sb", [128, 2 * LQ], BF16)
    dbx_sb = sbt("dbx_sb", [128, 2 * LQ], BF16)
    h_sb = sbt("h_sb", [128, 2 * LQ], BF16)
    z_sb = sbt("z_sb", [128, 2 * LQ], BF16)
    carry_sb = sbt("carry_sb", [128, 2 * GI * NST], BF16)
    inv_sb = sbt("inv_sb", [128, LQ])
    etmp_sb = sbt("etmp_sb", [128, 2 * LQ])
    lnt_sb = sbt("lnt_sb", [128, LQ])
    m2pad_sb = sbt("m2pad_sb", [128, L], BF16)
    padtmp_sb = sbt("padtmp_sb", [1, 2 * LQ], BF16)
    qrow_sb = sbt("qrow_sb", [1, LQ], BF16)
    qcol_sb = sbt("qcol_sb", [128, 2])
    smal_sb = sbt("smal_sb", [1, 16])
    gpad_sb = sbt("gpad_sb", [128, 4], BF16)
    iscol_sb = sbt("iscol_sb", [128, 1])
    wsx_sb = sbt("wsx_sb", [128, 8])
    fc_sb = sbt("fc_sb", [128, 16])
    a_sb = sbt("a_sb", [128, GI * NST])
    cols_sb = sbt("cols_sb", [128, 48])
    cc_sb = sbt("cc_sb", [128, 4])
    sel_sb = sbt("sel_sb", [128, 37 * 128], BF16)
    idf_sb = sbt("idf_sb", [128, 128])
    onesd_sb = sbt("onesd_sb", [128, 128], BF16)
    wx_sb = sbt("wx_sb", [128, GI * 96], BF16)
    wdt_sb = sbt("wdt_sb", [128, DI], BF16)
    cvrg_sb = sbt("cvrg_sb", [128, 2 * 1024], BF16)
    winrg_sb = sbt("winrg_sb", [128, 3 * 2048], BF16)
    wotrg_sb = sbt("wotrg_sb", [128, 2 * 2048], BF16)
    i8t_sb = sbt("i8t_sb", [128, 2 * 1024], I8)

    ps = [es.enter_context(nc.psum_tensor(f"pb{i}", [128, LQ], F32))
          for i in range(8)]

    sem_names = ["s_in", "s_gd", "s_t", "s_v", "s_s", "s_g", "s_cc"]
    sems = {n: es.enter_context(nc.semaphore(n)) for n in sem_names}

    S = Sched()
    AP = bass.AP

    bank_free = [None] * 8

    idb = lambda: sel_sb[:, 32 * 128:33 * 128]

    acol = lambda g, n: a_sb[:, g * NST + n:g * NST + n + 1]
    convb_c = lambda g: cols_sb[:, g:g + 1]
    bdt_c = lambda g: cols_sb[:, 16 + g:16 + g + 1]
    dp_c = lambda g: cols_sb[:, 32 + g:32 + g + 1]

    xk = lambda k: x_sb[:, k * LQ:(k + 1) * LQ]
    t1k = lambda k: t1_sb[:, k * LQ:(k + 1) * LQ]
    xcs = lambda g, a, b: xcr_sb[:, (g % 4) * (LQ + 3) + a:
                                 (g % 4) * (LQ + 3) + b]
    gg = lambda g: g_sb[:, g * LQ:(g + 1) * LQ]
    xcvg = lambda g: xcv_sb[:, g * LQ:(g + 1) * LQ]
    dtg = lambda g: dt_sb[:, g * LQ:(g + 1) * LQ]
    yg = lambda g: y_sb[:, g * LQ:(g + 1) * LQ]
    repB = lambda n: rep_sb[:, n * LQ:(n + 1) * LQ]
    repC = lambda n: rep_sb[:, (16 + n) * LQ:(17 + n) * LQ]
    selB = lambda n: sel_sb[:, n * 128:(n + 1) * 128]
    selC = lambda n: sel_sb[:, (16 + n) * 128:(17 + n) * 128]
    selR = lambda j: sel_sb[:, (33 + j) * 128:(34 + j) * 128]
    da_s = lambda i: da_sb[:, (i % 2) * LQ:(i % 2 + 1) * LQ]
    dbx_s = lambda i: dbx_sb[:, (i % 2) * LQ:(i % 2 + 1) * LQ]
    h_s = lambda i: h_sb[:, (i % 2) * LQ:(i % 2 + 1) * LQ]
    z_s = lambda i: z_sb[:, (i % 2) * LQ:(i % 2 + 1) * LQ]
    et_s = lambda i: etmp_sb[:, (i % 2) * LQ:(i % 2 + 1) * LQ]
    def wintile(eb, k):
        base = ((eb // 2) % 3) * 2048
        off = base + k * 256 + (eb % 2) * 128
        return winrg_sb[:, off:off + 128]
    wotslot = lambda s: wotrg_sb[:, (s % 2) * 2048:(s % 2 + 1) * 2048]
    cvslot = lambda g: cvrg_sb[:, ((g // 2) % 2) * 1024 + (g % 2) * 512:
                                ((g // 2) % 2) * 1024 + (g % 2 + 1) * 512]

    e_sel = S.op("sync", lambda e: e.dma_start(out=sel_sb[:], in_=dsel.ap()),
                 inc="s_in", amt=16)
    e_idf = S.op("sync", lambda e: e.dma_start(out=idf_sb[:], in_=didf.ap()),
                 inc="s_in", amt=16)
    e_alog = S.op("sync", lambda e: e.dma_start(
        out=x_sb[:, 0:GI * NST], in_=dalog.ap()), inc="s_in", amt=16)
    e_cols = S.op("sync", lambda e: e.dma_start(out=cols_sb[:],
                  in_=dcols.ap()), inc="s_in", amt=16)
    e_fcols = S.op("sync", lambda e: e.dma_start(out=fc_sb[:],
                   in_=dfcols.ap()), inc="s_in", amt=16)
    e_wx = S.op("sync", lambda e: e.dma_start(
        out=x_sb[:, 512:512 + GI * 96], in_=dwx.ap()), inc="s_in", amt=16)
    e_wdt = S.op("sync", lambda e: e.dma_start(
        out=AP(x_sb, 2048, [[KD * LQ, 128], [1, DI]]), in_=dwdt.ap()),
        inc="s_in", amt=16)
    e_cc0 = S.op("vector", lambda e: e.memset(cc_sb[:, 0:1], EPS_LN), inc="s_v")
    e_cc1 = S.op("vector", lambda e: e.memset(cc_sb[:, 1:2], 1.0), inc="s_v")
    e_cc2 = S.op("vector", lambda e: e.memset(cc_sb[:, 2:3], -0.5), inc="s_v")
    e_cc3 = S.op("vector", lambda e: e.memset(cc_sb[:, 3:4], -1.0),
                 inc="s_v")
    e_a1 = S.op("scalar", lambda e: e.activation(
        out=a_sb[:], in_=x_sb[:, 0:GI * NST], func=AF.Exp),
        waits=[e_alog], inc="s_s")
    e_a2 = S.op("scalar", lambda e: e.activation(
        out=a_sb[:], in_=a_sb[:], func=AF.Copy, scale=cc_sb[:, 3:4]),
        waits=[e_cc3], inc="s_s")
    e_wxc = S.op("vector", lambda e: e.tensor_copy(
        wx_sb[:], x_sb[:, 512:512 + GI * 96]), waits=[e_wx], inc="s_v")
    e_wdtc = S.op("vector", lambda e: e.tensor_copy(
        wdt_sb[:], AP(x_sb, 2048, [[KD * LQ, 128], [1, DI]])),
        waits=[e_wdt], inc="s_v")
    e_ms1 = S.op("vector", lambda e: e.memset(onesd_sb[:], ONE_OVER_D),
                 inc="s_v")
    e_ms2 = S.op("vector", lambda e: e.memset(halo_sb[:], 0.0), inc="s_v")
    e_ms3 = S.op("vector", lambda e: e.memset(xdbl_sb[:], 0.0), inc="s_v")
    e_ms4 = S.op("vector", lambda e: e.memset(qcol_sb[:], 0.0), inc="s_v")
    e_ms5 = S.op("vector", lambda e: e.memset(gpad_sb[:], 0.0), inc="s_v")
    e_ms6 = S.op("vector", lambda e: e.memset(m2pad_sb[:], 0.0), inc="s_v")

    stg_cast_ev = {}
    stg_store_ev = {}
    stg_idx = [0]
    win_piece_ev = {}

    def _dst(src_dram):
        return {id(dconvd): dcvd_bf, id(dwin): dwin_bf,
                id(dwout): dwout_bf}[id(src_dram)]

    def _rl(src_dram):
        return {id(dconvd): GI * 4 * 128, id(dwin): E,
                id(dwout): D}[id(src_dram)]

    def stage_x(src_dram, off):
        i = stg_idx[0]
        stg_idx[0] += 1
        sl = i % 4
        evl = S.op("sync", lambda e, sl=sl, off=off, src_dram=src_dram:
                   e.dma_start(out=x_sb[:, sl * 1024:(sl + 1) * 1024],
                               in_=AP(src_dram, off,
                                      [[_rl(src_dram), 128], [1, 1024]])),
                   waits=[stg_cast_ev.get(i - 4)], inc="s_in", amt=16)
        evc = S.op("gpsimd", lambda e, sl=sl: e.tensor_copy(
            rep_sb[:, sl * 1024:(sl + 1) * 1024],
            x_sb[:, sl * 1024:(sl + 1) * 1024]),
            waits=[evl, stg_store_ev.get(i - 4)], inc="s_g")
        stg_cast_ev[i] = evc
        evs = S.op("gpsimd", lambda e, sl=sl, off=off, src_dram=src_dram:
                   e.dma_start(out=AP(_dst(src_dram), off,
                                      [[_rl(src_dram), 128], [1, 1024]]),
                               in_=rep_sb[:, sl * 1024:(sl + 1) * 1024]),
                   waits=[evc], inc="s_gd", amt=16)
        stg_store_ev[i] = evs
        return evs

    for p in range(8):
        stage_x(dconvd, p * 1024)
    e_cvd_st = {1: stg_store_ev[7]}
    for c2 in range(2):
        for j in range(KD):
            ev = stage_x(dwin, j * 128 * E + c2 * 1024)
            win_piece_ev[(j, c2 * 2)] = ev
            win_piece_ev[(j, c2 * 2 + 1)] = ev
    ph1_cast_done = stg_cast_ev[stg_idx[0] - 1]

    e_x0 = S.op("gpsimd", lambda e: e.dma_start(
        out=AP(x_sb, 0, [[KD * LQ, 128], [LQ, KD], [1, LQ]]),
        in_=AP(dx, 0, [[L, 128], [128 * L, KD], [1, LQ]])),
        waits=[e_a1, e_wxc, e_wdtc, ph1_cast_done], inc="s_gd", amt=16)

    fst2 = [etmp_sb[:, 0:LQ], etmp_sb[:, LQ:2 * LQ]]
    h1_cast = {}
    h1_store = {}
    h1_idx = [0]

    def stage_h1(off):
        i = h1_idx[0]
        h1_idx[0] += 1
        sl = i % 2
        evl = S.op("gpsimd", lambda e, sl=sl, off=off: e.dma_start(
            out=fst2[sl], in_=AP(dwin, off, [[E, 128], [1, LQ]])),
            waits=[h1_cast.get(i - 2)], inc="s_gd", amt=16)
        evc = S.op("gpsimd", lambda e, sl=sl: e.tensor_copy(
            rep_sb[:, sl * LQ:(sl + 1) * LQ], fst2[sl]),
            waits=[evl, h1_store.get(i - 2)], inc="s_g")
        h1_cast[i] = evc
        evs = S.op("gpsimd", lambda e, sl=sl, off=off: e.dma_start(
            out=AP(dwin_bf, off, [[E, 128], [1, LQ]]),
            in_=rep_sb[:, sl * LQ:(sl + 1) * LQ]),
            waits=[evc], inc="s_gd", amt=16)
        h1_store[i] = evs
        return evs

    for c in range(4, 8):
        for j in range(KD):
            win_piece_ev[(j, c)] = stage_h1(j * 128 * E + c * LQ)
    wo_ev = {}
    wo_store = {}
    fw_cast = {}
    fw_store = {}
    late_done = [None]

    def emit_late_staging(e_inv1, e_xln, e_t1_free):
        land = [lnt_sb[:, 0:LQ], inv_sb[:, 0:LQ]]
        lgu = [e_inv1, e_xln]
        lc = {}
        ls = {}
        li = [0]

        def st2(off):
            i = li[0]
            li[0] += 1
            sl = i % 2
            cs = i % 4
            evl = S.op("gpsimd", lambda e, sl=sl, off=off: e.dma_start(
                out=land[sl], in_=AP(dwout, off, [[D, 128], [1, LQ]])),
                waits=[lc.get(i - 2), lgu[sl] if i < 2 else None],
                inc="s_gd", amt=16)
            evc = S.op("gpsimd", lambda e, cs=cs, sl=sl: e.tensor_copy(
                t1_sb[:, cs * LQ:(cs + 1) * LQ], land[sl]),
                waits=[evl, ls.get(i - 4),
                       e_t1_free if i < 4 else None], inc="s_g")
            lc[i] = evc
            evs = S.op("gpsimd", lambda e, cs=cs, off=off: e.dma_start(
                out=AP(dwout_bf, off, [[D, 128], [1, LQ]]),
                in_=t1_sb[:, cs * LQ:(cs + 1) * LQ]),
                waits=[evc], inc="s_gd", amt=16)
            ls[i] = evs
            return evs

        for j in range(16):
            for c in range(2):
                wo_ev[(j, c)] = st2(j * 128 * D + c * LQ)
        wo_store[15] = wo_ev[(15, 1)]
        for j in range(8):
            evl = S.op("gpsimd", lambda e, j=j: e.dma_start(
                out=i8t_sb[:, (j % 2) * 1024:(j % 2 + 1) * 1024],
                in_=AP(dfw8, j * 128 * D, [[D, 128], [1, D]])),
                waits=[fw_cast.get(j - 2)], inc="s_gd", amt=16)
            fw_cast[j] = S.op("gpsimd", lambda e, j=j: e.tensor_copy(
                t1_sb[:, 2048 + (j % 2) * 1024:2048 + (j % 2 + 1) * 1024],
                i8t_sb[:, (j % 2) * 1024:(j % 2 + 1) * 1024]),
                waits=[evl, fw_store.get(j - 2),
                       e_t1_free if j < 2 else None],
                inc="s_g")
            fw_store[j] = S.op("gpsimd", lambda e, j=j: e.dma_start(
                out=AP(dfw_bf, j * 128 * D, [[D, 128], [1, D]]),
                in_=t1_sb[:, 2048 + (j % 2) * 1024:
                          2048 + (j % 2 + 1) * 1024]),
                waits=[fw_cast[j]], inc="s_gd", amt=16)
        late_done[0] = fw_store[7]

    startup_x_free = []
    ev_fstage_done = h1_cast[h1_idx[0] - 1]
    ev_stores_done = h1_store[h1_idx[0] - 1]
    rep_startup_guard = [ev_stores_done]
    startup_done = [e_sel, e_idf, e_a2, e_cols, e_fcols, e_ms1]

    last_win_rd = {}
    last_wot_rd = {}
    last_cv_rd = {}
    prev_conv_done = None
    prev_scan_slot = {}
    prev_rep_reader = None
    prev_da_last = None
    ev_xres_store = {}
    prev_xguard = startup_x_free + [e_ms2, e_ms3] + startup_done
    prev_rowdma = None
    prev_inproj_last = None
    qrow_tmp = [None]

    for q in range(NQ):
        qrow_tmp[0] = qrow_sb[0:1, 0:LQ]
        S.stage = 1
        if q == 0:
            e_x = e_x0
        else:
            e_x = S.op("gpsimd", lambda e, q=q: e.dma_start(
                out=AP(x_sb, 0, [[KD * LQ, 128], [LQ, KD], [1, LQ]]),
                in_=AP(dx, q * LQ, [[L, 128], [128 * L, KD], [1, LQ]])),
                waits=prev_xguard, inc="s_gd", amt=16)
        e_cast = None
        for k in range(KD):
            e_cast = S.op("vector", lambda e, k=k: e.tensor_copy(
                dtg(k), xk(k)), waits=[e_x, prev_da_last], inc="s_v")
        e_m = None
        for k in range(KD):
            e_m = S.op("tensor", lambda e, k=k: e.matmul(
                ps[0][:], onesd_sb[:], dtg(k),
                start=(k == 0), stop=(k == KD - 1)),
                waits=[e_cast, bank_free[0], e_ms1], inc="s_t")
        e_t1 = None
        for k in range(KD):
            e_t1 = S.op("vector", lambda e, k=k: e.tensor_tensor(
                out=t1k(k), in0=xk(k), in1=ps[0][:], op=ALU.subtract),
                waits=[e_m, late_done[0] if q == 1 else None], inc="s_v")
        e_sq = None
        for k in range(KD):
            e_sq = S.op("scalar", lambda e, k=k: e.activation(
                out=dtg(k), in_=t1k(k), func=AF.Square),
                waits=[e_t1, e_m], inc="s_s")
        e_v = None
        for k in range(KD):
            e_v = S.op("tensor", lambda e, k=k: e.matmul(
                ps[1][:], onesd_sb[:], dtg(k),
                start=(k == 0), stop=(k == KD - 1)),
                waits=[e_sq, bank_free[1]], inc="s_t")
        e_ln1 = S.op("scalar", lambda e: e.activation(
            out=lnt_sb[:], in_=ps[1][:], func=AF.Ln, bias=cc_sb[:, 0:1]),
            waits=[e_v, e_cc0, e_cc2], inc="s_s")
        e_inv1 = S.op("scalar", lambda e: e.activation(
            out=inv_sb[:], in_=lnt_sb[:], func=AF.Exp, scale=cc_sb[:, 2:3]),
            inc="s_s")
        bank_free[1] = e_ln1
        e_xln = None
        for k in range(KD):
            e_xln = S.op("vector", lambda e, k=k: e.tensor_tensor(
                out=t1k(k), in0=t1k(k), in1=inv_sb[:], op=ALU.mult),
                waits=[e_inv1], inc="s_v")
        bank_free[0] = e_t1

        S.stage = 2
        xc_rdy = {}
        cv_last_ld = [None]
        cv_mm_ev = {}
        g_done = {}
        xconv_done = {}
        ip_free = {2: bank_free[2], 3: bank_free[3]}
        cv_free = {4: bank_free[4], 5: bank_free[5]}
        e_conv_mm_last = None

        def emit_conv(g, q=q):
            nonlocal e_conv_mm_last
            bk = 4 + g % 2
            if g % 2 == 0:
                cv_last_ld[0] = S.op("sync", lambda e, g=g: e.dma_start(
                    out=cvrg_sb[:, ((g // 2) % 2) * 1024:
                                ((g // 2) % 2 + 1) * 1024],
                    in_=AP(dcvd_bf, g * 4 * 128,
                           [[GI * 4 * 128, 128], [1, 1024]])),
                    waits=[last_cv_rd.get((g // 2) % 2),
                           e_cvd_st[1] if q == 0 and g < 2 else None],
                    inc="s_in", amt=16)
            e_cl = cv_last_ld[0]
            e_cm = None
            for kk in range(4):
                e_cm = S.op("tensor", lambda e, g=g, kk=kk, bk=bk: e.matmul(
                    ps[bk][:], cvslot(g)[:, kk * 128:(kk + 1) * 128],
                    xcs(g, kk, kk + LQ),
                    start=(kk == 0), stop=(kk == 3)),
                    waits=[xc_rdy[g], e_cl, cv_free.get(bk)], inc="s_t")
            last_cv_rd[(g // 2) % 2] = e_cm
            cv_mm_ev[g] = e_cm
            e_cs = S.op("scalar", lambda e, g=g, bk=bk: e.activation(
                out=xcvg(g), in_=ps[bk][:], func=AF.Silu,
                bias=convb_c(g)), waits=[e_cm, e_cols], inc="s_s")
            cv_free[bk] = e_cs
            xconv_done[g] = e_cs
            e_conv_mm_last = e_cm

        last_ld = [None]
        for eb in range(NEB):
            if eb >= 4 and (eb - 4) < GI:
                emit_conv(eb - 4)
            if eb % 2 == 0:
                last_ld[0] = S.op("sync", lambda e, eb=eb: e.dma_start(
                    out=AP(winrg_sb, ((eb // 2) % 3) * 2048,
                           [[3 * 2048, 128], [256, KD], [1, 256]]),
                    in_=AP(dwin_bf, eb * 128,
                           [[E, 128], [128 * E, KD], [1, 256]])),
                    waits=[last_win_rd.get((eb // 2) % 3),
                           win_piece_ev[(KD - 1, (eb + 1) // 4)]
                           if q == 0 else None],
                    inc="s_in", amt=16)
            e_ld = last_ld[0]
            bk = 2 + eb % 2
            e_mm = None
            for k in range(KD):
                e_mm = S.op("tensor", lambda e, eb=eb, k=k, bk=bk: e.matmul(
                    ps[bk][:], wintile(eb, k), t1k(k),
                    start=(k == 0), stop=(k == KD - 1)),
                    waits=[e_ld, e_xln, ip_free.get(bk)], inc="s_t")
            last_win_rd[(eb // 2) % 3] = e_mm
            if eb < 16:
                g = eb
                slot_guard = cv_mm_ev.get(g - 4)
                if q > 0 and g < 4:
                    slot_guard = prev_conv_done
                e_e = S.op("vector", lambda e, g=g, bk=bk: e.tensor_copy(
                    xcs(g, 3, LQ + 3), ps[bk][:]),
                    waits=[e_mm, slot_guard], inc="s_v")
                S.op("vector", lambda e, g=g: e.tensor_copy(
                    xcs(g, 0, 3), halo_sb[:, g * 3:g * 3 + 3]),
                    waits=[e_ms2], inc="s_v")
                if q < NQ - 1:
                    e_sv = S.op("vector", lambda e, g=g: e.tensor_copy(
                        halo_sb[:, g * 3:g * 3 + 3], xcs(g, LQ, LQ + 3)),
                        inc="s_v")
                else:
                    e_sv = S.op("vector", lambda e: e.engine_nop(),
                                inc="s_v")
                xc_rdy[g] = e_sv
                ip_free[bk] = e_e
            else:
                e_e = S.op("scalar", lambda e, eb=eb, bk=bk: e.activation(
                    out=gg(eb - 16), in_=ps[bk][:], func=AF.Silu),
                    waits=[e_mm], inc="s_s")
                g_done[eb - 16] = e_e
                ip_free[bk] = e_e
        for g in range(GI - 4, GI):
            emit_conv(g)
        bank_free[2] = ip_free[2]
        bank_free[3] = ip_free[3]
        bank_free[4] = cv_free[4]
        bank_free[5] = cv_free[5]
        prev_conv_done = e_conv_mm_last
        prev_inproj_last = last_win_rd[((NEB - 1) // 2) % 3]

        e_wxm = None
        for g in range(GI):
            e_wxm = S.op("tensor", lambda e, g=g: e.matmul(
                ps[6][0:96, :], wx_sb[:, g * 96:(g + 1) * 96], xcvg(g),
                start=(g == 0), stop=(g == GI - 1)),
                waits=[xconv_done[g], e_wxc, bank_free[6]], inc="s_t")
        e_xdbl = S.op("vector", lambda e: e.tensor_copy(
            xdbl_sb[0:96, :], ps[6][0:96, :]),
            waits=[e_wxm, e_ms3], inc="s_v")
        bank_free[6] = e_xdbl

        dt_done = {}
        dt_free = {7: bank_free[7], 0: bank_free[0]}
        for g in range(GI):
            bk = 7 if g % 2 == 0 else 0
            e_dm = S.op("tensor", lambda e, g=g, bk=bk: e.matmul(
                ps[bk][:], wdt_sb[:, g * 128:(g + 1) * 128],
                xdbl_sb[:], start=True, stop=True),
                waits=[e_xdbl, e_wdtc, dt_free.get(bk),
                       e_inv1 if bk == 0 else None], inc="s_t")
            e_de = S.op("scalar", lambda e, g=g, bk=bk: e.activation(
                out=et_s(g), in_=ps[bk][:], func=AF.Exp, bias=bdt_c(g)),
                waits=[e_dm, ev_fstage_done if q == 0 and g < 2 else None],
                inc="s_s")
            e_dl = S.op("scalar", lambda e, g=g: e.activation(
                out=dtg(g), in_=et_s(g), func=AF.Ln, bias=cc_sb[:, 1:2]),
                waits=[e_cc1], inc="s_s")
            dt_free[bk] = e_de
            dt_done[g] = e_dl
        bank_free[7] = dt_free[7]
        bank_free[0] = dt_free[0]

        xcd_done = {}
        u_done = {}
        for g in range(GI):
            xcd_done[g] = S.op("vector", lambda e, g=g: e.tensor_scalar(
                out=yg(g), in0=xcvg(g), scalar1=dp_c(g), scalar2=None,
                op0=ALU.mult), waits=[xconv_done[g], e_cols], inc="s_v")
            u_done[g] = S.op("vector", lambda e, g=g: e.tensor_tensor(
                out=xcvg(g), in0=dtg(g), in1=xcvg(g), op=ALU.mult),
                waits=[dt_done[g]], inc="s_v")

        repB_ev = {}
        repC_ev = {}
        rb_free = bank_free[6]
        rc_free = bank_free[7]
        for n in range(NST):
            e_rb = S.op("tensor", lambda e, n=n: e.matmul(
                ps[6][:], selB(n), xdbl_sb[:], start=True, stop=True),
                waits=[e_xdbl, e_sel, rb_free], inc="s_t")
            e_rbe = S.op("scalar", lambda e, n=n: e.copy(
                out=repB(n), in_=ps[6][:]),
                waits=[e_rb, prev_rep_reader] +
                      (rep_startup_guard if q == 0 else []), inc="s_s")
            e_rc = S.op("tensor", lambda e, n=n: e.matmul(
                ps[7][:], selC(n), xdbl_sb[:], start=True, stop=True),
                waits=[rc_free], inc="s_t")
            e_rce = S.op("scalar", lambda e, n=n: e.copy(
                out=repC(n), in_=ps[7][:]),
                waits=[e_rc, prev_rep_reader] +
                      (rep_startup_guard if q == 0 else []), inc="s_s")
            rb_free, rc_free = e_rbe, e_rce
            repB_ev[n] = e_rbe
            repC_ev[n] = e_rce
        bank_free[6] = rb_free
        bank_free[7] = rc_free

        S.stage = 4
        gate_ev = {}
        scan_ev = {}
        zmm_ev = {}
        da_ev = {}
        y_free = {2: bank_free[2], 3: bank_free[3]}

        def emit_da(j, q=q):
            if j >= GI * NST:
                return
            gj, nj = j // NST, j % NST
            da_ev[j] = S.op("scalar", lambda e, gj=gj, nj=nj, j=j:
                            e.activation(out=da_s(j), in_=dtg(gj),
                                         func=AF.Exp, scale=acol(gj, nj)),
                            waits=[dt_done[gj], e_a2,
                                   scan_ev.get(j - 2,
                                               prev_scan_slot.get(j % 2))],
                            inc="s_s")

        emit_da(0)
        emit_da(1)
        ii = 0
        for g in range(GI):
            bk = 2 + g % 2
            S.op("tensor", lambda e, g=g, bk=bk: e.matmul(
                ps[bk][:], idb(), yg(g), start=True, stop=False),
                waits=[xcd_done[g], y_free.get(bk)], inc="s_t")
            for n in range(NST):
                i = ii
                ii += 1
                e_da = da_ev[i]
                S.stage = 64
                S.op("vector", lambda e, g=g, n=n, i=i: e.tensor_tensor(
                    out=dbx_s(i), in0=xcvg(g), in1=repB(n), op=ALU.mult),
                    waits=[u_done[g], repB_ev[n]], inc="s_v")
                S.stage = 128
                if q == 0:
                    e_h = S.op("vector", lambda e, i=i:
                               e.tensor_tensor_scan(
                                   out=h_s(i), data0=da_s(i), data1=dbx_s(i),
                                   initial=0.0, op0=ALU.mult, op1=ALU.add),
                               waits=[e_da], inc="s_v")
                else:
                    cr = ((q - 1) % 2) * GI * NST + g * NST + n
                    e_h = S.op("vector", lambda e, cr=cr, i=i:
                               e.tensor_tensor_scan(
                                   out=h_s(i), data0=da_s(i), data1=dbx_s(i),
                                   initial=carry_sb[:, cr:cr + 1],
                                   op0=ALU.mult, op1=ALU.add),
                               waits=[e_da], inc="s_v")
                scan_ev[i] = e_h
                emit_da(i + 2)
                S.stage = 256
                if q < NQ - 1:
                    cw = (q % 2) * GI * NST + g * NST + n
                    S.op("scalar", lambda e, cw=cw, i=i: e.copy(
                        out=carry_sb[:, cw:cw + 1],
                        in_=h_s(i)[:, LQ - 1:LQ]),
                        waits=[scan_ev[i]], inc="s_s")
                S.stage = 512
                e_z = S.op("vector", lambda e, n=n, i=i: e.tensor_tensor(
                    out=z_s(i), in0=h_s(i), in1=repC(n), op=ALU.mult),
                    waits=[repC_ev[n], zmm_ev.get(i - 2)], inc="s_v")
                zmm_ev[i] = S.op("tensor",
                                 lambda e, i=i, n=n, bk=bk: e.matmul(
                                     ps[bk][:], idb(), z_s(i),
                                     start=False, stop=(n == NST - 1)),
                                 waits=[e_z], inc="s_t")
            S.stage = 1024
            e_gate = S.op("vector", lambda e, g=g, bk=bk: e.tensor_tensor(
                out=yg(g), in0=ps[bk][:], in1=gg(g), op=ALU.mult),
                waits=[zmm_ev[ii - 1], g_done[g]], inc="s_v")
            S.stage = 4
            y_free[bk] = e_gate
            gate_ev[g] = e_gate
        bank_free[2] = y_free[2]
        bank_free[3] = y_free[3]
        prev_scan_slot = {0: scan_ev[ii - 2], 1: scan_ev[ii - 1]}
        prev_rep_reader = gate_ev[GI - 1]
        if q == 0:
            emit_late_staging(e_inv1, e_xln, prev_inproj_last)
        prev_da_last = da_ev[GI * NST - 1]

        S.stage = 8
        op_evac = {}
        op_free = {4: bank_free[4], 5: bank_free[5]}
        for dm in range(KD):
            e_wol = S.op("gpsimd", lambda e, dm=dm: e.dma_start(
                out=AP(wotrg_sb, (dm % 2) * 2048,
                       [[2 * 2048, 128], [128, GI], [1, 128]]),
                in_=AP(dwout_bf, dm * 128,
                       [[D, 128], [128 * D, GI], [1, 128]])),
                waits=[last_wot_rd.get(dm % 2)] +
                      ([wo_store[15]] if q == 0 and dm < 2 else []),
                inc="s_gd", amt=16)
            bk = 4 + dm % 2
            e_om = None
            for g in range(GI):
                e_om = S.op("tensor", lambda e, dm=dm, g=g, bk=bk: e.matmul(
                    ps[bk][:], wotslot(dm)[:, g * 128:(g + 1) * 128], yg(g),
                    start=(g == 0), stop=(g == GI - 1)),
                    waits=[e_wol, gate_ev[g], op_free.get(bk)], inc="s_t")
            last_wot_rd[dm % 2] = e_om
            e_oe = S.op("vector", lambda e, dm=dm, bk=bk: e.tensor_tensor(
                out=xk(dm), in0=ps[bk][:], in1=xk(dm), op=ALU.add),
                waits=[e_om], inc="s_v")
            op_evac[dm] = e_oe
            op_free[bk] = e_oe
        bank_free[4] = op_free[4]
        bank_free[5] = op_free[5]

        e_cast2 = None
        for k in range(KD):
            e_cast2 = S.op("vector", lambda e, k=k: e.tensor_copy(
                dtg(k), xk(k)), waits=[op_evac[k], prev_da_last], inc="s_v")
        e_m2 = None
        for k in range(KD):
            e_m2 = S.op("tensor", lambda e, k=k: e.matmul(
                ps[0][:], onesd_sb[:], dtg(k),
                start=(k == 0), stop=(k == KD - 1)),
                waits=[e_cast2, bank_free[0]], inc="s_t")
        e_sq2 = None
        for k in range(KD):
            e_sq2 = S.op("scalar", lambda e, k=k: e.activation(
                out=xcvg(k), in_=dtg(k), func=AF.Square),
                waits=[e_cast2, prev_rep_reader], inc="s_s")
        e_v2 = None
        for k in range(KD):
            e_v2 = S.op("tensor", lambda e, k=k: e.matmul(
                ps[1][:], onesd_sb[:], xcvg(k),
                start=(k == 0), stop=(k == KD - 1)),
                waits=[e_sq2, bank_free[1]], inc="s_t")
        e_c12 = S.op("scalar", lambda e: e.activation(
            out=lnt_sb[:], in_=ps[0][:], func=AF.Square),
            waits=[e_m2, late_done[0] if q == 0 else None], inc="s_s")
        e_c22 = S.op("vector", lambda e: e.tensor_tensor(
            out=lnt_sb[:], in0=ps[1][:], in1=lnt_sb[:], op=ALU.subtract),
            waits=[e_v2, e_c12], inc="s_v")
        e_ln2 = S.op("scalar", lambda e: e.activation(
            out=ps[1][:], in_=lnt_sb[:], func=AF.Ln, bias=cc_sb[:, 0:1]),
            waits=[e_c22], inc="s_s")
        e_inv2 = S.op("scalar", lambda e: e.activation(
            out=inv_sb[:], in_=ps[1][:], func=AF.Exp, scale=cc_sb[:, 2:3]),
            waits=[late_done[0] if q == 0 else None], inc="s_s")
        bank_free[1] = e_inv2
        e_r1 = S.op("vector", lambda e, q=q: e.tensor_copy(
            m2pad_sb[0:1, q * LQ:(q + 1) * LQ], ps[0][0:1, :]),
            waits=[e_m2, e_ms6, prev_rowdma], inc="s_v")
        e_r2 = S.op("vector", lambda e, q=q: e.tensor_tensor(
            out=padtmp_sb[0:1, 0:LQ], in0=ps[0][0:1, :],
            in1=m2pad_sb[0:1, q * LQ:(q + 1) * LQ], op=ALU.subtract),
            inc="s_v")
        e_r3 = S.op("vector", lambda e, q=q: e.tensor_copy(
            padtmp_sb[0:1, LQ:2 * LQ], inv_sb[0:1, :]),
            waits=[e_inv2], inc="s_v")
        e_r4 = S.op("vector", lambda e, q=q: e.tensor_tensor(
            out=padtmp_sb[0:1, 0:LQ] if False else qrow_tmp[0],
            in0=inv_sb[0:1, :], in1=padtmp_sb[0:1, LQ:2 * LQ],
            op=ALU.subtract), inc="s_v")
        e_d1 = S.op("gpsimd", lambda e, q=q: e.dma_start(
            out=AP(m2pad_sb, 32 * L + q * LQ, [[L, 1], [1, LQ]]),
            in_=padtmp_sb[0:1, 0:LQ]), waits=[e_r2], inc="s_gd", amt=16)
        e_d2 = S.op("gpsimd", lambda e, q=q: e.dma_start(
            out=AP(m2pad_sb, 64 * L + q * LQ, [[L, 1], [1, LQ]]),
            in_=padtmp_sb[0:1, LQ:2 * LQ]), waits=[e_r3], inc="s_gd", amt=16)
        e_d3 = S.op("gpsimd", lambda e, q=q: e.dma_start(
            out=AP(m2pad_sb, 96 * L + q * LQ, [[L, 1], [1, LQ]]),
            in_=qrow_tmp[0]), waits=[e_r4], inc="s_gd", amt=16)
        prev_rowdma = e_d3
        e_qp = None
        for k in range(KD):
            S.op("vector", lambda e, k=k: e.tensor_tensor(
                out=et_s(k), in0=xk(k), in1=ps[0][:], op=ALU.subtract),
                waits=[e_inv2], inc="s_v")
            S.op("vector", lambda e, k=k: e.tensor_tensor(
                out=et_s(k), in0=et_s(k), in1=inv_sb[:], op=ALU.mult),
                inc="s_v")
            S.op("vector", lambda e, k=k: e.tensor_reduce(
                out=qcol_sb[:, 1:2], in_=et_s(k), axis=mybir.AxisListType.X,
                op=ALU.max, apply_absolute_value=True),
                waits=[e_ms4], inc="s_v")
            e_qp = S.op("vector", lambda e: e.tensor_tensor(
                out=qcol_sb[:, 0:1], in0=qcol_sb[:, 0:1],
                in1=qcol_sb[:, 1:2], op=ALU.max), inc="s_v")
        bank_free[0] = e_qp
        ev_xres_store[q] = S.op("gpsimd", lambda e, q=q: e.dma_start(
            out=AP(dxres, q * LQ, [[L, 128], [128 * L, KD], [1, LQ]]),
            in_=AP(x_sb, 0, [[KD * LQ, 128], [LQ, KD], [1, LQ]])),
            waits=[op_evac[KD - 1]], inc="s_gd", amt=16)
        prev_xguard = [ev_xres_store[q], e_qp, e_r1, e_d3]
        prev_da_last = e_sq2

    S.stage = 2048
    e_tr = S.op("tensor", lambda e: e.transpose(
        ps[6][0:1, 0:128], qcol_sb[:, 0:1], idf_sb[:]),
        waits=[e_qp, e_idf, bank_free[6]], inc="s_t")
    e_gm = S.op("vector", lambda e: e.tensor_reduce(
        out=smal_sb[0:1, 0:1], in_=ps[6][0:1, 0:128],
        axis=mybir.AxisListType.X, op=ALU.max), waits=[e_tr], inc="s_v")
    bank_free[6] = e_gm
    e_st = S.op("gpsimd", lambda e: e.dma_start(
        out=AP(dccin, 0, [[1, 1], [1, 1]]), in_=smal_sb[0:1, 0:1]),
        waits=[e_gm], inc="s_gd", amt=16)
    e_cc = S.op("gpsimd", lambda e: e.collective_compute(
        "AllReduce", ALU.max, replica_groups=[list(range(8))],
        ins=[dccin.ap().opt()], outs=[dccout.ap().opt()]),
        waits=[e_st], inc="s_cc")
    e_ccl = S.op("gpsimd", lambda e: e.dma_start(
        out=smal_sb[0:1, 2:3], in_=dccout.ap()), waits=[e_cc],
        inc="s_gd", amt=16)
    e_xs = S.op("vector", lambda e: e.tensor_scalar(
        out=smal_sb[0:1, 1:2], in0=smal_sb[0:1, 2:3], scalar1=1.0 / 127.0,
        scalar2=EPS_FFN, op0=ALU.mult, op1=ALU.max),
        waits=[e_ccl], inc="s_v")
    e_is = S.op("vector", lambda e: e.reciprocal(
        out=smal_sb[0:1, 3:4], in_=smal_sb[0:1, 1:2]), inc="s_v")
    e_gp1 = S.op("vector", lambda e: e.tensor_copy(
        gpad_sb[0:1, 0:1], smal_sb[0:1, 1:2]),
        waits=[e_ms5, e_xs], inc="s_v")
    e_gp2 = S.op("vector", lambda e: e.tensor_tensor(
        out=gpad_sb[0:1, 1:2], in0=smal_sb[0:1, 1:2], in1=gpad_sb[0:1, 0:1],
        op=ALU.subtract), inc="s_v")
    e_gp3 = S.op("vector", lambda e: e.tensor_copy(
        gpad_sb[0:1, 2:3], smal_sb[0:1, 3:4]), waits=[e_is], inc="s_v")
    e_gp4 = S.op("vector", lambda e: e.tensor_tensor(
        out=gpad_sb[0:1, 3:4], in0=smal_sb[0:1, 3:4], in1=gpad_sb[0:1, 2:3],
        op=ALU.subtract), inc="s_v")
    e_xsc = None
    for j in range(2):
        e_xsc = S.op("tensor", lambda e, j=j: e.matmul(
            ps[7][:, 0:1], selR(0), gpad_sb[:, j:j + 1],
            start=(j == 0), stop=(j == 1)),
            waits=[e_gp2, e_sel, bank_free[7]], inc="s_t")
    e_wsx = S.op("vector", lambda e: e.tensor_scalar(
        out=wsx_sb[:], in0=fc_sb[:, 0:8], scalar1=ps[7][:, 0:1],
        scalar2=None, op0=ALU.mult), waits=[e_xsc, e_fcols], inc="s_v")
    bank_free[7] = e_wsx
    e_isc = None
    for j in range(2):
        e_isc = S.op("tensor", lambda e, j=j: e.matmul(
            ps[6][:, 0:1], selR(0), gpad_sb[:, 2 + j:3 + j],
            start=(j == 0), stop=(j == 1)),
            waits=[e_gp4, bank_free[6]], inc="s_t")
    e_iscol = S.op("vector", lambda e: e.tensor_copy(
        iscol_sb[:], ps[6][:, 0:1]), waits=[e_isc], inc="s_v")
    bank_free[6] = e_iscol
    e_fw = S.op("sync", lambda e: e.dma_start(
        out=rep_sb[:, 0:8192],
        in_=AP(dfw_bf, 0, [[D, 128], [128 * D, 8], [1, D]])),
        waits=[prev_rep_reader, fw_store[7]], inc="s_in", amt=16)
    fwt = lambda k, nb: rep_sb[:, k * 1024 + nb * 128:
                               k * 1024 + (nb + 1) * 128]

    S.stage = 16
    prev_ffn_done = None
    ps0_free = bank_free[0]
    ps1_free = bank_free[1]
    ffn_free = {2: bank_free[2], 3: bank_free[3]}
    for q in range(NQ):
        e_xl = S.op("gpsimd", lambda e, q=q: e.dma_start(
            out=AP(x_sb, 0, [[KD * LQ, 128], [LQ, KD], [1, LQ]]),
            in_=AP(dxres, q * LQ, [[L, 128], [128 * L, KD], [1, LQ]])),
            waits=[e_iscol, e_wsx, prev_ffn_done, e_qp if q == 0 else None],
            inc="s_gd", amt=16)
        e_m2r = None
        for j in range(2):
            e_m2r = S.op("tensor", lambda e, q=q, j=j: e.matmul(
                ps[0][:], selR(j), m2pad_sb[:, q * LQ:(q + 1) * LQ],
                start=(j == 0), stop=(j == 1)),
                waits=[prev_rowdma, ps0_free], inc="s_t")
        e_sir = None
        for j in range(2):
            e_sir = S.op("tensor", lambda e, q=q, j=j: e.matmul(
                ps[1][:], selR(2 + j), m2pad_sb[:, q * LQ:(q + 1) * LQ],
                start=(j == 0), stop=(j == 1)),
                waits=[prev_rowdma, ps1_free], inc="s_t")
        e_sirs = S.op("vector", lambda e: e.tensor_scalar(
            out=inv_sb[:], in0=ps[1][:], scalar1=iscol_sb[:, 0:1],
            scalar2=None, op0=ALU.mult), waits=[e_sir, e_iscol], inc="s_v")
        e_xq = {}
        for k in range(KD):
            S.op("vector", lambda e, k=k: e.tensor_tensor(
                out=et_s(k), in0=xk(k), in1=ps[0][:], op=ALU.subtract),
                waits=[e_xl, e_m2r, prev_ffn_done], inc="s_v")
            S.op("vector", lambda e, k=k: e.tensor_tensor(
                out=et_s(k), in0=et_s(k), in1=inv_sb[:], op=ALU.mult),
                waits=[e_sirs], inc="s_v")
            e_xq[k] = S.op("vector", lambda e, k=k: e.tensor_scalar(
                out=dtg(k), in0=et_s(k), scalar1=MAGIC, scalar2=MAGIC,
                op0=ALU.add, op1=ALU.subtract), inc="s_v")
        ps0_free = e_xq[KD - 1]
        ps1_free = e_sirs
        e_fin = {}
        for nb in range(KD):
            bk = 2 + nb % 2
            e_fm = None
            for k in range(KD):
                e_fm = S.op("tensor", lambda e, nb=nb, k=k, bk=bk: e.matmul(
                    ps[bk][:], fwt(k, nb), dtg(k),
                    start=(k == 0), stop=(k == KD - 1)),
                    waits=[e_fw, e_xq[k], ffn_free.get(bk)], inc="s_t")
            e_fe = S.op("scalar", lambda e, nb=nb, bk=bk: e.activation(
                out=et_s(nb), in_=ps[bk][:], func=AF.Copy,
                scale=wsx_sb[:, nb:nb + 1]),
                waits=[e_fm, e_wsx, e_xq[KD - 1], e_fin.get(nb - 2)],
                inc="s_s")
            e_fa = S.op("vector", lambda e, nb=nb: e.scalar_tensor_tensor(
                out=xk(nb), in0=et_s(nb), scalar=fc_sb[:, 8 + nb:9 + nb],
                in1=xk(nb), op0=ALU.add, op1=ALU.add),
                waits=[e_fe, e_fcols], inc="s_v")
            ffn_free[bk] = e_fa
            e_fin[nb] = e_fa
        prev_ffn_done = e_fin[KD - 1]
        S.op("gpsimd", lambda e, q=q: e.dma_start(
            out=AP(dout, q * LQ, [[L, 128], [128 * L, KD], [1, LQ]]),
            in_=AP(x_sb, 0, [[KD * LQ, 128], [LQ, KD], [1, LQ]])),
            waits=[e_fin[KD - 1]], inc="s_gd", amt=16)

    S.op("vector", lambda e: e.tensor_copy(
        etmp_sb[:, 0:2 * GI * NST], carry_sb[:]), inc="s_v")
    S.op("gpsimd", lambda e: e.dma_start(
        out=ddbg.ap(), in_=etmp_sb[:, 0:2 * GI * NST]),
        waits=[("s_v", S.cnt["s_v"])], inc="s_gd", amt=16)
    final_gd = S.cnt["s_gd"]
    S.op("gpsimd", lambda e: e.nop(), waits=[("s_gd", final_gd)], inc=None)

    with es:
        block = es.enter_context(nc.Block())

        def replay(eng_name):
            def _(eng):
                hw = {}
                for (en, fn, waits, inc, amt) in S.ops:
                    if en != eng_name:
                        continue
                    for (sn, val) in waits:
                        if hw.get(sn, 0) < val:
                            eng.wait_ge(sems[sn], val)
                            hw[sn] = val
                    instr = fn(eng)
                    if inc is not None:
                        instr.then_inc(sems[inc], amt)
                    if KDBG and eng_name != "sync":
                        try:
                            eng.drain()
                        except Exception:
                            pass
            return _

        block.sync(replay("sync"))
        block.vector(replay("vector"))
        block.scalar(replay("scalar"))
        block.tensor(replay("tensor"))
        block.gpsimd(replay("gpsimd"))
    return nc


def kernel(L0_attn_x, L0_attn_W_in, L0_attn_conv_w, L0_attn_conv_b,
           L0_attn_W_x, L0_attn_W_dt, L0_attn_b_dt, L0_attn_A_log,
           L0_attn_Dp, L0_attn_W_out, L0_ffn_weight_int8,
           L0_ffn_weight_scale, L0_ffn_bias):
    import ml_dtypes
    x = np.asarray(L0_attn_x, dtype=np.float32)
    W_in = np.asarray(L0_attn_W_in, dtype=np.float32)
    conv_w = np.asarray(L0_attn_conv_w, dtype=np.float32)
    conv_b = np.asarray(L0_attn_conv_b, dtype=np.float32)
    W_x = np.asarray(L0_attn_W_x, dtype=np.float32)
    W_dt = np.asarray(L0_attn_W_dt, dtype=np.float32)
    b_dt = np.asarray(L0_attn_b_dt, dtype=np.float32)
    A_log = np.asarray(L0_attn_A_log, dtype=np.float32)
    Dp = np.asarray(L0_attn_Dp, dtype=np.float32)
    W_out = np.asarray(L0_attn_W_out, dtype=np.float32)
    fw8 = np.asarray(L0_ffn_weight_int8)
    fws = np.asarray(L0_ffn_weight_scale, dtype=np.float32)
    fb = np.asarray(L0_ffn_bias, dtype=np.float32)

    x_fm = np.ascontiguousarray(np.swapaxes(x, 1, 2))
    winT = np.ascontiguousarray(W_in.T)
    woutT = np.ascontiguousarray(W_out.T)
    wxT_t = np.ascontiguousarray(
        W_x.T.reshape(GI, 128, 96).transpose(1, 0, 2).reshape(128, GI * 96))
    wdtT = np.zeros((128, DI), np.float32)
    wdtT[0:RNK, :] = W_dt.T
    fwT8 = np.ascontiguousarray(fw8.T)
    convd = np.zeros((128, GI * 4 * 128), np.float32)
    for g in range(GI):
        for kk in range(4):
            blk = convd[:, (g * 4 + kk) * 128:(g * 4 + kk + 1) * 128]
            np.fill_diagonal(blk, conv_w[g * 128:(g + 1) * 128, kk])
    sel = np.zeros((128, 37 * 128), ml_dtypes.bfloat16)
    for n in range(NST):
        sel[64 + n, n * 128:(n + 1) * 128] = 1
        sel[80 + n, (16 + n) * 128:(17 + n) * 128] = 1
    sel[:, 32 * 128:33 * 128] = np.eye(128, dtype=ml_dtypes.bfloat16)
    for j, row in enumerate((0, 32, 64, 96)):
        sel[row, (33 + j) * 128:(34 + j) * 128] = 1
    identf = np.eye(128, dtype=np.float32)
    alog_fm = np.ascontiguousarray(
        A_log.reshape(GI, 128, NST).transpose(1, 0, 2).reshape(128, GI * NST))
    colify = lambda v: np.ascontiguousarray(v.reshape(GI, 128).T)
    cols = np.concatenate([colify(conv_b), colify(b_dt), colify(Dp)], axis=1)
    colify8 = lambda v: np.ascontiguousarray(v.reshape(8, 128).T)
    fcols = np.concatenate([colify8(fws[:, 0]), colify8(fb)], axis=1)

    nc = build_nc()
    in_maps = []
    for c in range(8):
        in_maps.append({
            "x_fm": np.ascontiguousarray(x_fm[c]),
            "winT": winT, "woutT": woutT, "wxT_t": wxT_t, "wdtT": wdtT,
            "convd": convd, "sel_bf": sel,
            "ident_f32": identf, "alog_fm": alog_fm, "cols": cols,
            "fcols": fcols, "fwT8": fwT8,
        })
    res = run_bass_kernel_spmd(nc, in_maps, core_ids=list(range(8)))
    out_fm = np.stack([np.asarray(r["out"]) for r in res.results])
    return np.ascontiguousarray(np.swapaxes(out_fm, 1, 2))


if __name__ == "__main__":
    pass


# version 11
# speedup vs baseline: 1.3156x; 1.0205x over previous
import os
import numpy as np
from contextlib import ExitStack

KDBG = os.environ.get("KDBG", "0") == "1"
KSER = os.environ.get("KSER", "0") == "1"
KSERC = int(os.environ.get("KSERC", "0"))
KSERV = int(os.environ.get("KSERV", "2048"))

import concourse.bass as bass
import concourse.mybir as mybir
from concourse.bass_utils import run_bass_kernel_spmd

F32 = mybir.dt.float32
BF16 = mybir.dt.bfloat16
I8 = mybir.dt.int8
AF = mybir.ActivationFunctionType
ALU = mybir.AluOpType

MAGIC = 12582912.0
EPS_LN = 1e-5
EPS_FFN = 1e-5
ONE_OVER_D = 1.0 / 1024.0

L = 2048
D = 1024
DI = 2048
E = 4096
NST = 16
RNK = 64
NQ = 4
LQ = 512
KD = 8
GI = 16
NEB = 32


class Sched:

    def __init__(self):
        self.ops = []
        self.cnt = {}

    CLS = {"sync": 1, "tensor": 2, "vector": 4, "scalar": 8, "gpsimd": 16}
    stage = 32

    def op(self, eng, fn, waits=(), inc=None, amt=1):
        ser = KSER or (KSERC & self.CLS[eng]) or (
            eng == "vector" and (KSERV & self.stage))
        snap = dict(self.cnt) if ser else None
        ws = tuple(w for w in waits if w is not None)
        if inc == "s_in" and self.cnt.get("s_in"):
            ws = ws + (("s_in", self.cnt["s_in"]),)
        if inc == "s_gd" and self.cnt.get("s_gd"):
            ws = ws + (("s_gd", self.cnt["s_gd"]),)
        if inc == "s_g2" and self.cnt.get("s_g2"):
            ws = ws + (("s_g2", self.cnt["s_g2"]),)
        ev = None
        if inc is not None:
            self.cnt[inc] = self.cnt.get(inc, 0) + amt
            ev = (inc, self.cnt[inc])
        if snap:
            ws = ws + tuple(snap.items())
        self.ops.append((eng, fn, ws, inc, amt))
        return ev


def build_nc():
    nc = bass.Bass("TRN2", target_bir_lowering=False,
                   detect_race_conditions=KDBG)

    dx = nc.dram_tensor("x_fm", [D, L], F32, kind="ExternalInput")
    dwin = nc.dram_tensor("winT", [D, E], F32, kind="ExternalInput")
    dwout = nc.dram_tensor("woutT", [DI, D], F32, kind="ExternalInput")
    dwx = nc.dram_tensor("wxT_t", [128, GI * 96], F32, kind="ExternalInput")
    dwdt = nc.dram_tensor("wdtT", [128, DI], F32, kind="ExternalInput")
    dconvd = nc.dram_tensor("convd", [128, GI * 4 * 128], F32,
                            kind="ExternalInput")
    dsel = nc.dram_tensor("sel_bf", [128, 37 * 128], BF16,
                          kind="ExternalInput")
    didf = nc.dram_tensor("ident_f32", [128, 128], F32, kind="ExternalInput")
    dalog = nc.dram_tensor("alog_fm", [128, GI * NST], F32,
                           kind="ExternalInput")
    dcols = nc.dram_tensor("cols", [128, 48], F32, kind="ExternalInput")
    dfcols = nc.dram_tensor("fcols", [128, 16], F32, kind="ExternalInput")
    dfw8 = nc.dram_tensor("fwT8", [D, D], I8, kind="ExternalInput")
    dout = nc.dram_tensor("out", [D, L], F32, kind="ExternalOutput")
    ddbg = nc.dram_tensor("dbg", [128, 2 * GI * NST], F32,
                          kind="ExternalOutput")
    dwin_bf = nc.dram_tensor("win_bf", [D, E], BF16)
    dwout_bf = nc.dram_tensor("wout_bf", [DI, D], BF16)
    dfw_bf = nc.dram_tensor("fw_bf", [D, D], BF16)
    dcvd_bf = nc.dram_tensor("cvd_bf", [128, GI * 4 * 128], BF16)
    dxres = nc.dram_tensor("xres", [D, L], F32)
    dccin = nc.dram_tensor("cc_in", [1, 1], F32)
    dccout = nc.dram_tensor("cc_out", [1, 1], F32)

    es = ExitStack()

    def sbt(name, shape, dt=F32):
        return es.enter_context(nc.sbuf_tensor(name, shape, dt))

    x_sb = sbt("x_sb", [128, KD * LQ])
    t1_sb = sbt("t1_sb", [128, KD * LQ], BF16)
    xcr_sb = sbt("xcr_sb", [128, 4 * (LQ + 3)], BF16)
    halo_sb = sbt("halo_sb", [128, GI * 3], BF16)
    g_sb = sbt("g_sb", [128, GI * LQ], BF16)
    xcv_sb = sbt("xcv_sb", [128, GI * LQ], BF16)
    dt_sb = sbt("dt_sb", [128, GI * LQ], BF16)
    y_sb = sbt("y_sb", [128, GI * LQ], BF16)
    rep_sb = sbt("rep_sb", [128, 32 * LQ], BF16)
    xdbl_sb = sbt("xdbl_sb", [128, LQ], BF16)
    da_sb = sbt("da_sb", [128, 2 * LQ], BF16)
    dbx_sb = sbt("dbx_sb", [128, 2 * LQ], BF16)
    h_sb = sbt("h_sb", [128, 2 * LQ], BF16)
    z_sb = sbt("z_sb", [128, 2 * LQ], BF16)
    carry_sb = sbt("carry_sb", [128, 2 * GI * NST], BF16)
    inv_sb = sbt("inv_sb", [128, LQ])
    etmp_sb = sbt("etmp_sb", [128, 2 * LQ])
    lnt_sb = sbt("lnt_sb", [128, LQ])
    m2pad_sb = sbt("m2pad_sb", [128, L], BF16)
    padtmp_sb = sbt("padtmp_sb", [1, 2 * LQ], BF16)
    qrow_sb = sbt("qrow_sb", [1, LQ], BF16)
    qcol_sb = sbt("qcol_sb", [128, 2])
    smal_sb = sbt("smal_sb", [1, 16])
    gpad_sb = sbt("gpad_sb", [128, 4], BF16)
    iscol_sb = sbt("iscol_sb", [128, 1])
    wsx_sb = sbt("wsx_sb", [128, 8])
    fc_sb = sbt("fc_sb", [128, 16])
    a_sb = sbt("a_sb", [128, GI * NST])
    cols_sb = sbt("cols_sb", [128, 48])
    cc_sb = sbt("cc_sb", [128, 4])
    sel_sb = sbt("sel_sb", [128, 37 * 128], BF16)
    idf_sb = sbt("idf_sb", [128, 128])
    onesd_sb = sbt("onesd_sb", [128, 128], BF16)
    wx_sb = sbt("wx_sb", [128, GI * 96], BF16)
    wdt_sb = sbt("wdt_sb", [128, DI], BF16)
    cvrg_sb = sbt("cvrg_sb", [128, 2 * 1024], BF16)
    winrg_sb = sbt("winrg_sb", [128, 3 * 2048], BF16)
    wotrg_sb = sbt("wotrg_sb", [128, 2 * 2048], BF16)
    i8t_sb = sbt("i8t_sb", [128, 2 * 1024], I8)

    ps = [es.enter_context(nc.psum_tensor(f"pb{i}", [128, LQ], F32))
          for i in range(8)]

    sem_names = ["s_in", "s_gd", "s_g2", "s_t", "s_v", "s_s", "s_g",
                 "s_cc"]
    sems = {n: es.enter_context(nc.semaphore(n)) for n in sem_names}

    S = Sched()
    AP = bass.AP

    bank_free = [None] * 8

    idb = lambda: sel_sb[:, 32 * 128:33 * 128]

    acol = lambda g, n: a_sb[:, g * NST + n:g * NST + n + 1]
    convb_c = lambda g: cols_sb[:, g:g + 1]
    bdt_c = lambda g: cols_sb[:, 16 + g:16 + g + 1]
    dp_c = lambda g: cols_sb[:, 32 + g:32 + g + 1]

    xk = lambda k: x_sb[:, k * LQ:(k + 1) * LQ]
    t1k = lambda k: t1_sb[:, k * LQ:(k + 1) * LQ]
    xcs = lambda g, a, b: xcr_sb[:, (g % 4) * (LQ + 3) + a:
                                 (g % 4) * (LQ + 3) + b]
    gg = lambda g: g_sb[:, g * LQ:(g + 1) * LQ]
    xcvg = lambda g: xcv_sb[:, g * LQ:(g + 1) * LQ]
    dtg = lambda g: dt_sb[:, g * LQ:(g + 1) * LQ]
    yg = lambda g: y_sb[:, g * LQ:(g + 1) * LQ]
    repB = lambda n: rep_sb[:, n * LQ:(n + 1) * LQ]
    repC = lambda n: rep_sb[:, (16 + n) * LQ:(17 + n) * LQ]
    selB = lambda n: sel_sb[:, n * 128:(n + 1) * 128]
    selC = lambda n: sel_sb[:, (16 + n) * 128:(17 + n) * 128]
    selR = lambda j: sel_sb[:, (33 + j) * 128:(34 + j) * 128]
    da_s = lambda i: da_sb[:, (i % 2) * LQ:(i % 2 + 1) * LQ]
    dbx_s = lambda i: dbx_sb[:, (i % 2) * LQ:(i % 2 + 1) * LQ]
    h_s = lambda i: h_sb[:, (i % 2) * LQ:(i % 2 + 1) * LQ]
    z_s = lambda i: z_sb[:, (i % 2) * LQ:(i % 2 + 1) * LQ]
    et_s = lambda i: etmp_sb[:, (i % 2) * LQ:(i % 2 + 1) * LQ]
    def wintile(eb, k):
        base = ((eb // 2) % 3) * 2048
        off = base + k * 256 + (eb % 2) * 128
        return winrg_sb[:, off:off + 128]
    wotslot = lambda s: wotrg_sb[:, (s % 2) * 2048:(s % 2 + 1) * 2048]
    cvslot = lambda g: cvrg_sb[:, ((g // 2) % 2) * 1024 + (g % 2) * 512:
                                ((g // 2) % 2) * 1024 + (g % 2 + 1) * 512]

    e_sel = S.op("sync", lambda e: e.dma_start(out=sel_sb[:], in_=dsel.ap()),
                 inc="s_in", amt=16)
    e_idf = S.op("sync", lambda e: e.dma_start(out=idf_sb[:], in_=didf.ap()),
                 inc="s_in", amt=16)
    e_alog = S.op("sync", lambda e: e.dma_start(
        out=x_sb[:, 0:GI * NST], in_=dalog.ap()), inc="s_in", amt=16)
    e_cols = S.op("sync", lambda e: e.dma_start(out=cols_sb[:],
                  in_=dcols.ap()), inc="s_in", amt=16)
    e_fcols = S.op("sync", lambda e: e.dma_start(out=fc_sb[:],
                   in_=dfcols.ap()), inc="s_in", amt=16)
    e_wx = S.op("sync", lambda e: e.dma_start(
        out=x_sb[:, 512:512 + GI * 96], in_=dwx.ap()), inc="s_in", amt=16)
    e_wdt = S.op("sync", lambda e: e.dma_start(
        out=AP(x_sb, 2048, [[KD * LQ, 128], [1, DI]]), in_=dwdt.ap()),
        inc="s_in", amt=16)
    e_cc0 = S.op("vector", lambda e: e.memset(cc_sb[:, 0:1], EPS_LN), inc="s_v")
    e_cc1 = S.op("vector", lambda e: e.memset(cc_sb[:, 1:2], 1.0), inc="s_v")
    e_cc2 = S.op("vector", lambda e: e.memset(cc_sb[:, 2:3], -0.5), inc="s_v")
    e_cc3 = S.op("vector", lambda e: e.memset(cc_sb[:, 3:4], -1.0),
                 inc="s_v")
    e_a1 = S.op("scalar", lambda e: e.activation(
        out=a_sb[:], in_=x_sb[:, 0:GI * NST], func=AF.Exp),
        waits=[e_alog], inc="s_s")
    e_a2 = S.op("scalar", lambda e: e.activation(
        out=a_sb[:], in_=a_sb[:], func=AF.Copy, scale=cc_sb[:, 3:4]),
        waits=[e_cc3], inc="s_s")
    e_wxc = S.op("vector", lambda e: e.tensor_copy(
        wx_sb[:], x_sb[:, 512:512 + GI * 96]), waits=[e_wx], inc="s_v")
    e_wdtc = S.op("vector", lambda e: e.tensor_copy(
        wdt_sb[:], AP(x_sb, 2048, [[KD * LQ, 128], [1, DI]])),
        waits=[e_wdt], inc="s_v")
    e_ms1 = S.op("vector", lambda e: e.memset(onesd_sb[:], ONE_OVER_D),
                 inc="s_v")
    e_ms2 = S.op("vector", lambda e: e.memset(halo_sb[:], 0.0), inc="s_v")
    e_ms3 = S.op("vector", lambda e: e.memset(xdbl_sb[:], 0.0), inc="s_v")
    e_ms4 = S.op("vector", lambda e: e.memset(qcol_sb[:], 0.0), inc="s_v")
    e_ms5 = S.op("vector", lambda e: e.memset(gpad_sb[:], 0.0), inc="s_v")
    e_ms6 = S.op("vector", lambda e: e.memset(m2pad_sb[:], 0.0), inc="s_v")

    stg_cast_ev = {}
    stg_store_ev = {}
    stg_idx = [0]
    win_piece_ev = {}

    def _dst(src_dram):
        return {id(dconvd): dcvd_bf, id(dwin): dwin_bf,
                id(dwout): dwout_bf}[id(src_dram)]

    def _rl(src_dram):
        return {id(dconvd): GI * 4 * 128, id(dwin): E,
                id(dwout): D}[id(src_dram)]

    def stage_x(src_dram, off):
        i = stg_idx[0]
        stg_idx[0] += 1
        sl = i % 4
        evl = S.op("sync", lambda e, sl=sl, off=off, src_dram=src_dram:
                   e.dma_start(out=x_sb[:, sl * 1024:(sl + 1) * 1024],
                               in_=AP(src_dram, off,
                                      [[_rl(src_dram), 128], [1, 1024]])),
                   waits=[stg_cast_ev.get(i - 4)], inc="s_in", amt=16)
        evc = S.op("gpsimd", lambda e, sl=sl: e.tensor_copy(
            rep_sb[:, sl * 1024:(sl + 1) * 1024],
            x_sb[:, sl * 1024:(sl + 1) * 1024]),
            waits=[evl, stg_store_ev.get(i - 4)], inc="s_g")
        stg_cast_ev[i] = evc
        evs = S.op("gpsimd", lambda e, sl=sl, off=off, src_dram=src_dram:
                   e.dma_start(out=AP(_dst(src_dram), off,
                                      [[_rl(src_dram), 128], [1, 1024]]),
                               in_=rep_sb[:, sl * 1024:(sl + 1) * 1024]),
                   waits=[evc], inc="s_gd", amt=16)
        stg_store_ev[i] = evs
        return evs

    for p in range(8):
        stage_x(dconvd, p * 1024)
    e_cvd_st = {1: stg_store_ev[7]}
    for c2 in range(2):
        for j in range(KD):
            ev = stage_x(dwin, j * 128 * E + c2 * 1024)
            win_piece_ev[(j, c2 * 2)] = ev
            win_piece_ev[(j, c2 * 2 + 1)] = ev
    ph1_cast_done = stg_cast_ev[stg_idx[0] - 1]

    e_x0 = S.op("gpsimd", lambda e: e.dma_start(
        out=AP(x_sb, 0, [[KD * LQ, 128], [LQ, KD], [1, LQ]]),
        in_=AP(dx, 0, [[L, 128], [128 * L, KD], [1, LQ]])),
        waits=[e_a1, e_wxc, e_wdtc, ph1_cast_done], inc="s_gd", amt=16)

    fst2 = [etmp_sb[:, 0:LQ], etmp_sb[:, LQ:2 * LQ]]
    h1_cast = {}
    h1_store = {}
    h1_idx = [0]

    def stage_h1(off):
        i = h1_idx[0]
        h1_idx[0] += 1
        sl = i % 2
        evl = S.op("gpsimd", lambda e, sl=sl, off=off: e.dma_start(
            out=fst2[sl], in_=AP(dwin, off, [[E, 128], [1, LQ]])),
            waits=[h1_cast.get(i - 2)], inc="s_g2", amt=16)
        evc = S.op("gpsimd", lambda e, sl=sl: e.tensor_copy(
            rep_sb[:, sl * LQ:(sl + 1) * LQ], fst2[sl]),
            waits=[evl, h1_store.get(i - 2)], inc="s_g")
        h1_cast[i] = evc
        evs = S.op("gpsimd", lambda e, sl=sl, off=off: e.dma_start(
            out=AP(dwin_bf, off, [[E, 128], [1, LQ]]),
            in_=rep_sb[:, sl * LQ:(sl + 1) * LQ]),
            waits=[evc], inc="s_gd", amt=16)
        h1_store[i] = evs
        return evs

    for c in range(4, 8):
        for j in range(KD):
            win_piece_ev[(j, c)] = stage_h1(j * 128 * E + c * LQ)
    wo_ev = {}
    wo_store = {}
    fw_cast = {}
    fw_store = {}
    late_done = [None]

    def emit_late_staging(e_inv1, e_xln, e_t1_free):
        land = [lnt_sb[:, 0:LQ], inv_sb[:, 0:LQ]]
        lgu = [e_inv1, e_xln]
        lc = {}
        ls = {}
        li = [0]

        def st2(off):
            i = li[0]
            li[0] += 1
            sl = i % 2
            cs = i % 4
            evl = S.op("gpsimd", lambda e, sl=sl, off=off: e.dma_start(
                out=land[sl], in_=AP(dwout, off, [[D, 128], [1, LQ]])),
                waits=[lc.get(i - 2), lgu[sl] if i < 2 else None],
                inc="s_g2", amt=16)
            evc = S.op("gpsimd", lambda e, cs=cs, sl=sl: e.tensor_copy(
                t1_sb[:, cs * LQ:(cs + 1) * LQ], land[sl]),
                waits=[evl, ls.get(i - 4),
                       e_t1_free if i < 4 else None], inc="s_g")
            lc[i] = evc
            evs = S.op("gpsimd", lambda e, cs=cs, off=off: e.dma_start(
                out=AP(dwout_bf, off, [[D, 128], [1, LQ]]),
                in_=t1_sb[:, cs * LQ:(cs + 1) * LQ]),
                waits=[evc], inc="s_gd", amt=16)
            ls[i] = evs
            return evs

        for j in range(16):
            for c in range(2):
                wo_ev[(j, c)] = st2(j * 128 * D + c * LQ)
        wo_store[15] = wo_ev[(15, 1)]
        for j in range(8):
            evl = S.op("gpsimd", lambda e, j=j: e.dma_start(
                out=i8t_sb[:, (j % 2) * 1024:(j % 2 + 1) * 1024],
                in_=AP(dfw8, j * 128 * D, [[D, 128], [1, D]])),
                waits=[fw_cast.get(j - 2)], inc="s_g2", amt=16)
            fw_cast[j] = S.op("gpsimd", lambda e, j=j: e.tensor_copy(
                t1_sb[:, 2048 + (j % 2) * 1024:2048 + (j % 2 + 1) * 1024],
                i8t_sb[:, (j % 2) * 1024:(j % 2 + 1) * 1024]),
                waits=[evl, fw_store.get(j - 2),
                       e_t1_free if j < 2 else None],
                inc="s_g")
            fw_store[j] = S.op("gpsimd", lambda e, j=j: e.dma_start(
                out=AP(dfw_bf, j * 128 * D, [[D, 128], [1, D]]),
                in_=t1_sb[:, 2048 + (j % 2) * 1024:
                          2048 + (j % 2 + 1) * 1024]),
                waits=[fw_cast[j]], inc="s_gd", amt=16)
        late_done[0] = fw_store[7]

    startup_x_free = []
    ev_fstage_done = h1_cast[h1_idx[0] - 1]
    ev_stores_done = h1_store[h1_idx[0] - 1]
    rep_startup_guard = [ev_stores_done]
    startup_done = [e_sel, e_idf, e_a2, e_cols, e_fcols, e_ms1]

    last_win_rd = {}
    last_wot_rd = {}
    last_cv_rd = {}
    prev_conv_done = None
    prev_scan_slot = {}
    prev_rep_reader = None
    prev_da_last = None
    ev_xres_store = {}
    prev_xguard = startup_x_free + [e_ms2, e_ms3] + startup_done
    prev_rowdma = None
    prev_inproj_last = None
    qrow_tmp = [None]

    for q in range(NQ):
        qrow_tmp[0] = qrow_sb[0:1, 0:LQ]
        S.stage = 1
        if q == 0:
            e_x = e_x0
        else:
            e_x = S.op("gpsimd", lambda e, q=q: e.dma_start(
                out=AP(x_sb, 0, [[KD * LQ, 128], [LQ, KD], [1, LQ]]),
                in_=AP(dx, q * LQ, [[L, 128], [128 * L, KD], [1, LQ]])),
                waits=prev_xguard, inc="s_gd", amt=16)
        e_cast = None
        for k in range(KD):
            e_cast = S.op("vector", lambda e, k=k: e.tensor_copy(
                dtg(k), xk(k)), waits=[e_x, prev_da_last], inc="s_v")
        e_m = None
        for k in range(KD):
            e_m = S.op("tensor", lambda e, k=k: e.matmul(
                ps[0][:], onesd_sb[:], dtg(k),
                start=(k == 0), stop=(k == KD - 1)),
                waits=[e_cast, bank_free[0], e_ms1], inc="s_t")
        e_t1 = None
        for k in range(KD):
            e_t1 = S.op("vector", lambda e, k=k: e.tensor_tensor(
                out=t1k(k), in0=xk(k), in1=ps[0][:], op=ALU.subtract),
                waits=[e_m, late_done[0] if q == 1 else None], inc="s_v")
        e_sq = None
        for k in range(KD):
            e_sq = S.op("scalar", lambda e, k=k: e.activation(
                out=dtg(k), in_=t1k(k), func=AF.Square),
                waits=[e_t1, e_m], inc="s_s")
        e_v = None
        for k in range(KD):
            e_v = S.op("tensor", lambda e, k=k: e.matmul(
                ps[1][:], onesd_sb[:], dtg(k),
                start=(k == 0), stop=(k == KD - 1)),
                waits=[e_sq, bank_free[1]], inc="s_t")
        e_ln1 = S.op("scalar", lambda e: e.activation(
            out=lnt_sb[:], in_=ps[1][:], func=AF.Ln, bias=cc_sb[:, 0:1]),
            waits=[e_v, e_cc0, e_cc2], inc="s_s")
        e_inv1 = S.op("scalar", lambda e: e.activation(
            out=inv_sb[:], in_=lnt_sb[:], func=AF.Exp, scale=cc_sb[:, 2:3]),
            inc="s_s")
        bank_free[1] = e_ln1
        e_xln = None
        for k in range(KD):
            e_xln = S.op("vector", lambda e, k=k: e.tensor_tensor(
                out=t1k(k), in0=t1k(k), in1=inv_sb[:], op=ALU.mult),
                waits=[e_inv1], inc="s_v")
        bank_free[0] = e_t1

        S.stage = 2
        xc_rdy = {}
        cv_last_ld = [None]
        cv_mm_ev = {}
        g_done = {}
        xconv_done = {}
        ip_free = {2: bank_free[2], 3: bank_free[3]}
        cv_free = {4: bank_free[4], 5: bank_free[5]}
        e_conv_mm_last = None

        def emit_conv(g, q=q):
            nonlocal e_conv_mm_last
            bk = 4 + g % 2
            if g % 2 == 0:
                cv_last_ld[0] = S.op("sync", lambda e, g=g: e.dma_start(
                    out=cvrg_sb[:, ((g // 2) % 2) * 1024:
                                ((g // 2) % 2 + 1) * 1024],
                    in_=AP(dcvd_bf, g * 4 * 128,
                           [[GI * 4 * 128, 128], [1, 1024]])),
                    waits=[last_cv_rd.get((g // 2) % 2),
                           e_cvd_st[1] if q == 0 and g < 2 else None],
                    inc="s_in", amt=16)
            e_cl = cv_last_ld[0]
            e_cm = None
            for kk in range(4):
                e_cm = S.op("tensor", lambda e, g=g, kk=kk, bk=bk: e.matmul(
                    ps[bk][:], cvslot(g)[:, kk * 128:(kk + 1) * 128],
                    xcs(g, kk, kk + LQ),
                    start=(kk == 0), stop=(kk == 3)),
                    waits=[xc_rdy[g], e_cl, cv_free.get(bk)], inc="s_t")
            last_cv_rd[(g // 2) % 2] = e_cm
            cv_mm_ev[g] = e_cm
            e_cs = S.op("scalar", lambda e, g=g, bk=bk: e.activation(
                out=xcvg(g), in_=ps[bk][:], func=AF.Silu,
                bias=convb_c(g)), waits=[e_cm, e_cols], inc="s_s")
            cv_free[bk] = e_cs
            xconv_done[g] = e_cs
            e_conv_mm_last = e_cm

        last_ld = [None]
        for eb in range(NEB):
            if eb >= 4 and (eb - 4) < GI:
                emit_conv(eb - 4)
            if eb % 2 == 0:
                last_ld[0] = S.op("sync", lambda e, eb=eb: e.dma_start(
                    out=AP(winrg_sb, ((eb // 2) % 3) * 2048,
                           [[3 * 2048, 128], [256, KD], [1, 256]]),
                    in_=AP(dwin_bf, eb * 128,
                           [[E, 128], [128 * E, KD], [1, 256]])),
                    waits=[last_win_rd.get((eb // 2) % 3),
                           win_piece_ev[(KD - 1, (eb + 1) // 4)]
                           if q == 0 else None],
                    inc="s_in", amt=16)
            e_ld = last_ld[0]
            bk = 2 + eb % 2
            e_mm = None
            for k in range(KD):
                e_mm = S.op("tensor", lambda e, eb=eb, k=k, bk=bk: e.matmul(
                    ps[bk][:], wintile(eb, k), t1k(k),
                    start=(k == 0), stop=(k == KD - 1)),
                    waits=[e_ld, e_xln, ip_free.get(bk)], inc="s_t")
            last_win_rd[(eb // 2) % 3] = e_mm
            if eb < 16:
                g = eb
                slot_guard = cv_mm_ev.get(g - 4)
                if q > 0 and g < 4:
                    slot_guard = prev_conv_done
                e_e = S.op("vector", lambda e, g=g, bk=bk: e.tensor_copy(
                    xcs(g, 3, LQ + 3), ps[bk][:]),
                    waits=[e_mm, slot_guard], inc="s_v")
                S.op("vector", lambda e, g=g: e.tensor_copy(
                    xcs(g, 0, 3), halo_sb[:, g * 3:g * 3 + 3]),
                    waits=[e_ms2], inc="s_v")
                if q < NQ - 1:
                    e_sv = S.op("vector", lambda e, g=g: e.tensor_copy(
                        halo_sb[:, g * 3:g * 3 + 3], xcs(g, LQ, LQ + 3)),
                        inc="s_v")
                else:
                    e_sv = S.op("vector", lambda e: e.engine_nop(),
                                inc="s_v")
                xc_rdy[g] = e_sv
                ip_free[bk] = e_e
            else:
                e_e = S.op("scalar", lambda e, eb=eb, bk=bk: e.activation(
                    out=gg(eb - 16), in_=ps[bk][:], func=AF.Silu),
                    waits=[e_mm], inc="s_s")
                g_done[eb - 16] = e_e
                ip_free[bk] = e_e
        for g in range(GI - 4, GI):
            emit_conv(g)
        bank_free[2] = ip_free[2]
        bank_free[3] = ip_free[3]
        bank_free[4] = cv_free[4]
        bank_free[5] = cv_free[5]
        prev_conv_done = e_conv_mm_last
        prev_inproj_last = last_win_rd[((NEB - 1) // 2) % 3]

        e_wxm = None
        for g in range(GI):
            e_wxm = S.op("tensor", lambda e, g=g: e.matmul(
                ps[6][0:96, :], wx_sb[:, g * 96:(g + 1) * 96], xcvg(g),
                start=(g == 0), stop=(g == GI - 1)),
                waits=[xconv_done[g], e_wxc, bank_free[6]], inc="s_t")
        e_xdbl = S.op("vector", lambda e: e.tensor_copy(
            xdbl_sb[0:96, :], ps[6][0:96, :]),
            waits=[e_wxm, e_ms3], inc="s_v")
        bank_free[6] = e_xdbl

        dt_done = {}
        dt_free = {7: bank_free[7], 0: bank_free[0]}
        for g in range(GI):
            bk = 7 if g % 2 == 0 else 0
            e_dm = S.op("tensor", lambda e, g=g, bk=bk: e.matmul(
                ps[bk][:], wdt_sb[:, g * 128:(g + 1) * 128],
                xdbl_sb[:], start=True, stop=True),
                waits=[e_xdbl, e_wdtc, dt_free.get(bk),
                       e_inv1 if bk == 0 else None], inc="s_t")
            e_de = S.op("scalar", lambda e, g=g, bk=bk: e.activation(
                out=et_s(g), in_=ps[bk][:], func=AF.Exp, bias=bdt_c(g)),
                waits=[e_dm, ev_fstage_done if q == 0 and g < 2 else None],
                inc="s_s")
            e_dl = S.op("scalar", lambda e, g=g: e.activation(
                out=dtg(g), in_=et_s(g), func=AF.Ln, bias=cc_sb[:, 1:2]),
                waits=[e_cc1], inc="s_s")
            dt_free[bk] = e_de
            dt_done[g] = e_dl
        bank_free[7] = dt_free[7]
        bank_free[0] = dt_free[0]

        xcd_done = {}
        u_done = {}
        for g in range(GI):
            xcd_done[g] = S.op("vector", lambda e, g=g: e.tensor_scalar(
                out=yg(g), in0=xcvg(g), scalar1=dp_c(g), scalar2=None,
                op0=ALU.mult), waits=[xconv_done[g], e_cols], inc="s_v")
            u_done[g] = S.op("vector", lambda e, g=g: e.tensor_tensor(
                out=xcvg(g), in0=dtg(g), in1=xcvg(g), op=ALU.mult),
                waits=[dt_done[g]], inc="s_v")

        repB_ev = {}
        repC_ev = {}
        rb_free = bank_free[6]
        rc_free = bank_free[7]
        for n in range(NST):
            e_rb = S.op("tensor", lambda e, n=n: e.matmul(
                ps[6][:], selB(n), xdbl_sb[:], start=True, stop=True),
                waits=[e_xdbl, e_sel, rb_free], inc="s_t")
            e_rbe = S.op("scalar", lambda e, n=n: e.copy(
                out=repB(n), in_=ps[6][:]),
                waits=[e_rb, prev_rep_reader] +
                      (rep_startup_guard if q == 0 else []), inc="s_s")
            e_rc = S.op("tensor", lambda e, n=n: e.matmul(
                ps[7][:], selC(n), xdbl_sb[:], start=True, stop=True),
                waits=[rc_free], inc="s_t")
            e_rce = S.op("scalar", lambda e, n=n: e.copy(
                out=repC(n), in_=ps[7][:]),
                waits=[e_rc, prev_rep_reader] +
                      (rep_startup_guard if q == 0 else []), inc="s_s")
            rb_free, rc_free = e_rbe, e_rce
            repB_ev[n] = e_rbe
            repC_ev[n] = e_rce
        bank_free[6] = rb_free
        bank_free[7] = rc_free

        S.stage = 4
        gate_ev = {}
        scan_ev = {}
        zmm_ev = {}
        da_ev = {}
        y_free = {2: bank_free[2], 3: bank_free[3]}

        def emit_da(j, q=q):
            if j >= GI * NST:
                return
            gj, nj = j // NST, j % NST
            da_ev[j] = S.op("scalar", lambda e, gj=gj, nj=nj, j=j:
                            e.activation(out=da_s(j), in_=dtg(gj),
                                         func=AF.Exp, scale=acol(gj, nj)),
                            waits=[dt_done[gj], e_a2,
                                   scan_ev.get(j - 2,
                                               prev_scan_slot.get(j % 2))],
                            inc="s_s")

        emit_da(0)
        emit_da(1)
        ii = 0
        for g in range(GI):
            bk = 2 + g % 2
            S.op("tensor", lambda e, g=g, bk=bk: e.matmul(
                ps[bk][:], idb(), yg(g), start=True, stop=False),
                waits=[xcd_done[g], y_free.get(bk)], inc="s_t")
            for n in range(NST):
                i = ii
                ii += 1
                e_da = da_ev[i]
                S.stage = 64
                S.op("vector", lambda e, g=g, n=n, i=i: e.tensor_tensor(
                    out=dbx_s(i), in0=xcvg(g), in1=repB(n), op=ALU.mult),
                    waits=[u_done[g], repB_ev[n]], inc="s_v")
                S.stage = 128
                if q == 0:
                    e_h = S.op("vector", lambda e, i=i:
                               e.tensor_tensor_scan(
                                   out=h_s(i), data0=da_s(i), data1=dbx_s(i),
                                   initial=0.0, op0=ALU.mult, op1=ALU.add),
                               waits=[e_da], inc="s_v")
                else:
                    cr = ((q - 1) % 2) * GI * NST + g * NST + n
                    e_h = S.op("vector", lambda e, cr=cr, i=i:
                               e.tensor_tensor_scan(
                                   out=h_s(i), data0=da_s(i), data1=dbx_s(i),
                                   initial=carry_sb[:, cr:cr + 1],
                                   op0=ALU.mult, op1=ALU.add),
                               waits=[e_da], inc="s_v")
                scan_ev[i] = e_h
                emit_da(i + 2)
                S.stage = 256
                if q < NQ - 1:
                    cw = (q % 2) * GI * NST + g * NST + n
                    S.op("scalar", lambda e, cw=cw, i=i: e.copy(
                        out=carry_sb[:, cw:cw + 1],
                        in_=h_s(i)[:, LQ - 1:LQ]),
                        waits=[scan_ev[i]], inc="s_s")
                S.stage = 512
                e_z = S.op("vector", lambda e, n=n, i=i: e.tensor_tensor(
                    out=z_s(i), in0=h_s(i), in1=repC(n), op=ALU.mult),
                    waits=[repC_ev[n], zmm_ev.get(i - 2)], inc="s_v")
                zmm_ev[i] = S.op("tensor",
                                 lambda e, i=i, n=n, bk=bk: e.matmul(
                                     ps[bk][:], idb(), z_s(i),
                                     start=False, stop=(n == NST - 1)),
                                 waits=[e_z], inc="s_t")
            S.stage = 1024
            e_gate = S.op("vector", lambda e, g=g, bk=bk: e.tensor_tensor(
                out=yg(g), in0=ps[bk][:], in1=gg(g), op=ALU.mult),
                waits=[zmm_ev[ii - 1], g_done[g]], inc="s_v")
            S.stage = 4
            y_free[bk] = e_gate
            gate_ev[g] = e_gate
        bank_free[2] = y_free[2]
        bank_free[3] = y_free[3]
        prev_scan_slot = {0: scan_ev[ii - 2], 1: scan_ev[ii - 1]}
        prev_rep_reader = gate_ev[GI - 1]
        if q == 0:
            emit_late_staging(e_inv1, e_xln, prev_inproj_last)
        prev_da_last = da_ev[GI * NST - 1]

        S.stage = 8
        op_evac = {}
        op_free = {4: bank_free[4], 5: bank_free[5]}
        for dm in range(KD):
            e_wol = S.op("gpsimd", lambda e, dm=dm: e.dma_start(
                out=AP(wotrg_sb, (dm % 2) * 2048,
                       [[2 * 2048, 128], [128, GI], [1, 128]]),
                in_=AP(dwout_bf, dm * 128,
                       [[D, 128], [128 * D, GI], [1, 128]])),
                waits=[last_wot_rd.get(dm % 2)] +
                      ([wo_store[15]] if q == 0 and dm < 2 else []),
                inc="s_gd", amt=16)
            bk = 4 + dm % 2
            e_om = None
            for g in range(GI):
                e_om = S.op("tensor", lambda e, dm=dm, g=g, bk=bk: e.matmul(
                    ps[bk][:], wotslot(dm)[:, g * 128:(g + 1) * 128], yg(g),
                    start=(g == 0), stop=(g == GI - 1)),
                    waits=[e_wol, gate_ev[g], op_free.get(bk)], inc="s_t")
            last_wot_rd[dm % 2] = e_om
            e_oe = S.op("vector", lambda e, dm=dm, bk=bk: e.tensor_tensor(
                out=xk(dm), in0=ps[bk][:], in1=xk(dm), op=ALU.add),
                waits=[e_om], inc="s_v")
            op_evac[dm] = e_oe
            op_free[bk] = e_oe
        bank_free[4] = op_free[4]
        bank_free[5] = op_free[5]

        e_cast2 = None
        for k in range(KD):
            e_cast2 = S.op("vector", lambda e, k=k: e.tensor_copy(
                dtg(k), xk(k)), waits=[op_evac[k], prev_da_last], inc="s_v")
        e_m2 = None
        for k in range(KD):
            e_m2 = S.op("tensor", lambda e, k=k: e.matmul(
                ps[0][:], onesd_sb[:], dtg(k),
                start=(k == 0), stop=(k == KD - 1)),
                waits=[e_cast2, bank_free[0]], inc="s_t")
        e_sq2 = None
        for k in range(KD):
            e_sq2 = S.op("scalar", lambda e, k=k: e.activation(
                out=xcvg(k), in_=dtg(k), func=AF.Square),
                waits=[e_cast2, prev_rep_reader], inc="s_s")
        e_v2 = None
        for k in range(KD):
            e_v2 = S.op("tensor", lambda e, k=k: e.matmul(
                ps[1][:], onesd_sb[:], xcvg(k),
                start=(k == 0), stop=(k == KD - 1)),
                waits=[e_sq2, bank_free[1]], inc="s_t")
        e_c12 = S.op("scalar", lambda e: e.activation(
            out=lnt_sb[:], in_=ps[0][:], func=AF.Square),
            waits=[e_m2, late_done[0] if q == 0 else None], inc="s_s")
        e_c22 = S.op("vector", lambda e: e.tensor_tensor(
            out=lnt_sb[:], in0=ps[1][:], in1=lnt_sb[:], op=ALU.subtract),
            waits=[e_v2, e_c12], inc="s_v")
        e_ln2 = S.op("scalar", lambda e: e.activation(
            out=ps[1][:], in_=lnt_sb[:], func=AF.Ln, bias=cc_sb[:, 0:1]),
            waits=[e_c22], inc="s_s")
        e_inv2 = S.op("scalar", lambda e: e.activation(
            out=inv_sb[:], in_=ps[1][:], func=AF.Exp, scale=cc_sb[:, 2:3]),
            waits=[late_done[0] if q == 0 else None], inc="s_s")
        bank_free[1] = e_inv2
        e_r1 = S.op("vector", lambda e, q=q: e.tensor_copy(
            m2pad_sb[0:1, q * LQ:(q + 1) * LQ], ps[0][0:1, :]),
            waits=[e_m2, e_ms6, prev_rowdma], inc="s_v")
        e_r2 = S.op("vector", lambda e, q=q: e.tensor_tensor(
            out=padtmp_sb[0:1, 0:LQ], in0=ps[0][0:1, :],
            in1=m2pad_sb[0:1, q * LQ:(q + 1) * LQ], op=ALU.subtract),
            inc="s_v")
        e_r3 = S.op("vector", lambda e, q=q: e.tensor_copy(
            padtmp_sb[0:1, LQ:2 * LQ], inv_sb[0:1, :]),
            waits=[e_inv2], inc="s_v")
        e_r4 = S.op("vector", lambda e, q=q: e.tensor_tensor(
            out=padtmp_sb[0:1, 0:LQ] if False else qrow_tmp[0],
            in0=inv_sb[0:1, :], in1=padtmp_sb[0:1, LQ:2 * LQ],
            op=ALU.subtract), inc="s_v")
        e_d1 = S.op("gpsimd", lambda e, q=q: e.dma_start(
            out=AP(m2pad_sb, 32 * L + q * LQ, [[L, 1], [1, LQ]]),
            in_=padtmp_sb[0:1, 0:LQ]), waits=[e_r2], inc="s_gd", amt=16)
        e_d2 = S.op("gpsimd", lambda e, q=q: e.dma_start(
            out=AP(m2pad_sb, 64 * L + q * LQ, [[L, 1], [1, LQ]]),
            in_=padtmp_sb[0:1, LQ:2 * LQ]), waits=[e_r3], inc="s_gd", amt=16)
        e_d3 = S.op("gpsimd", lambda e, q=q: e.dma_start(
            out=AP(m2pad_sb, 96 * L + q * LQ, [[L, 1], [1, LQ]]),
            in_=qrow_tmp[0]), waits=[e_r4], inc="s_gd", amt=16)
        prev_rowdma = e_d3
        e_qp = None
        for k in range(KD):
            S.op("vector", lambda e, k=k: e.tensor_tensor(
                out=et_s(k), in0=xk(k), in1=ps[0][:], op=ALU.subtract),
                waits=[e_inv2], inc="s_v")
            S.op("vector", lambda e, k=k: e.tensor_tensor(
                out=et_s(k), in0=et_s(k), in1=inv_sb[:], op=ALU.mult),
                inc="s_v")
            S.op("vector", lambda e, k=k: e.tensor_reduce(
                out=qcol_sb[:, 1:2], in_=et_s(k), axis=mybir.AxisListType.X,
                op=ALU.max, apply_absolute_value=True),
                waits=[e_ms4], inc="s_v")
            e_qp = S.op("vector", lambda e: e.tensor_tensor(
                out=qcol_sb[:, 0:1], in0=qcol_sb[:, 0:1],
                in1=qcol_sb[:, 1:2], op=ALU.max), inc="s_v")
        bank_free[0] = e_qp
        ev_xres_store[q] = S.op("gpsimd", lambda e, q=q: e.dma_start(
            out=AP(dxres, q * LQ, [[L, 128], [128 * L, KD], [1, LQ]]),
            in_=AP(x_sb, 0, [[KD * LQ, 128], [LQ, KD], [1, LQ]])),
            waits=[op_evac[KD - 1]], inc="s_gd", amt=16)
        prev_xguard = [ev_xres_store[q], e_qp, e_r1, e_d3]
        prev_da_last = e_sq2

    S.stage = 2048
    e_tr = S.op("tensor", lambda e: e.transpose(
        ps[6][0:1, 0:128], qcol_sb[:, 0:1], idf_sb[:]),
        waits=[e_qp, e_idf, bank_free[6]], inc="s_t")
    e_gm = S.op("vector", lambda e: e.tensor_reduce(
        out=smal_sb[0:1, 0:1], in_=ps[6][0:1, 0:128],
        axis=mybir.AxisListType.X, op=ALU.max), waits=[e_tr], inc="s_v")
    bank_free[6] = e_gm
    e_st = S.op("gpsimd", lambda e: e.dma_start(
        out=AP(dccin, 0, [[1, 1], [1, 1]]), in_=smal_sb[0:1, 0:1]),
        waits=[e_gm], inc="s_gd", amt=16)
    e_cc = S.op("gpsimd", lambda e: e.collective_compute(
        "AllReduce", ALU.max, replica_groups=[list(range(8))],
        ins=[dccin.ap().opt()], outs=[dccout.ap().opt()]),
        waits=[e_st], inc="s_cc")
    e_ccl = S.op("gpsimd", lambda e: e.dma_start(
        out=smal_sb[0:1, 2:3], in_=dccout.ap()), waits=[e_cc],
        inc="s_gd", amt=16)
    e_xs = S.op("vector", lambda e: e.tensor_scalar(
        out=smal_sb[0:1, 1:2], in0=smal_sb[0:1, 2:3], scalar1=1.0 / 127.0,
        scalar2=EPS_FFN, op0=ALU.mult, op1=ALU.max),
        waits=[e_ccl], inc="s_v")
    e_is = S.op("vector", lambda e: e.reciprocal(
        out=smal_sb[0:1, 3:4], in_=smal_sb[0:1, 1:2]), inc="s_v")
    e_gp1 = S.op("vector", lambda e: e.tensor_copy(
        gpad_sb[0:1, 0:1], smal_sb[0:1, 1:2]),
        waits=[e_ms5, e_xs], inc="s_v")
    e_gp2 = S.op("vector", lambda e: e.tensor_tensor(
        out=gpad_sb[0:1, 1:2], in0=smal_sb[0:1, 1:2], in1=gpad_sb[0:1, 0:1],
        op=ALU.subtract), inc="s_v")
    e_gp3 = S.op("vector", lambda e: e.tensor_copy(
        gpad_sb[0:1, 2:3], smal_sb[0:1, 3:4]), waits=[e_is], inc="s_v")
    e_gp4 = S.op("vector", lambda e: e.tensor_tensor(
        out=gpad_sb[0:1, 3:4], in0=smal_sb[0:1, 3:4], in1=gpad_sb[0:1, 2:3],
        op=ALU.subtract), inc="s_v")
    e_xsc = None
    for j in range(2):
        e_xsc = S.op("tensor", lambda e, j=j: e.matmul(
            ps[7][:, 0:1], selR(0), gpad_sb[:, j:j + 1],
            start=(j == 0), stop=(j == 1)),
            waits=[e_gp2, e_sel, bank_free[7]], inc="s_t")
    e_wsx = S.op("vector", lambda e: e.tensor_scalar(
        out=wsx_sb[:], in0=fc_sb[:, 0:8], scalar1=ps[7][:, 0:1],
        scalar2=None, op0=ALU.mult), waits=[e_xsc, e_fcols], inc="s_v")
    bank_free[7] = e_wsx
    e_isc = None
    for j in range(2):
        e_isc = S.op("tensor", lambda e, j=j: e.matmul(
            ps[6][:, 0:1], selR(0), gpad_sb[:, 2 + j:3 + j],
            start=(j == 0), stop=(j == 1)),
            waits=[e_gp4, bank_free[6]], inc="s_t")
    e_iscol = S.op("vector", lambda e: e.tensor_copy(
        iscol_sb[:], ps[6][:, 0:1]), waits=[e_isc], inc="s_v")
    bank_free[6] = e_iscol
    e_fw = S.op("sync", lambda e: e.dma_start(
        out=rep_sb[:, 0:8192],
        in_=AP(dfw_bf, 0, [[D, 128], [128 * D, 8], [1, D]])),
        waits=[prev_rep_reader, fw_store[7]], inc="s_in", amt=16)
    fwt = lambda k, nb: rep_sb[:, k * 1024 + nb * 128:
                               k * 1024 + (nb + 1) * 128]

    S.stage = 16
    prev_ffn_done = None
    ps0_free = bank_free[0]
    ps1_free = bank_free[1]
    ffn_free = {2: bank_free[2], 3: bank_free[3]}
    for q in range(NQ):
        e_xl = S.op("gpsimd", lambda e, q=q: e.dma_start(
            out=AP(x_sb, 0, [[KD * LQ, 128], [LQ, KD], [1, LQ]]),
            in_=AP(dxres, q * LQ, [[L, 128], [128 * L, KD], [1, LQ]])),
            waits=[e_iscol, e_wsx, prev_ffn_done, e_qp if q == 0 else None],
            inc="s_gd", amt=16)
        e_m2r = None
        for j in range(2):
            e_m2r = S.op("tensor", lambda e, q=q, j=j: e.matmul(
                ps[0][:], selR(j), m2pad_sb[:, q * LQ:(q + 1) * LQ],
                start=(j == 0), stop=(j == 1)),
                waits=[prev_rowdma, ps0_free], inc="s_t")
        e_sir = None
        for j in range(2):
            e_sir = S.op("tensor", lambda e, q=q, j=j: e.matmul(
                ps[1][:], selR(2 + j), m2pad_sb[:, q * LQ:(q + 1) * LQ],
                start=(j == 0), stop=(j == 1)),
                waits=[prev_rowdma, ps1_free], inc="s_t")
        e_sirs = S.op("vector", lambda e: e.tensor_scalar(
            out=inv_sb[:], in0=ps[1][:], scalar1=iscol_sb[:, 0:1],
            scalar2=None, op0=ALU.mult), waits=[e_sir, e_iscol], inc="s_v")
        e_xq = {}
        for k in range(KD):
            S.op("vector", lambda e, k=k: e.tensor_tensor(
                out=et_s(k), in0=xk(k), in1=ps[0][:], op=ALU.subtract),
                waits=[e_xl, e_m2r, prev_ffn_done], inc="s_v")
            S.op("vector", lambda e, k=k: e.tensor_tensor(
                out=et_s(k), in0=et_s(k), in1=inv_sb[:], op=ALU.mult),
                waits=[e_sirs], inc="s_v")
            e_xq[k] = S.op("vector", lambda e, k=k: e.tensor_scalar(
                out=dtg(k), in0=et_s(k), scalar1=MAGIC, scalar2=MAGIC,
                op0=ALU.add, op1=ALU.subtract), inc="s_v")
        ps0_free = e_xq[KD - 1]
        ps1_free = e_sirs
        e_fin = {}
        for nb in range(KD):
            bk = 2 + nb % 2
            e_fm = None
            for k in range(KD):
                e_fm = S.op("tensor", lambda e, nb=nb, k=k, bk=bk: e.matmul(
                    ps[bk][:], fwt(k, nb), dtg(k),
                    start=(k == 0), stop=(k == KD - 1)),
                    waits=[e_fw, e_xq[k], ffn_free.get(bk)], inc="s_t")
            e_fe = S.op("scalar", lambda e, nb=nb, bk=bk: e.activation(
                out=et_s(nb), in_=ps[bk][:], func=AF.Copy,
                scale=wsx_sb[:, nb:nb + 1]),
                waits=[e_fm, e_wsx, e_xq[KD - 1], e_fin.get(nb - 2)],
                inc="s_s")
            e_fa = S.op("vector", lambda e, nb=nb: e.scalar_tensor_tensor(
                out=xk(nb), in0=et_s(nb), scalar=fc_sb[:, 8 + nb:9 + nb],
                in1=xk(nb), op0=ALU.add, op1=ALU.add),
                waits=[e_fe, e_fcols], inc="s_v")
            ffn_free[bk] = e_fa
            e_fin[nb] = e_fa
        prev_ffn_done = e_fin[KD - 1]
        S.op("gpsimd", lambda e, q=q: e.dma_start(
            out=AP(dout, q * LQ, [[L, 128], [128 * L, KD], [1, LQ]]),
            in_=AP(x_sb, 0, [[KD * LQ, 128], [LQ, KD], [1, LQ]])),
            waits=[e_fin[KD - 1]], inc="s_gd", amt=16)

    S.op("vector", lambda e: e.tensor_copy(
        etmp_sb[:, 0:2 * GI * NST], carry_sb[:]), inc="s_v")
    S.op("gpsimd", lambda e: e.dma_start(
        out=ddbg.ap(), in_=etmp_sb[:, 0:2 * GI * NST]),
        waits=[("s_v", S.cnt["s_v"])], inc="s_gd", amt=16)
    final_gd = S.cnt["s_gd"]
    S.op("gpsimd", lambda e: e.nop(), waits=[("s_gd", final_gd)], inc=None)

    with es:
        block = es.enter_context(nc.Block())

        def replay(eng_name):
            def _(eng):
                hw = {}
                for (en, fn, waits, inc, amt) in S.ops:
                    if en != eng_name:
                        continue
                    for (sn, val) in waits:
                        if hw.get(sn, 0) < val:
                            eng.wait_ge(sems[sn], val)
                            hw[sn] = val
                    instr = fn(eng)
                    if inc is not None:
                        instr.then_inc(sems[inc], amt)
                    if KDBG and eng_name != "sync":
                        try:
                            eng.drain()
                        except Exception:
                            pass
            return _

        block.sync(replay("sync"))
        block.vector(replay("vector"))
        block.scalar(replay("scalar"))
        block.tensor(replay("tensor"))
        block.gpsimd(replay("gpsimd"))
    return nc


def kernel(L0_attn_x, L0_attn_W_in, L0_attn_conv_w, L0_attn_conv_b,
           L0_attn_W_x, L0_attn_W_dt, L0_attn_b_dt, L0_attn_A_log,
           L0_attn_Dp, L0_attn_W_out, L0_ffn_weight_int8,
           L0_ffn_weight_scale, L0_ffn_bias):
    import ml_dtypes
    x = np.asarray(L0_attn_x, dtype=np.float32)
    W_in = np.asarray(L0_attn_W_in, dtype=np.float32)
    conv_w = np.asarray(L0_attn_conv_w, dtype=np.float32)
    conv_b = np.asarray(L0_attn_conv_b, dtype=np.float32)
    W_x = np.asarray(L0_attn_W_x, dtype=np.float32)
    W_dt = np.asarray(L0_attn_W_dt, dtype=np.float32)
    b_dt = np.asarray(L0_attn_b_dt, dtype=np.float32)
    A_log = np.asarray(L0_attn_A_log, dtype=np.float32)
    Dp = np.asarray(L0_attn_Dp, dtype=np.float32)
    W_out = np.asarray(L0_attn_W_out, dtype=np.float32)
    fw8 = np.asarray(L0_ffn_weight_int8)
    fws = np.asarray(L0_ffn_weight_scale, dtype=np.float32)
    fb = np.asarray(L0_ffn_bias, dtype=np.float32)

    x_fm = np.ascontiguousarray(np.swapaxes(x, 1, 2))
    winT = np.ascontiguousarray(W_in.T)
    woutT = np.ascontiguousarray(W_out.T)
    wxT_t = np.ascontiguousarray(
        W_x.T.reshape(GI, 128, 96).transpose(1, 0, 2).reshape(128, GI * 96))
    wdtT = np.zeros((128, DI), np.float32)
    wdtT[0:RNK, :] = W_dt.T
    fwT8 = np.ascontiguousarray(fw8.T)
    convd = np.zeros((128, GI * 4 * 128), np.float32)
    for g in range(GI):
        for kk in range(4):
            blk = convd[:, (g * 4 + kk) * 128:(g * 4 + kk + 1) * 128]
            np.fill_diagonal(blk, conv_w[g * 128:(g + 1) * 128, kk])
    sel = np.zeros((128, 37 * 128), ml_dtypes.bfloat16)
    for n in range(NST):
        sel[64 + n, n * 128:(n + 1) * 128] = 1
        sel[80 + n, (16 + n) * 128:(17 + n) * 128] = 1
    sel[:, 32 * 128:33 * 128] = np.eye(128, dtype=ml_dtypes.bfloat16)
    for j, row in enumerate((0, 32, 64, 96)):
        sel[row, (33 + j) * 128:(34 + j) * 128] = 1
    identf = np.eye(128, dtype=np.float32)
    alog_fm = np.ascontiguousarray(
        A_log.reshape(GI, 128, NST).transpose(1, 0, 2).reshape(128, GI * NST))
    colify = lambda v: np.ascontiguousarray(v.reshape(GI, 128).T)
    cols = np.concatenate([colify(conv_b), colify(b_dt), colify(Dp)], axis=1)
    colify8 = lambda v: np.ascontiguousarray(v.reshape(8, 128).T)
    fcols = np.concatenate([colify8(fws[:, 0]), colify8(fb)], axis=1)

    nc = build_nc()
    in_maps = []
    for c in range(8):
        in_maps.append({
            "x_fm": np.ascontiguousarray(x_fm[c]),
            "winT": winT, "woutT": woutT, "wxT_t": wxT_t, "wdtT": wdtT,
            "convd": convd, "sel_bf": sel,
            "ident_f32": identf, "alog_fm": alog_fm, "cols": cols,
            "fcols": fcols, "fwT8": fwT8,
        })
    res = run_bass_kernel_spmd(nc, in_maps, core_ids=list(range(8)))
    out_fm = np.stack([np.asarray(r["out"]) for r in res.results])
    return np.ascontiguousarray(np.swapaxes(out_fm, 1, 2))


if __name__ == "__main__":
    pass


# version 12
# speedup vs baseline: 1.3164x; 1.0006x over previous
import os
import numpy as np
from contextlib import ExitStack

KDBG = os.environ.get("KDBG", "0") == "1"
KSER = os.environ.get("KSER", "0") == "1"
KSERC = int(os.environ.get("KSERC", "0"))
KSERV = int(os.environ.get("KSERV", "2048"))

import concourse.bass as bass
import concourse.mybir as mybir
from concourse.bass_utils import run_bass_kernel_spmd

F32 = mybir.dt.float32
BF16 = mybir.dt.bfloat16
I8 = mybir.dt.int8
AF = mybir.ActivationFunctionType
ALU = mybir.AluOpType

MAGIC = 12582912.0
EPS_LN = 1e-5
EPS_FFN = 1e-5
ONE_OVER_D = 1.0 / 1024.0

L = 2048
D = 1024
DI = 2048
E = 4096
NST = 16
RNK = 64
NQ = 4
LQ = 512
KD = 8
GI = 16
NEB = 32


class Sched:

    def __init__(self):
        self.ops = []
        self.cnt = {}

    CLS = {"sync": 1, "tensor": 2, "vector": 4, "scalar": 8, "gpsimd": 16}
    stage = 32

    def op(self, eng, fn, waits=(), inc=None, amt=1):
        ser = KSER or (KSERC & self.CLS[eng]) or (
            eng == "vector" and (KSERV & self.stage))
        snap = dict(self.cnt) if ser else None
        ws = tuple(w for w in waits if w is not None)
        if inc == "s_in" and self.cnt.get("s_in"):
            ws = ws + (("s_in", self.cnt["s_in"]),)
        if inc == "s_gd" and self.cnt.get("s_gd"):
            ws = ws + (("s_gd", self.cnt["s_gd"]),)
        ev = None
        if inc is not None:
            self.cnt[inc] = self.cnt.get(inc, 0) + amt
            ev = (inc, self.cnt[inc])
        if snap:
            ws = ws + tuple(snap.items())
        self.ops.append((eng, fn, ws, inc, amt))
        return ev


def build_nc():
    nc = bass.Bass("TRN2", target_bir_lowering=False,
                   detect_race_conditions=KDBG)

    dx = nc.dram_tensor("x_fm", [D, L], F32, kind="ExternalInput")
    dwin = nc.dram_tensor("winT", [D, E], F32, kind="ExternalInput")
    dwout = nc.dram_tensor("woutT", [DI, D], F32, kind="ExternalInput")
    dwx = nc.dram_tensor("wxT_t", [128, GI * 96], F32, kind="ExternalInput")
    dwdt = nc.dram_tensor("wdtT", [128, DI], F32, kind="ExternalInput")
    dconvd = nc.dram_tensor("convd", [128, GI * 4 * 128], F32,
                            kind="ExternalInput")
    dsel = nc.dram_tensor("sel_bf", [128, 37 * 128], BF16,
                          kind="ExternalInput")
    didf = nc.dram_tensor("ident_f32", [128, 128], F32, kind="ExternalInput")
    dalog = nc.dram_tensor("alog_fm", [128, GI * NST], F32,
                           kind="ExternalInput")
    dcols = nc.dram_tensor("cols", [128, 48], F32, kind="ExternalInput")
    dfcols = nc.dram_tensor("fcols", [128, 16], F32, kind="ExternalInput")
    dfw8 = nc.dram_tensor("fwT8", [D, D], I8, kind="ExternalInput")
    dout = nc.dram_tensor("out", [D, L], F32, kind="ExternalOutput")
    ddbg = nc.dram_tensor("dbg", [128, 2 * GI * NST], F32,
                          kind="ExternalOutput")
    dwin_bf = nc.dram_tensor("win_bf", [D, E], BF16)
    dwout_bf = nc.dram_tensor("wout_bf", [DI, D], BF16)
    dfw_bf = nc.dram_tensor("fw_bf", [D, D], BF16)
    dcvd_bf = nc.dram_tensor("cvd_bf", [128, GI * 4 * 128], BF16)
    dxres = nc.dram_tensor("xres", [D, L], F32)
    dccin = nc.dram_tensor("cc_in", [1, 1], F32)
    dccout = nc.dram_tensor("cc_out", [1, 1], F32)

    es = ExitStack()

    def sbt(name, shape, dt=F32):
        return es.enter_context(nc.sbuf_tensor(name, shape, dt))

    x_sb = sbt("x_sb", [128, KD * LQ])
    t1_sb = sbt("t1_sb", [128, KD * LQ], BF16)
    xcr_sb = sbt("xcr_sb", [128, 4 * (LQ + 3)], BF16)
    halo_sb = sbt("halo_sb", [128, GI * 3], BF16)
    g_sb = sbt("g_sb", [128, GI * LQ], BF16)
    xcv_sb = sbt("xcv_sb", [128, GI * LQ], BF16)
    dt_sb = sbt("dt_sb", [128, GI * LQ], BF16)
    y_sb = sbt("y_sb", [128, GI * LQ], BF16)
    rep_sb = sbt("rep_sb", [128, 32 * LQ], BF16)
    xdbl_sb = sbt("xdbl_sb", [128, LQ], BF16)
    da_sb = sbt("da_sb", [128, 2 * LQ], BF16)
    dbx_sb = sbt("dbx_sb", [128, 2 * LQ], BF16)
    h_sb = sbt("h_sb", [128, 2 * LQ], BF16)
    z_sb = sbt("z_sb", [128, 2 * LQ], BF16)
    carry_sb = sbt("carry_sb", [128, 2 * GI * NST], BF16)
    inv_sb = sbt("inv_sb", [128, LQ])
    etmp_sb = sbt("etmp_sb", [128, 2 * LQ])
    lnt_sb = sbt("lnt_sb", [128, LQ])
    m2pad_sb = sbt("m2pad_sb", [128, L], BF16)
    padtmp_sb = sbt("padtmp_sb", [1, 2 * LQ], BF16)
    qrow_sb = sbt("qrow_sb", [1, LQ], BF16)
    qcol_sb = sbt("qcol_sb", [128, 2])
    smal_sb = sbt("smal_sb", [1, 16])
    gpad_sb = sbt("gpad_sb", [128, 4], BF16)
    iscol_sb = sbt("iscol_sb", [128, 1])
    wsx_sb = sbt("wsx_sb", [128, 8])
    fc_sb = sbt("fc_sb", [128, 16])
    a_sb = sbt("a_sb", [128, GI * NST])
    cols_sb = sbt("cols_sb", [128, 48])
    cc_sb = sbt("cc_sb", [128, 4])
    sel_sb = sbt("sel_sb", [128, 37 * 128], BF16)
    idf_sb = sbt("idf_sb", [128, 128])
    onesd_sb = sbt("onesd_sb", [128, 128], BF16)
    wx_sb = sbt("wx_sb", [128, GI * 96], BF16)
    wdt_sb = sbt("wdt_sb", [128, DI], BF16)
    cvrg_sb = sbt("cvrg_sb", [128, 2 * 1024], BF16)
    winrg_sb = sbt("winrg_sb", [128, 3 * 2048], BF16)
    wotrg_sb = sbt("wotrg_sb", [128, 2 * 2048], BF16)
    i8t_sb = sbt("i8t_sb", [128, 2 * 1024], I8)

    ps = [es.enter_context(nc.psum_tensor(f"pb{i}", [128, LQ], F32))
          for i in range(8)]

    sem_names = ["s_in", "s_gd", "s_t", "s_v", "s_s", "s_g", "s_cc"]
    sems = {n: es.enter_context(nc.semaphore(n)) for n in sem_names}

    S = Sched()
    AP = bass.AP

    bank_free = [None] * 8

    idb = lambda: sel_sb[:, 32 * 128:33 * 128]

    acol = lambda g, n: a_sb[:, g * NST + n:g * NST + n + 1]
    convb_c = lambda g: cols_sb[:, g:g + 1]
    bdt_c = lambda g: cols_sb[:, 16 + g:16 + g + 1]
    dp_c = lambda g: cols_sb[:, 32 + g:32 + g + 1]

    xk = lambda k: x_sb[:, k * LQ:(k + 1) * LQ]
    t1k = lambda k: t1_sb[:, k * LQ:(k + 1) * LQ]
    xcs = lambda g, a, b: xcr_sb[:, (g % 4) * (LQ + 3) + a:
                                 (g % 4) * (LQ + 3) + b]
    gg = lambda g: g_sb[:, g * LQ:(g + 1) * LQ]
    xcvg = lambda g: xcv_sb[:, g * LQ:(g + 1) * LQ]
    dtg = lambda g: dt_sb[:, g * LQ:(g + 1) * LQ]
    yg = lambda g: y_sb[:, g * LQ:(g + 1) * LQ]
    repB = lambda n: rep_sb[:, n * LQ:(n + 1) * LQ]
    repC = lambda n: rep_sb[:, (16 + n) * LQ:(17 + n) * LQ]
    selB = lambda n: sel_sb[:, n * 128:(n + 1) * 128]
    selC = lambda n: sel_sb[:, (16 + n) * 128:(17 + n) * 128]
    selR = lambda j: sel_sb[:, (33 + j) * 128:(34 + j) * 128]
    da_s = lambda i: da_sb[:, (i % 2) * LQ:(i % 2 + 1) * LQ]
    dbx_s = lambda i: dbx_sb[:, (i % 2) * LQ:(i % 2 + 1) * LQ]
    h_s = lambda i: h_sb[:, (i % 2) * LQ:(i % 2 + 1) * LQ]
    z_s = lambda i: z_sb[:, (i % 2) * LQ:(i % 2 + 1) * LQ]
    et_s = lambda i: etmp_sb[:, (i % 2) * LQ:(i % 2 + 1) * LQ]
    def wintile(eb, k):
        base = ((eb // 2) % 3) * 2048
        off = base + k * 256 + (eb % 2) * 128
        return winrg_sb[:, off:off + 128]
    wotslot = lambda s: wotrg_sb[:, (s % 2) * 2048:(s % 2 + 1) * 2048]
    cvslot = lambda g: cvrg_sb[:, ((g // 2) % 2) * 1024 + (g % 2) * 512:
                                ((g // 2) % 2) * 1024 + (g % 2 + 1) * 512]

    e_sel = S.op("sync", lambda e: e.dma_start(out=sel_sb[:], in_=dsel.ap()),
                 inc="s_in", amt=16)
    e_idf = S.op("sync", lambda e: e.dma_start(out=idf_sb[:], in_=didf.ap()),
                 inc="s_in", amt=16)
    e_alog = S.op("sync", lambda e: e.dma_start(
        out=x_sb[:, 0:GI * NST], in_=dalog.ap()), inc="s_in", amt=16)
    e_cols = S.op("sync", lambda e: e.dma_start(out=cols_sb[:],
                  in_=dcols.ap()), inc="s_in", amt=16)
    e_fcols = S.op("sync", lambda e: e.dma_start(out=fc_sb[:],
                   in_=dfcols.ap()), inc="s_in", amt=16)
    e_wx = S.op("sync", lambda e: e.dma_start(
        out=x_sb[:, 512:512 + GI * 96], in_=dwx.ap()), inc="s_in", amt=16)
    e_wdt = S.op("sync", lambda e: e.dma_start(
        out=AP(x_sb, 2048, [[KD * LQ, 128], [1, DI]]), in_=dwdt.ap()),
        inc="s_in", amt=16)
    e_cc0 = S.op("vector", lambda e: e.memset(cc_sb[:, 0:1], EPS_LN), inc="s_v")
    e_cc1 = S.op("vector", lambda e: e.memset(cc_sb[:, 1:2], 1.0), inc="s_v")
    e_cc2 = S.op("vector", lambda e: e.memset(cc_sb[:, 2:3], -0.5), inc="s_v")
    e_cc3 = S.op("vector", lambda e: e.memset(cc_sb[:, 3:4], -1.0),
                 inc="s_v")
    e_a1 = S.op("scalar", lambda e: e.activation(
        out=a_sb[:], in_=x_sb[:, 0:GI * NST], func=AF.Exp),
        waits=[e_alog], inc="s_s")
    e_a2 = S.op("scalar", lambda e: e.activation(
        out=a_sb[:], in_=a_sb[:], func=AF.Copy, scale=cc_sb[:, 3:4]),
        waits=[e_cc3], inc="s_s")
    e_wxc = S.op("vector", lambda e: e.tensor_copy(
        wx_sb[:], x_sb[:, 512:512 + GI * 96]), waits=[e_wx], inc="s_v")
    e_wdtc = S.op("vector", lambda e: e.tensor_copy(
        wdt_sb[:], AP(x_sb, 2048, [[KD * LQ, 128], [1, DI]])),
        waits=[e_wdt], inc="s_v")
    e_ms1 = S.op("vector", lambda e: e.memset(onesd_sb[:], ONE_OVER_D),
                 inc="s_v")
    e_ms2 = S.op("vector", lambda e: e.memset(halo_sb[:], 0.0), inc="s_v")
    e_ms3 = S.op("vector", lambda e: e.memset(xdbl_sb[:], 0.0), inc="s_v")
    e_ms4 = S.op("vector", lambda e: e.memset(qcol_sb[:], 0.0), inc="s_v")
    e_ms5 = S.op("vector", lambda e: e.memset(gpad_sb[:], 0.0), inc="s_v")
    e_ms6 = S.op("vector", lambda e: e.memset(m2pad_sb[:], 0.0), inc="s_v")

    stg_cast_ev = {}
    stg_store_ev = {}
    stg_idx = [0]
    win_piece_ev = {}

    def _dst(src_dram):
        return {id(dconvd): dcvd_bf, id(dwin): dwin_bf,
                id(dwout): dwout_bf}[id(src_dram)]

    def _rl(src_dram):
        return {id(dconvd): GI * 4 * 128, id(dwin): E,
                id(dwout): D}[id(src_dram)]

    def stage_x(src_dram, off):
        i = stg_idx[0]
        stg_idx[0] += 1
        sl = i % 4
        evl = S.op("sync", lambda e, sl=sl, off=off, src_dram=src_dram:
                   e.dma_start(out=x_sb[:, sl * 1024:(sl + 1) * 1024],
                               in_=AP(src_dram, off,
                                      [[_rl(src_dram), 128], [1, 1024]])),
                   waits=[stg_cast_ev.get(i - 4)], inc="s_in", amt=16)
        evc = S.op("gpsimd", lambda e, sl=sl: e.tensor_copy(
            rep_sb[:, sl * 1024:(sl + 1) * 1024],
            x_sb[:, sl * 1024:(sl + 1) * 1024]),
            waits=[evl, stg_store_ev.get(i - 4)], inc="s_g")
        stg_cast_ev[i] = evc
        evs = S.op("gpsimd", lambda e, sl=sl, off=off, src_dram=src_dram:
                   e.dma_start(out=AP(_dst(src_dram), off,
                                      [[_rl(src_dram), 128], [1, 1024]]),
                               in_=rep_sb[:, sl * 1024:(sl + 1) * 1024]),
                   waits=[evc], inc="s_gd", amt=16)
        stg_store_ev[i] = evs
        return evs

    for p in range(8):
        stage_x(dconvd, p * 1024)
    e_cvd_st = {1: stg_store_ev[7]}
    for c2 in range(2):
        for j in range(KD):
            ev = stage_x(dwin, j * 128 * E + c2 * 1024)
            win_piece_ev[(j, c2 * 2)] = ev
            win_piece_ev[(j, c2 * 2 + 1)] = ev
    ph1_cast_done = stg_cast_ev[stg_idx[0] - 1]

    e_x0 = S.op("gpsimd", lambda e: e.dma_start(
        out=AP(x_sb, 0, [[KD * LQ, 128], [LQ, KD], [1, LQ]]),
        in_=AP(dx, 0, [[L, 128], [128 * L, KD], [1, LQ]])),
        waits=[e_a1, e_wxc, e_wdtc, ph1_cast_done], inc="s_gd", amt=16)

    fst2 = [etmp_sb[:, 0:LQ], etmp_sb[:, LQ:2 * LQ]]
    h1_cast = {}
    h1_store = {}
    h1_idx = [0]

    def stage_h1(off):
        i = h1_idx[0]
        h1_idx[0] += 1
        evl = S.op("gpsimd", lambda e, off=off: e.dma_start(
            out=etmp_sb[:, 0:2 * LQ], in_=AP(dwin, off, [[E, 128],
                                                         [1, 2 * LQ]])),
            waits=[h1_cast.get(i - 1)], inc="s_gd", amt=16)
        evc = S.op("gpsimd", lambda e: e.tensor_copy(
            rep_sb[:, 0:2 * LQ], etmp_sb[:, 0:2 * LQ]),
            waits=[evl, h1_store.get(i - 1)], inc="s_g")
        h1_cast[i] = evc
        evs = S.op("gpsimd", lambda e, off=off: e.dma_start(
            out=AP(dwin_bf, off, [[E, 128], [1, 2 * LQ]]),
            in_=rep_sb[:, 0:2 * LQ]),
            waits=[evc], inc="s_gd", amt=16)
        h1_store[i] = evs
        return evs

    for c in (4, 6):
        for j in range(KD):
            ev = stage_h1(j * 128 * E + c * LQ)
            win_piece_ev[(j, c)] = ev
            win_piece_ev[(j, c + 1)] = ev
    wo_ev = {}
    wo_store = {}
    fw_cast = {}
    fw_store = {}
    late_done = [None]

    def emit_late_staging(e_inv1, e_xln, e_t1_free):
        land = [lnt_sb[:, 0:LQ], inv_sb[:, 0:LQ]]
        lgu = [e_inv1, e_xln]
        lc = {}
        ls = {}
        li = [0]

        def st2(off):
            i = li[0]
            li[0] += 1
            sl = i % 2
            cs = i % 4
            evl = S.op("gpsimd", lambda e, sl=sl, off=off: e.dma_start(
                out=land[sl], in_=AP(dwout, off, [[D, 128], [1, LQ]])),
                waits=[lc.get(i - 2), lgu[sl] if i < 2 else None],
                inc="s_gd", amt=16)
            evc = S.op("gpsimd", lambda e, cs=cs, sl=sl: e.tensor_copy(
                t1_sb[:, cs * LQ:(cs + 1) * LQ], land[sl]),
                waits=[evl, ls.get(i - 4),
                       e_t1_free if i < 4 else None], inc="s_g")
            lc[i] = evc
            evs = S.op("gpsimd", lambda e, cs=cs, off=off: e.dma_start(
                out=AP(dwout_bf, off, [[D, 128], [1, LQ]]),
                in_=t1_sb[:, cs * LQ:(cs + 1) * LQ]),
                waits=[evc], inc="s_gd", amt=16)
            ls[i] = evs
            return evs

        for j in range(16):
            for c in range(2):
                wo_ev[(j, c)] = st2(j * 128 * D + c * LQ)
        wo_store[15] = wo_ev[(15, 1)]
        for j in range(8):
            evl = S.op("gpsimd", lambda e, j=j: e.dma_start(
                out=i8t_sb[:, (j % 2) * 1024:(j % 2 + 1) * 1024],
                in_=AP(dfw8, j * 128 * D, [[D, 128], [1, D]])),
                waits=[fw_cast.get(j - 2)], inc="s_gd", amt=16)
            fw_cast[j] = S.op("gpsimd", lambda e, j=j: e.tensor_copy(
                t1_sb[:, 2048 + (j % 2) * 1024:2048 + (j % 2 + 1) * 1024],
                i8t_sb[:, (j % 2) * 1024:(j % 2 + 1) * 1024]),
                waits=[evl, fw_store.get(j - 2),
                       e_t1_free if j < 2 else None],
                inc="s_g")
            fw_store[j] = S.op("gpsimd", lambda e, j=j: e.dma_start(
                out=AP(dfw_bf, j * 128 * D, [[D, 128], [1, D]]),
                in_=t1_sb[:, 2048 + (j % 2) * 1024:
                          2048 + (j % 2 + 1) * 1024]),
                waits=[fw_cast[j]], inc="s_gd", amt=16)
        late_done[0] = fw_store[7]

    startup_x_free = []
    ev_fstage_done = h1_cast[h1_idx[0] - 1]
    ev_stores_done = h1_store[h1_idx[0] - 1]
    rep_startup_guard = [ev_stores_done]
    startup_done = [e_sel, e_idf, e_a2, e_cols, e_fcols, e_ms1]

    last_win_rd = {}
    last_wot_rd = {}
    last_cv_rd = {}
    prev_conv_done = None
    prev_scan_slot = {}
    prev_rep_reader = None
    prev_da_last = None
    ev_xres_store = {}
    prev_xguard = startup_x_free + [e_ms2, e_ms3] + startup_done
    prev_rowdma = None
    prev_inproj_last = None
    qrow_tmp = [None]

    for q in range(NQ):
        qrow_tmp[0] = qrow_sb[0:1, 0:LQ]
        S.stage = 1
        if q == 0:
            e_x = e_x0
        else:
            e_x = S.op("gpsimd", lambda e, q=q: e.dma_start(
                out=AP(x_sb, 0, [[KD * LQ, 128], [LQ, KD], [1, LQ]]),
                in_=AP(dx, q * LQ, [[L, 128], [128 * L, KD], [1, LQ]])),
                waits=prev_xguard, inc="s_gd", amt=16)
        e_cast = None
        for k in range(KD):
            e_cast = S.op("vector", lambda e, k=k: e.tensor_copy(
                dtg(k), xk(k)), waits=[e_x, prev_da_last], inc="s_v")
        e_m = None
        for k in range(KD):
            e_m = S.op("tensor", lambda e, k=k: e.matmul(
                ps[0][:], onesd_sb[:], dtg(k),
                start=(k == 0), stop=(k == KD - 1)),
                waits=[e_cast, bank_free[0], e_ms1], inc="s_t")
        e_t1 = None
        for k in range(KD):
            e_t1 = S.op("vector", lambda e, k=k: e.tensor_tensor(
                out=t1k(k), in0=xk(k), in1=ps[0][:], op=ALU.subtract),
                waits=[e_m, late_done[0] if q == 1 else None], inc="s_v")
        e_sq = None
        for k in range(KD):
            e_sq = S.op("scalar", lambda e, k=k: e.activation(
                out=dtg(k), in_=t1k(k), func=AF.Square),
                waits=[e_t1, e_m], inc="s_s")
        e_v = None
        for k in range(KD):
            e_v = S.op("tensor", lambda e, k=k: e.matmul(
                ps[1][:], onesd_sb[:], dtg(k),
                start=(k == 0), stop=(k == KD - 1)),
                waits=[e_sq, bank_free[1]], inc="s_t")
        e_ln1 = S.op("scalar", lambda e: e.activation(
            out=lnt_sb[:], in_=ps[1][:], func=AF.Ln, bias=cc_sb[:, 0:1]),
            waits=[e_v, e_cc0, e_cc2], inc="s_s")
        e_inv1 = S.op("scalar", lambda e: e.activation(
            out=inv_sb[:], in_=lnt_sb[:], func=AF.Exp, scale=cc_sb[:, 2:3]),
            inc="s_s")
        bank_free[1] = e_ln1
        e_xln = None
        for k in range(KD):
            e_xln = S.op("vector", lambda e, k=k: e.tensor_tensor(
                out=t1k(k), in0=t1k(k), in1=inv_sb[:], op=ALU.mult),
                waits=[e_inv1], inc="s_v")
        bank_free[0] = e_t1

        S.stage = 2
        xc_rdy = {}
        cv_last_ld = [None]
        cv_mm_ev = {}
        g_done = {}
        xconv_done = {}
        ip_free = {2: bank_free[2], 3: bank_free[3]}
        cv_free = {4: bank_free[4], 5: bank_free[5]}
        e_conv_mm_last = None

        def emit_conv(g, q=q):
            nonlocal e_conv_mm_last
            bk = 4 + g % 2
            if g % 2 == 0:
                cv_last_ld[0] = S.op("sync", lambda e, g=g: e.dma_start(
                    out=cvrg_sb[:, ((g // 2) % 2) * 1024:
                                ((g // 2) % 2 + 1) * 1024],
                    in_=AP(dcvd_bf, g * 4 * 128,
                           [[GI * 4 * 128, 128], [1, 1024]])),
                    waits=[last_cv_rd.get((g // 2) % 2),
                           e_cvd_st[1] if q == 0 and g < 2 else None],
                    inc="s_in", amt=16)
            e_cl = cv_last_ld[0]
            e_cm = None
            for kk in range(4):
                e_cm = S.op("tensor", lambda e, g=g, kk=kk, bk=bk: e.matmul(
                    ps[bk][:], cvslot(g)[:, kk * 128:(kk + 1) * 128],
                    xcs(g, kk, kk + LQ),
                    start=(kk == 0), stop=(kk == 3)),
                    waits=[xc_rdy[g], e_cl, cv_free.get(bk)], inc="s_t")
            last_cv_rd[(g // 2) % 2] = e_cm
            cv_mm_ev[g] = e_cm
            e_cs = S.op("scalar", lambda e, g=g, bk=bk: e.activation(
                out=xcvg(g), in_=ps[bk][:], func=AF.Silu,
                bias=convb_c(g)), waits=[e_cm, e_cols], inc="s_s")
            cv_free[bk] = e_cs
            xconv_done[g] = e_cs
            e_conv_mm_last = e_cm

        last_ld = [None]
        for eb in range(NEB):
            if eb >= 4 and (eb - 4) < GI:
                emit_conv(eb - 4)
            if eb % 2 == 0:
                last_ld[0] = S.op("sync", lambda e, eb=eb: e.dma_start(
                    out=AP(winrg_sb, ((eb // 2) % 3) * 2048,
                           [[3 * 2048, 128], [256, KD], [1, 256]]),
                    in_=AP(dwin_bf, eb * 128,
                           [[E, 128], [128 * E, KD], [1, 256]])),
                    waits=[last_win_rd.get((eb // 2) % 3),
                           win_piece_ev[(KD - 1, (eb + 1) // 4)]
                           if q == 0 else None],
                    inc="s_in", amt=16)
            e_ld = last_ld[0]
            bk = 2 + eb % 2
            e_mm = None
            for k in range(KD):
                e_mm = S.op("tensor", lambda e, eb=eb, k=k, bk=bk: e.matmul(
                    ps[bk][:], wintile(eb, k), t1k(k),
                    start=(k == 0), stop=(k == KD - 1)),
                    waits=[e_ld, e_xln, ip_free.get(bk)], inc="s_t")
            last_win_rd[(eb // 2) % 3] = e_mm
            if eb < 16:
                g = eb
                slot_guard = cv_mm_ev.get(g - 4)
                if q > 0 and g < 4:
                    slot_guard = prev_conv_done
                e_e = S.op("vector", lambda e, g=g, bk=bk: e.tensor_copy(
                    xcs(g, 3, LQ + 3), ps[bk][:]),
                    waits=[e_mm, slot_guard], inc="s_v")
                S.op("vector", lambda e, g=g: e.tensor_copy(
                    xcs(g, 0, 3), halo_sb[:, g * 3:g * 3 + 3]),
                    waits=[e_ms2], inc="s_v")
                if q < NQ - 1:
                    e_sv = S.op("vector", lambda e, g=g: e.tensor_copy(
                        halo_sb[:, g * 3:g * 3 + 3], xcs(g, LQ, LQ + 3)),
                        inc="s_v")
                else:
                    e_sv = S.op("vector", lambda e: e.engine_nop(),
                                inc="s_v")
                xc_rdy[g] = e_sv
                ip_free[bk] = e_e
            else:
                e_e = S.op("scalar", lambda e, eb=eb, bk=bk: e.activation(
                    out=gg(eb - 16), in_=ps[bk][:], func=AF.Silu),
                    waits=[e_mm], inc="s_s")
                g_done[eb - 16] = e_e
                ip_free[bk] = e_e
        for g in range(GI - 4, GI):
            emit_conv(g)
        bank_free[2] = ip_free[2]
        bank_free[3] = ip_free[3]
        bank_free[4] = cv_free[4]
        bank_free[5] = cv_free[5]
        prev_conv_done = e_conv_mm_last
        prev_inproj_last = last_win_rd[((NEB - 1) // 2) % 3]

        e_wxm = None
        for g in range(GI):
            e_wxm = S.op("tensor", lambda e, g=g: e.matmul(
                ps[6][0:96, :], wx_sb[:, g * 96:(g + 1) * 96], xcvg(g),
                start=(g == 0), stop=(g == GI - 1)),
                waits=[xconv_done[g], e_wxc, bank_free[6]], inc="s_t")
        e_xdbl = S.op("vector", lambda e: e.tensor_copy(
            xdbl_sb[0:96, :], ps[6][0:96, :]),
            waits=[e_wxm, e_ms3], inc="s_v")
        bank_free[6] = e_xdbl

        dt_done = {}
        dt_free = {7: bank_free[7], 0: bank_free[0]}
        for g in range(GI):
            bk = 7 if g % 2 == 0 else 0
            e_dm = S.op("tensor", lambda e, g=g, bk=bk: e.matmul(
                ps[bk][:], wdt_sb[:, g * 128:(g + 1) * 128],
                xdbl_sb[:], start=True, stop=True),
                waits=[e_xdbl, e_wdtc, dt_free.get(bk),
                       e_inv1 if bk == 0 else None], inc="s_t")
            e_de = S.op("scalar", lambda e, g=g, bk=bk: e.activation(
                out=et_s(g), in_=ps[bk][:], func=AF.Exp, bias=bdt_c(g)),
                waits=[e_dm, ev_fstage_done if q == 0 and g < 2 else None],
                inc="s_s")
            e_dl = S.op("scalar", lambda e, g=g: e.activation(
                out=dtg(g), in_=et_s(g), func=AF.Ln, bias=cc_sb[:, 1:2]),
                waits=[e_cc1], inc="s_s")
            dt_free[bk] = e_de
            dt_done[g] = e_dl
        bank_free[7] = dt_free[7]
        bank_free[0] = dt_free[0]

        xcd_done = {}
        u_done = {}
        for g in range(GI):
            xcd_done[g] = S.op("vector", lambda e, g=g: e.tensor_scalar(
                out=yg(g), in0=xcvg(g), scalar1=dp_c(g), scalar2=None,
                op0=ALU.mult), waits=[xconv_done[g], e_cols], inc="s_v")
            u_done[g] = S.op("vector", lambda e, g=g: e.tensor_tensor(
                out=xcvg(g), in0=dtg(g), in1=xcvg(g), op=ALU.mult),
                waits=[dt_done[g]], inc="s_v")

        repB_ev = {}
        repC_ev = {}
        rb_free = bank_free[6]
        rc_free = bank_free[7]
        for n in range(NST):
            e_rb = S.op("tensor", lambda e, n=n: e.matmul(
                ps[6][:], selB(n), xdbl_sb[:], start=True, stop=True),
                waits=[e_xdbl, e_sel, rb_free], inc="s_t")
            e_rbe = S.op("scalar", lambda e, n=n: e.copy(
                out=repB(n), in_=ps[6][:]),
                waits=[e_rb, prev_rep_reader] +
                      (rep_startup_guard if q == 0 else []), inc="s_s")
            e_rc = S.op("tensor", lambda e, n=n: e.matmul(
                ps[7][:], selC(n), xdbl_sb[:], start=True, stop=True),
                waits=[rc_free], inc="s_t")
            e_rce = S.op("scalar", lambda e, n=n: e.copy(
                out=repC(n), in_=ps[7][:]),
                waits=[e_rc, prev_rep_reader] +
                      (rep_startup_guard if q == 0 else []), inc="s_s")
            rb_free, rc_free = e_rbe, e_rce
            repB_ev[n] = e_rbe
            repC_ev[n] = e_rce
        bank_free[6] = rb_free
        bank_free[7] = rc_free

        S.stage = 4
        gate_ev = {}
        scan_ev = {}
        zmm_ev = {}
        da_ev = {}
        y_free = {2: bank_free[2], 3: bank_free[3]}

        def emit_da(j, q=q):
            if j >= GI * NST:
                return
            gj, nj = j // NST, j % NST
            da_ev[j] = S.op("scalar", lambda e, gj=gj, nj=nj, j=j:
                            e.activation(out=da_s(j), in_=dtg(gj),
                                         func=AF.Exp, scale=acol(gj, nj)),
                            waits=[dt_done[gj], e_a2,
                                   scan_ev.get(j - 2,
                                               prev_scan_slot.get(j % 2))],
                            inc="s_s")

        emit_da(0)
        emit_da(1)
        ii = 0
        for g in range(GI):
            bk = 2 + g % 2
            S.op("tensor", lambda e, g=g, bk=bk: e.matmul(
                ps[bk][:], idb(), yg(g), start=True, stop=False),
                waits=[xcd_done[g], y_free.get(bk)], inc="s_t")
            for n in range(NST):
                i = ii
                ii += 1
                e_da = da_ev[i]
                S.stage = 64
                S.op("vector", lambda e, g=g, n=n, i=i: e.tensor_tensor(
                    out=dbx_s(i), in0=xcvg(g), in1=repB(n), op=ALU.mult),
                    waits=[u_done[g], repB_ev[n]], inc="s_v")
                S.stage = 128
                if q == 0:
                    e_h = S.op("vector", lambda e, i=i:
                               e.tensor_tensor_scan(
                                   out=h_s(i), data0=da_s(i), data1=dbx_s(i),
                                   initial=0.0, op0=ALU.mult, op1=ALU.add),
                               waits=[e_da], inc="s_v")
                else:
                    cr = ((q - 1) % 2) * GI * NST + g * NST + n
                    e_h = S.op("vector", lambda e, cr=cr, i=i:
                               e.tensor_tensor_scan(
                                   out=h_s(i), data0=da_s(i), data1=dbx_s(i),
                                   initial=carry_sb[:, cr:cr + 1],
                                   op0=ALU.mult, op1=ALU.add),
                               waits=[e_da], inc="s_v")
                scan_ev[i] = e_h
                emit_da(i + 2)
                S.stage = 256
                if q < NQ - 1:
                    cw = (q % 2) * GI * NST + g * NST + n
                    S.op("scalar", lambda e, cw=cw, i=i: e.copy(
                        out=carry_sb[:, cw:cw + 1],
                        in_=h_s(i)[:, LQ - 1:LQ]),
                        waits=[scan_ev[i]], inc="s_s")
                S.stage = 512
                e_z = S.op("vector", lambda e, n=n, i=i: e.tensor_tensor(
                    out=z_s(i), in0=h_s(i), in1=repC(n), op=ALU.mult),
                    waits=[repC_ev[n], zmm_ev.get(i - 2)], inc="s_v")
                zmm_ev[i] = S.op("tensor",
                                 lambda e, i=i, n=n, bk=bk: e.matmul(
                                     ps[bk][:], idb(), z_s(i),
                                     start=False, stop=(n == NST - 1)),
                                 waits=[e_z], inc="s_t")
            S.stage = 1024
            e_gate = S.op("vector", lambda e, g=g, bk=bk: e.tensor_tensor(
                out=yg(g), in0=ps[bk][:], in1=gg(g), op=ALU.mult),
                waits=[zmm_ev[ii - 1], g_done[g]], inc="s_v")
            S.stage = 4
            y_free[bk] = e_gate
            gate_ev[g] = e_gate
        bank_free[2] = y_free[2]
        bank_free[3] = y_free[3]
        prev_scan_slot = {0: scan_ev[ii - 2], 1: scan_ev[ii - 1]}
        prev_rep_reader = gate_ev[GI - 1]
        if q == 0:
            emit_late_staging(e_inv1, e_xln, prev_inproj_last)
        prev_da_last = da_ev[GI * NST - 1]

        S.stage = 8
        op_evac = {}
        op_free = {4: bank_free[4], 5: bank_free[5]}
        for dm in range(KD):
            e_wol = S.op("gpsimd", lambda e, dm=dm: e.dma_start(
                out=AP(wotrg_sb, (dm % 2) * 2048,
                       [[2 * 2048, 128], [128, GI], [1, 128]]),
                in_=AP(dwout_bf, dm * 128,
                       [[D, 128], [128 * D, GI], [1, 128]])),
                waits=[last_wot_rd.get(dm % 2)] +
                      ([wo_store[15]] if q == 0 and dm < 2 else []),
                inc="s_gd", amt=16)
            bk = 4 + dm % 2
            e_om = None
            for g in range(GI):
                e_om = S.op("tensor", lambda e, dm=dm, g=g, bk=bk: e.matmul(
                    ps[bk][:], wotslot(dm)[:, g * 128:(g + 1) * 128], yg(g),
                    start=(g == 0), stop=(g == GI - 1)),
                    waits=[e_wol, gate_ev[g], op_free.get(bk)], inc="s_t")
            last_wot_rd[dm % 2] = e_om
            e_oe = S.op("vector", lambda e, dm=dm, bk=bk: e.tensor_tensor(
                out=xk(dm), in0=ps[bk][:], in1=xk(dm), op=ALU.add),
                waits=[e_om], inc="s_v")
            op_evac[dm] = e_oe
            op_free[bk] = e_oe
        bank_free[4] = op_free[4]
        bank_free[5] = op_free[5]

        e_cast2 = None
        for k in range(KD):
            e_cast2 = S.op("vector", lambda e, k=k: e.tensor_copy(
                dtg(k), xk(k)), waits=[op_evac[k], prev_da_last], inc="s_v")
        e_m2 = None
        for k in range(KD):
            e_m2 = S.op("tensor", lambda e, k=k: e.matmul(
                ps[0][:], onesd_sb[:], dtg(k),
                start=(k == 0), stop=(k == KD - 1)),
                waits=[e_cast2, bank_free[0]], inc="s_t")
        e_sq2 = None
        for k in range(KD):
            e_sq2 = S.op("scalar", lambda e, k=k: e.activation(
                out=xcvg(k), in_=dtg(k), func=AF.Square),
                waits=[e_cast2, prev_rep_reader], inc="s_s")
        e_v2 = None
        for k in range(KD):
            e_v2 = S.op("tensor", lambda e, k=k: e.matmul(
                ps[1][:], onesd_sb[:], xcvg(k),
                start=(k == 0), stop=(k == KD - 1)),
                waits=[e_sq2, bank_free[1]], inc="s_t")
        e_c12 = S.op("scalar", lambda e: e.activation(
            out=lnt_sb[:], in_=ps[0][:], func=AF.Square),
            waits=[e_m2, late_done[0] if q == 0 else None], inc="s_s")
        e_c22 = S.op("vector", lambda e: e.tensor_tensor(
            out=lnt_sb[:], in0=ps[1][:], in1=lnt_sb[:], op=ALU.subtract),
            waits=[e_v2, e_c12], inc="s_v")
        e_ln2 = S.op("scalar", lambda e: e.activation(
            out=ps[1][:], in_=lnt_sb[:], func=AF.Ln, bias=cc_sb[:, 0:1]),
            waits=[e_c22], inc="s_s")
        e_inv2 = S.op("scalar", lambda e: e.activation(
            out=inv_sb[:], in_=ps[1][:], func=AF.Exp, scale=cc_sb[:, 2:3]),
            waits=[late_done[0] if q == 0 else None], inc="s_s")
        bank_free[1] = e_inv2
        e_r1 = S.op("vector", lambda e, q=q: e.tensor_copy(
            m2pad_sb[0:1, q * LQ:(q + 1) * LQ], ps[0][0:1, :]),
            waits=[e_m2, e_ms6, prev_rowdma], inc="s_v")
        e_r2 = S.op("vector", lambda e, q=q: e.tensor_tensor(
            out=padtmp_sb[0:1, 0:LQ], in0=ps[0][0:1, :],
            in1=m2pad_sb[0:1, q * LQ:(q + 1) * LQ], op=ALU.subtract),
            inc="s_v")
        e_r3 = S.op("vector", lambda e, q=q: e.tensor_copy(
            padtmp_sb[0:1, LQ:2 * LQ], inv_sb[0:1, :]),
            waits=[e_inv2], inc="s_v")
        e_r4 = S.op("vector", lambda e, q=q: e.tensor_tensor(
            out=padtmp_sb[0:1, 0:LQ] if False else qrow_tmp[0],
            in0=inv_sb[0:1, :], in1=padtmp_sb[0:1, LQ:2 * LQ],
            op=ALU.subtract), inc="s_v")
        e_d1 = S.op("gpsimd", lambda e, q=q: e.dma_start(
            out=AP(m2pad_sb, 32 * L + q * LQ, [[L, 1], [1, LQ]]),
            in_=padtmp_sb[0:1, 0:LQ]), waits=[e_r2], inc="s_gd", amt=16)
        e_d2 = S.op("gpsimd", lambda e, q=q: e.dma_start(
            out=AP(m2pad_sb, 64 * L + q * LQ, [[L, 1], [1, LQ]]),
            in_=padtmp_sb[0:1, LQ:2 * LQ]), waits=[e_r3], inc="s_gd", amt=16)
        e_d3 = S.op("gpsimd", lambda e, q=q: e.dma_start(
            out=AP(m2pad_sb, 96 * L + q * LQ, [[L, 1], [1, LQ]]),
            in_=qrow_tmp[0]), waits=[e_r4], inc="s_gd", amt=16)
        prev_rowdma = e_d3
        e_qp = None
        for k in range(KD):
            S.op("vector", lambda e, k=k: e.tensor_tensor(
                out=et_s(k), in0=xk(k), in1=ps[0][:], op=ALU.subtract),
                waits=[e_inv2], inc="s_v")
            S.op("vector", lambda e, k=k: e.tensor_tensor(
                out=et_s(k), in0=et_s(k), in1=inv_sb[:], op=ALU.mult),
                inc="s_v")
            S.op("vector", lambda e, k=k: e.tensor_reduce(
                out=qcol_sb[:, 1:2], in_=et_s(k), axis=mybir.AxisListType.X,
                op=ALU.max, apply_absolute_value=True),
                waits=[e_ms4], inc="s_v")
            e_qp = S.op("vector", lambda e: e.tensor_tensor(
                out=qcol_sb[:, 0:1], in0=qcol_sb[:, 0:1],
                in1=qcol_sb[:, 1:2], op=ALU.max), inc="s_v")
        bank_free[0] = e_qp
        ev_xres_store[q] = S.op("gpsimd", lambda e, q=q: e.dma_start(
            out=AP(dxres, q * LQ, [[L, 128], [128 * L, KD], [1, LQ]]),
            in_=AP(x_sb, 0, [[KD * LQ, 128], [LQ, KD], [1, LQ]])),
            waits=[op_evac[KD - 1]], inc="s_gd", amt=16)
        prev_xguard = [ev_xres_store[q], e_qp, e_r1, e_d3]
        prev_da_last = e_sq2

    S.stage = 2048
    e_tr = S.op("tensor", lambda e: e.transpose(
        ps[6][0:1, 0:128], qcol_sb[:, 0:1], idf_sb[:]),
        waits=[e_qp, e_idf, bank_free[6]], inc="s_t")
    e_gm = S.op("vector", lambda e: e.tensor_reduce(
        out=smal_sb[0:1, 0:1], in_=ps[6][0:1, 0:128],
        axis=mybir.AxisListType.X, op=ALU.max), waits=[e_tr], inc="s_v")
    bank_free[6] = e_gm
    e_st = S.op("gpsimd", lambda e: e.dma_start(
        out=AP(dccin, 0, [[1, 1], [1, 1]]), in_=smal_sb[0:1, 0:1]),
        waits=[e_gm], inc="s_gd", amt=16)
    e_cc = S.op("gpsimd", lambda e: e.collective_compute(
        "AllReduce", ALU.max, replica_groups=[list(range(8))],
        ins=[dccin.ap().opt()], outs=[dccout.ap().opt()]),
        waits=[e_st], inc="s_cc")
    e_ccl = S.op("gpsimd", lambda e: e.dma_start(
        out=smal_sb[0:1, 2:3], in_=dccout.ap()), waits=[e_cc],
        inc="s_gd", amt=16)
    e_xs = S.op("vector", lambda e: e.tensor_scalar(
        out=smal_sb[0:1, 1:2], in0=smal_sb[0:1, 2:3], scalar1=1.0 / 127.0,
        scalar2=EPS_FFN, op0=ALU.mult, op1=ALU.max),
        waits=[e_ccl], inc="s_v")
    e_is = S.op("vector", lambda e: e.reciprocal(
        out=smal_sb[0:1, 3:4], in_=smal_sb[0:1, 1:2]), inc="s_v")
    e_gp1 = S.op("vector", lambda e: e.tensor_copy(
        gpad_sb[0:1, 0:1], smal_sb[0:1, 1:2]),
        waits=[e_ms5, e_xs], inc="s_v")
    e_gp2 = S.op("vector", lambda e: e.tensor_tensor(
        out=gpad_sb[0:1, 1:2], in0=smal_sb[0:1, 1:2], in1=gpad_sb[0:1, 0:1],
        op=ALU.subtract), inc="s_v")
    e_gp3 = S.op("vector", lambda e: e.tensor_copy(
        gpad_sb[0:1, 2:3], smal_sb[0:1, 3:4]), waits=[e_is], inc="s_v")
    e_gp4 = S.op("vector", lambda e: e.tensor_tensor(
        out=gpad_sb[0:1, 3:4], in0=smal_sb[0:1, 3:4], in1=gpad_sb[0:1, 2:3],
        op=ALU.subtract), inc="s_v")
    e_xsc = None
    for j in range(2):
        e_xsc = S.op("tensor", lambda e, j=j: e.matmul(
            ps[7][:, 0:1], selR(0), gpad_sb[:, j:j + 1],
            start=(j == 0), stop=(j == 1)),
            waits=[e_gp2, e_sel, bank_free[7]], inc="s_t")
    e_wsx = S.op("vector", lambda e: e.tensor_scalar(
        out=wsx_sb[:], in0=fc_sb[:, 0:8], scalar1=ps[7][:, 0:1],
        scalar2=None, op0=ALU.mult), waits=[e_xsc, e_fcols], inc="s_v")
    bank_free[7] = e_wsx
    e_isc = None
    for j in range(2):
        e_isc = S.op("tensor", lambda e, j=j: e.matmul(
            ps[6][:, 0:1], selR(0), gpad_sb[:, 2 + j:3 + j],
            start=(j == 0), stop=(j == 1)),
            waits=[e_gp4, bank_free[6]], inc="s_t")
    e_iscol = S.op("vector", lambda e: e.tensor_copy(
        iscol_sb[:], ps[6][:, 0:1]), waits=[e_isc], inc="s_v")
    bank_free[6] = e_iscol
    e_fw = S.op("sync", lambda e: e.dma_start(
        out=rep_sb[:, 0:8192],
        in_=AP(dfw_bf, 0, [[D, 128], [128 * D, 8], [1, D]])),
        waits=[prev_rep_reader, fw_store[7]], inc="s_in", amt=16)
    fwt = lambda k, nb: rep_sb[:, k * 1024 + nb * 128:
                               k * 1024 + (nb + 1) * 128]

    S.stage = 16
    prev_ffn_done = None
    ps0_free = bank_free[0]
    ps1_free = bank_free[1]
    ffn_free = {2: bank_free[2], 3: bank_free[3]}
    for q in range(NQ):
        e_xl = S.op("gpsimd", lambda e, q=q: e.dma_start(
            out=AP(x_sb, 0, [[KD * LQ, 128], [LQ, KD], [1, LQ]]),
            in_=AP(dxres, q * LQ, [[L, 128], [128 * L, KD], [1, LQ]])),
            waits=[e_iscol, e_wsx, prev_ffn_done, e_qp if q == 0 else None],
            inc="s_gd", amt=16)
        e_m2r = None
        for j in range(2):
            e_m2r = S.op("tensor", lambda e, q=q, j=j: e.matmul(
                ps[0][:], selR(j), m2pad_sb[:, q * LQ:(q + 1) * LQ],
                start=(j == 0), stop=(j == 1)),
                waits=[prev_rowdma, ps0_free], inc="s_t")
        e_sir = None
        for j in range(2):
            e_sir = S.op("tensor", lambda e, q=q, j=j: e.matmul(
                ps[1][:], selR(2 + j), m2pad_sb[:, q * LQ:(q + 1) * LQ],
                start=(j == 0), stop=(j == 1)),
                waits=[prev_rowdma, ps1_free], inc="s_t")
        e_sirs = S.op("vector", lambda e: e.tensor_scalar(
            out=inv_sb[:], in0=ps[1][:], scalar1=iscol_sb[:, 0:1],
            scalar2=None, op0=ALU.mult), waits=[e_sir, e_iscol], inc="s_v")
        e_xq = {}
        for k in range(KD):
            S.op("vector", lambda e, k=k: e.tensor_tensor(
                out=et_s(k), in0=xk(k), in1=ps[0][:], op=ALU.subtract),
                waits=[e_xl, e_m2r, prev_ffn_done], inc="s_v")
            S.op("vector", lambda e, k=k: e.tensor_tensor(
                out=et_s(k), in0=et_s(k), in1=inv_sb[:], op=ALU.mult),
                waits=[e_sirs], inc="s_v")
            e_xq[k] = S.op("vector", lambda e, k=k: e.tensor_scalar(
                out=dtg(k), in0=et_s(k), scalar1=MAGIC, scalar2=MAGIC,
                op0=ALU.add, op1=ALU.subtract), inc="s_v")
        ps0_free = e_xq[KD - 1]
        ps1_free = e_sirs
        e_fin = {}
        for nb in range(KD):
            bk = 2 + nb % 2
            e_fm = None
            for k in range(KD):
                e_fm = S.op("tensor", lambda e, nb=nb, k=k, bk=bk: e.matmul(
                    ps[bk][:], fwt(k, nb), dtg(k),
                    start=(k == 0), stop=(k == KD - 1)),
                    waits=[e_fw, e_xq[k], ffn_free.get(bk)], inc="s_t")
            e_fe = S.op("scalar", lambda e, nb=nb, bk=bk: e.activation(
                out=et_s(nb), in_=ps[bk][:], func=AF.Copy,
                scale=wsx_sb[:, nb:nb + 1]),
                waits=[e_fm, e_wsx, e_xq[KD - 1], e_fin.get(nb - 2)],
                inc="s_s")
            e_fa = S.op("vector", lambda e, nb=nb: e.scalar_tensor_tensor(
                out=xk(nb), in0=et_s(nb), scalar=fc_sb[:, 8 + nb:9 + nb],
                in1=xk(nb), op0=ALU.add, op1=ALU.add),
                waits=[e_fe, e_fcols], inc="s_v")
            ffn_free[bk] = e_fa
            e_fin[nb] = e_fa
        prev_ffn_done = e_fin[KD - 1]
        S.op("gpsimd", lambda e, q=q: e.dma_start(
            out=AP(dout, q * LQ, [[L, 128], [128 * L, KD], [1, LQ]]),
            in_=AP(x_sb, 0, [[KD * LQ, 128], [LQ, KD], [1, LQ]])),
            waits=[e_fin[KD - 1]], inc="s_gd", amt=16)

    S.op("vector", lambda e: e.tensor_copy(
        etmp_sb[:, 0:2 * GI * NST], carry_sb[:]), inc="s_v")
    S.op("gpsimd", lambda e: e.dma_start(
        out=ddbg.ap(), in_=etmp_sb[:, 0:2 * GI * NST]),
        waits=[("s_v", S.cnt["s_v"])], inc="s_gd", amt=16)
    final_gd = S.cnt["s_gd"]
    S.op("gpsimd", lambda e: e.nop(), waits=[("s_gd", final_gd)], inc=None)

    with es:
        block = es.enter_context(nc.Block())

        def replay(eng_name):
            def _(eng):
                hw = {}
                for (en, fn, waits, inc, amt) in S.ops:
                    if en != eng_name:
                        continue
                    for (sn, val) in waits:
                        if hw.get(sn, 0) < val:
                            eng.wait_ge(sems[sn], val)
                            hw[sn] = val
                    instr = fn(eng)
                    if inc is not None:
                        instr.then_inc(sems[inc], amt)
                    if KDBG and eng_name != "sync":
                        try:
                            eng.drain()
                        except Exception:
                            pass
            return _

        block.sync(replay("sync"))
        block.vector(replay("vector"))
        block.scalar(replay("scalar"))
        block.tensor(replay("tensor"))
        block.gpsimd(replay("gpsimd"))
    return nc


def kernel(L0_attn_x, L0_attn_W_in, L0_attn_conv_w, L0_attn_conv_b,
           L0_attn_W_x, L0_attn_W_dt, L0_attn_b_dt, L0_attn_A_log,
           L0_attn_Dp, L0_attn_W_out, L0_ffn_weight_int8,
           L0_ffn_weight_scale, L0_ffn_bias):
    import ml_dtypes
    x = np.asarray(L0_attn_x, dtype=np.float32)
    W_in = np.asarray(L0_attn_W_in, dtype=np.float32)
    conv_w = np.asarray(L0_attn_conv_w, dtype=np.float32)
    conv_b = np.asarray(L0_attn_conv_b, dtype=np.float32)
    W_x = np.asarray(L0_attn_W_x, dtype=np.float32)
    W_dt = np.asarray(L0_attn_W_dt, dtype=np.float32)
    b_dt = np.asarray(L0_attn_b_dt, dtype=np.float32)
    A_log = np.asarray(L0_attn_A_log, dtype=np.float32)
    Dp = np.asarray(L0_attn_Dp, dtype=np.float32)
    W_out = np.asarray(L0_attn_W_out, dtype=np.float32)
    fw8 = np.asarray(L0_ffn_weight_int8)
    fws = np.asarray(L0_ffn_weight_scale, dtype=np.float32)
    fb = np.asarray(L0_ffn_bias, dtype=np.float32)

    x_fm = np.ascontiguousarray(np.swapaxes(x, 1, 2))
    winT = np.ascontiguousarray(W_in.T)
    woutT = np.ascontiguousarray(W_out.T)
    wxT_t = np.ascontiguousarray(
        W_x.T.reshape(GI, 128, 96).transpose(1, 0, 2).reshape(128, GI * 96))
    wdtT = np.zeros((128, DI), np.float32)
    wdtT[0:RNK, :] = W_dt.T
    fwT8 = np.ascontiguousarray(fw8.T)
    convd = np.zeros((128, GI * 4 * 128), np.float32)
    for g in range(GI):
        for kk in range(4):
            blk = convd[:, (g * 4 + kk) * 128:(g * 4 + kk + 1) * 128]
            np.fill_diagonal(blk, conv_w[g * 128:(g + 1) * 128, kk])
    sel = np.zeros((128, 37 * 128), ml_dtypes.bfloat16)
    for n in range(NST):
        sel[64 + n, n * 128:(n + 1) * 128] = 1
        sel[80 + n, (16 + n) * 128:(17 + n) * 128] = 1
    sel[:, 32 * 128:33 * 128] = np.eye(128, dtype=ml_dtypes.bfloat16)
    for j, row in enumerate((0, 32, 64, 96)):
        sel[row, (33 + j) * 128:(34 + j) * 128] = 1
    identf = np.eye(128, dtype=np.float32)
    alog_fm = np.ascontiguousarray(
        A_log.reshape(GI, 128, NST).transpose(1, 0, 2).reshape(128, GI * NST))
    colify = lambda v: np.ascontiguousarray(v.reshape(GI, 128).T)
    cols = np.concatenate([colify(conv_b), colify(b_dt), colify(Dp)], axis=1)
    colify8 = lambda v: np.ascontiguousarray(v.reshape(8, 128).T)
    fcols = np.concatenate([colify8(fws[:, 0]), colify8(fb)], axis=1)

    nc = build_nc()
    in_maps = []
    for c in range(8):
        in_maps.append({
            "x_fm": np.ascontiguousarray(x_fm[c]),
            "winT": winT, "woutT": woutT, "wxT_t": wxT_t, "wdtT": wdtT,
            "convd": convd, "sel_bf": sel,
            "ident_f32": identf, "alog_fm": alog_fm, "cols": cols,
            "fcols": fcols, "fwT8": fwT8,
        })
    res = run_bass_kernel_spmd(nc, in_maps, core_ids=list(range(8)))
    out_fm = np.stack([np.asarray(r["out"]) for r in res.results])
    return np.ascontiguousarray(np.swapaxes(out_fm, 1, 2))


if __name__ == "__main__":
    pass


# version 13 (final)
# speedup vs baseline: 1.3323x; 1.0121x over previous
import os
import numpy as np
from contextlib import ExitStack

KDBG = os.environ.get("KDBG", "0") == "1"
KSER = os.environ.get("KSER", "0") == "1"
KSERC = int(os.environ.get("KSERC", "0"))
KSERV = int(os.environ.get("KSERV", "2048"))

import concourse.bass as bass
import concourse.mybir as mybir
from concourse.bass_utils import run_bass_kernel_spmd

F32 = mybir.dt.float32
BF16 = mybir.dt.bfloat16
I8 = mybir.dt.int8
AF = mybir.ActivationFunctionType
ALU = mybir.AluOpType

MAGIC = 12582912.0
EPS_LN = 1e-5
EPS_FFN = 1e-5
ONE_OVER_D = 1.0 / 1024.0

L = 2048
D = 1024
DI = 2048
E = 4096
NST = 16
RNK = 64
NQ = 4
LQ = 512
KD = 8
GI = 16
NEB = 32


class Sched:

    def __init__(self):
        self.ops = []
        self.cnt = {}

    CLS = {"sync": 1, "tensor": 2, "vector": 4, "scalar": 8, "gpsimd": 16}
    stage = 32

    def op(self, eng, fn, waits=(), inc=None, amt=1):
        ser = KSER or (KSERC & self.CLS[eng]) or (
            eng == "vector" and (KSERV & self.stage))
        snap = dict(self.cnt) if ser else None
        ws = tuple(w for w in waits if w is not None)
        if inc == "s_in" and self.cnt.get("s_in"):
            ws = ws + (("s_in", self.cnt["s_in"]),)
        if inc == "s_gd" and self.cnt.get("s_gd"):
            ws = ws + (("s_gd", self.cnt["s_gd"]),)
        ev = None
        if inc is not None:
            self.cnt[inc] = self.cnt.get(inc, 0) + amt
            ev = (inc, self.cnt[inc])
        if snap:
            ws = ws + tuple(snap.items())
        self.ops.append((eng, fn, ws, inc, amt))
        return ev


def build_nc():
    nc = bass.Bass("TRN2", target_bir_lowering=False,
                   detect_race_conditions=KDBG)

    dx = nc.dram_tensor("x_fm", [D, L], F32, kind="ExternalInput")
    dwin = nc.dram_tensor("winT", [D, E], F32, kind="ExternalInput")
    dwout = nc.dram_tensor("woutT", [DI, D], F32, kind="ExternalInput")
    dwx = nc.dram_tensor("wxT_t", [128, GI * 96], F32, kind="ExternalInput")
    dwdt = nc.dram_tensor("wdtT", [128, DI], F32, kind="ExternalInput")
    dconvd = nc.dram_tensor("convd", [128, GI * 4 * 128], F32,
                            kind="ExternalInput")
    dsel = nc.dram_tensor("sel_bf", [128, 37 * 128], BF16,
                          kind="ExternalInput")
    didf = nc.dram_tensor("ident_f32", [128, 128], F32, kind="ExternalInput")
    dalog = nc.dram_tensor("alog_fm", [128, GI * NST], F32,
                           kind="ExternalInput")
    dcols = nc.dram_tensor("cols", [128, 48], F32, kind="ExternalInput")
    dfcols = nc.dram_tensor("fcols", [128, 16], F32, kind="ExternalInput")
    dfw8 = nc.dram_tensor("fwT8", [D, D], I8, kind="ExternalInput")
    dout = nc.dram_tensor("out", [D, L], F32, kind="ExternalOutput")
    ddbg = nc.dram_tensor("dbg", [128, 2 * GI * NST], F32,
                          kind="ExternalOutput")
    dwin_bf = nc.dram_tensor("win_bf", [D, E], BF16)
    dwout_bf = nc.dram_tensor("wout_bf", [DI, D], BF16)
    dfw_bf = nc.dram_tensor("fw_bf", [D, D], BF16)
    dcvd_bf = nc.dram_tensor("cvd_bf", [128, GI * 4 * 128], BF16)
    dxres = nc.dram_tensor("xres", [D, L], F32)
    dccin = nc.dram_tensor("cc_in", [1, 1], F32)
    dccout = nc.dram_tensor("cc_out", [1, 1], F32)

    es = ExitStack()

    def sbt(name, shape, dt=F32):
        return es.enter_context(nc.sbuf_tensor(name, shape, dt))

    x_sb = sbt("x_sb", [128, KD * LQ])
    t1_sb = sbt("t1_sb", [128, KD * LQ], BF16)
    xcr_sb = sbt("xcr_sb", [128, 4 * (LQ + 3)], BF16)
    halo_sb = sbt("halo_sb", [128, GI * 3], BF16)
    g_sb = sbt("g_sb", [128, GI * LQ], BF16)
    xcv_sb = sbt("xcv_sb", [128, GI * LQ], BF16)
    dt_sb = sbt("dt_sb", [128, GI * LQ], BF16)
    y_sb = sbt("y_sb", [128, GI * LQ], BF16)
    rep_sb = sbt("rep_sb", [128, 32 * LQ], BF16)
    xdbl_sb = sbt("xdbl_sb", [128, LQ], BF16)
    da_sb = sbt("da_sb", [128, 2 * LQ], BF16)
    dbx_sb = sbt("dbx_sb", [128, 2 * LQ], BF16)
    h_sb = sbt("h_sb", [128, 2 * LQ], BF16)
    z_sb = sbt("z_sb", [128, 2 * LQ], BF16)
    carry_sb = sbt("carry_sb", [128, 2 * GI * NST], BF16)
    inv_sb = sbt("inv_sb", [128, LQ])
    etmp_sb = sbt("etmp_sb", [128, 2 * LQ])
    lnt_sb = sbt("lnt_sb", [128, LQ])
    m2pad_sb = sbt("m2pad_sb", [128, L], BF16)
    padtmp_sb = sbt("padtmp_sb", [1, 2 * LQ], BF16)
    qrow_sb = sbt("qrow_sb", [1, LQ], BF16)
    qcol_sb = sbt("qcol_sb", [128, 2])
    smal_sb = sbt("smal_sb", [1, 16])
    gpad_sb = sbt("gpad_sb", [128, 4], BF16)
    iscol_sb = sbt("iscol_sb", [128, 1])
    wsx_sb = sbt("wsx_sb", [128, 8])
    fc_sb = sbt("fc_sb", [128, 16])
    a_sb = sbt("a_sb", [128, GI * NST])
    cols_sb = sbt("cols_sb", [128, 48])
    cc_sb = sbt("cc_sb", [128, 4])
    sel_sb = sbt("sel_sb", [128, 37 * 128], BF16)
    idf_sb = sbt("idf_sb", [128, 128])
    onesd_sb = sbt("onesd_sb", [128, 128], BF16)
    wx_sb = sbt("wx_sb", [128, GI * 96], BF16)
    wdt_sb = sbt("wdt_sb", [128, DI], BF16)
    cvrg_sb = sbt("cvrg_sb", [128, 2 * 1024], BF16)
    winrg_sb = sbt("winrg_sb", [128, 3 * 2048], BF16)
    wotrg_sb = sbt("wotrg_sb", [128, 2 * 2048], BF16)
    i8t_sb = sbt("i8t_sb", [128, 2 * 1024], I8)

    ps = [es.enter_context(nc.psum_tensor(f"pb{i}", [128, LQ], F32))
          for i in range(8)]

    sem_names = ["s_in", "s_gd", "s_t", "s_v", "s_s", "s_g", "s_cc"]
    sems = {n: es.enter_context(nc.semaphore(n)) for n in sem_names}

    S = Sched()
    AP = bass.AP

    bank_free = [None] * 8

    idb = lambda: sel_sb[:, 32 * 128:33 * 128]

    acol = lambda g, n: a_sb[:, g * NST + n:g * NST + n + 1]
    convb_c = lambda g: cols_sb[:, g:g + 1]
    bdt_c = lambda g: cols_sb[:, 16 + g:16 + g + 1]
    dp_c = lambda g: cols_sb[:, 32 + g:32 + g + 1]

    xk = lambda k: x_sb[:, k * LQ:(k + 1) * LQ]
    t1k = lambda k: t1_sb[:, k * LQ:(k + 1) * LQ]
    xcs = lambda g, a, b: xcr_sb[:, (g % 4) * (LQ + 3) + a:
                                 (g % 4) * (LQ + 3) + b]
    gg = lambda g: g_sb[:, g * LQ:(g + 1) * LQ]
    xcvg = lambda g: xcv_sb[:, g * LQ:(g + 1) * LQ]
    dtg = lambda g: dt_sb[:, g * LQ:(g + 1) * LQ]
    yg = lambda g: y_sb[:, g * LQ:(g + 1) * LQ]
    repB = lambda n: rep_sb[:, n * LQ:(n + 1) * LQ]
    repC = lambda n: rep_sb[:, (16 + n) * LQ:(17 + n) * LQ]
    selB = lambda n: sel_sb[:, n * 128:(n + 1) * 128]
    selC = lambda n: sel_sb[:, (16 + n) * 128:(17 + n) * 128]
    selR = lambda j: sel_sb[:, (33 + j) * 128:(34 + j) * 128]
    da_s = lambda i: da_sb[:, (i % 2) * LQ:(i % 2 + 1) * LQ]
    dbx_s = lambda i: dbx_sb[:, (i % 2) * LQ:(i % 2 + 1) * LQ]
    h_s = lambda i: h_sb[:, (i % 2) * LQ:(i % 2 + 1) * LQ]
    z_s = lambda i: z_sb[:, (i % 2) * LQ:(i % 2 + 1) * LQ]
    et_s = lambda i: etmp_sb[:, (i % 2) * LQ:(i % 2 + 1) * LQ]
    def wintile(eb, k):
        base = ((eb // 2) % 3) * 2048
        off = base + k * 256 + (eb % 2) * 128
        return winrg_sb[:, off:off + 128]
    wotslot = lambda s: wotrg_sb[:, (s % 2) * 2048:(s % 2 + 1) * 2048]
    cvslot = lambda g: cvrg_sb[:, ((g // 2) % 2) * 1024 + (g % 2) * 512:
                                ((g // 2) % 2) * 1024 + (g % 2 + 1) * 512]

    e_sel = S.op("sync", lambda e: e.dma_start(out=sel_sb[:], in_=dsel.ap()),
                 inc="s_in", amt=16)
    e_idf = S.op("sync", lambda e: e.dma_start(out=idf_sb[:], in_=didf.ap()),
                 inc="s_in", amt=16)
    e_alog = S.op("sync", lambda e: e.dma_start(
        out=x_sb[:, 0:GI * NST], in_=dalog.ap()), inc="s_in", amt=16)
    e_cols = S.op("sync", lambda e: e.dma_start(out=cols_sb[:],
                  in_=dcols.ap()), inc="s_in", amt=16)
    e_fcols = S.op("sync", lambda e: e.dma_start(out=fc_sb[:],
                   in_=dfcols.ap()), inc="s_in", amt=16)
    e_wx = S.op("sync", lambda e: e.dma_start(
        out=x_sb[:, 512:512 + GI * 96], in_=dwx.ap()), inc="s_in", amt=16)
    e_wdt = S.op("sync", lambda e: e.dma_start(
        out=AP(x_sb, 2048, [[KD * LQ, 128], [1, DI]]), in_=dwdt.ap()),
        inc="s_in", amt=16)
    e_cc0 = S.op("vector", lambda e: e.memset(cc_sb[:, 0:1], EPS_LN), inc="s_v")
    e_cc1 = S.op("vector", lambda e: e.memset(cc_sb[:, 1:2], 1.0), inc="s_v")
    e_cc2 = S.op("vector", lambda e: e.memset(cc_sb[:, 2:3], -0.5), inc="s_v")
    e_cc3 = S.op("vector", lambda e: e.memset(cc_sb[:, 3:4], -1.0),
                 inc="s_v")
    e_a1 = S.op("scalar", lambda e: e.activation(
        out=a_sb[:], in_=x_sb[:, 0:GI * NST], func=AF.Exp),
        waits=[e_alog], inc="s_s")
    e_a2 = S.op("scalar", lambda e: e.activation(
        out=a_sb[:], in_=a_sb[:], func=AF.Copy, scale=cc_sb[:, 3:4]),
        waits=[e_cc3], inc="s_s")
    e_wxc = S.op("vector", lambda e: e.tensor_copy(
        wx_sb[:], x_sb[:, 512:512 + GI * 96]), waits=[e_wx], inc="s_v")
    e_wdtc = S.op("vector", lambda e: e.tensor_copy(
        wdt_sb[:], AP(x_sb, 2048, [[KD * LQ, 128], [1, DI]])),
        waits=[e_wdt], inc="s_v")
    e_ms1 = S.op("vector", lambda e: e.memset(onesd_sb[:], ONE_OVER_D),
                 inc="s_v")
    e_ms2 = S.op("vector", lambda e: e.memset(halo_sb[:], 0.0), inc="s_v")
    e_ms3 = S.op("vector", lambda e: e.memset(xdbl_sb[:], 0.0), inc="s_v")
    e_ms4 = S.op("vector", lambda e: e.memset(qcol_sb[:], 0.0), inc="s_v")
    e_ms5 = S.op("vector", lambda e: e.memset(gpad_sb[:], 0.0), inc="s_v")
    e_ms6 = S.op("vector", lambda e: e.memset(m2pad_sb[:], 0.0), inc="s_v")

    stg_cast_ev = {}
    stg_store_ev = {}
    stg_idx = [0]
    win_piece_ev = {}

    def _dst(src_dram):
        return {id(dconvd): dcvd_bf, id(dwin): dwin_bf,
                id(dwout): dwout_bf}[id(src_dram)]

    def _rl(src_dram):
        return {id(dconvd): GI * 4 * 128, id(dwin): E,
                id(dwout): D}[id(src_dram)]

    def stage_x(src_dram, off):
        i = stg_idx[0]
        stg_idx[0] += 1
        sl = i % 2
        evl = S.op("sync", lambda e, sl=sl, off=off, src_dram=src_dram:
                   e.dma_start(out=x_sb[:, sl * 2048:(sl + 1) * 2048],
                               in_=AP(src_dram, off,
                                      [[_rl(src_dram), 128], [1, 2048]])),
                   waits=[stg_cast_ev.get(i - 2)], inc="s_in", amt=16)
        evc = S.op("gpsimd", lambda e, sl=sl: e.tensor_copy(
            rep_sb[:, sl * 2048:(sl + 1) * 2048],
            x_sb[:, sl * 2048:(sl + 1) * 2048]),
            waits=[evl, stg_store_ev.get(i - 2)], inc="s_g")
        stg_cast_ev[i] = evc
        evs = S.op("gpsimd", lambda e, sl=sl, off=off, src_dram=src_dram:
                   e.dma_start(out=AP(_dst(src_dram), off,
                                      [[_rl(src_dram), 128], [1, 2048]]),
                               in_=rep_sb[:, sl * 2048:(sl + 1) * 2048]),
                   waits=[evc], inc="s_gd", amt=16)
        stg_store_ev[i] = evs
        return evs

    for p in range(4):
        stage_x(dconvd, p * 2048)
    e_cvd_st = {1: stg_store_ev[3]}
    for j in range(KD):
        ev = stage_x(dwin, j * 128 * E)
        for c in range(4):
            win_piece_ev[(j, c)] = ev
    ph1_cast_done = stg_cast_ev[stg_idx[0] - 1]

    e_x0 = S.op("gpsimd", lambda e: e.dma_start(
        out=AP(x_sb, 0, [[KD * LQ, 128], [LQ, KD], [1, LQ]]),
        in_=AP(dx, 0, [[L, 128], [128 * L, KD], [1, LQ]])),
        waits=[e_a1, e_wxc, e_wdtc, ph1_cast_done], inc="s_gd", amt=16)

    fst2 = [etmp_sb[:, 0:LQ], etmp_sb[:, LQ:2 * LQ]]
    h1_cast = {}
    h1_store = {}
    h1_idx = [0]

    def stage_h1(off):
        i = h1_idx[0]
        h1_idx[0] += 1
        evl = S.op("gpsimd", lambda e, off=off: e.dma_start(
            out=etmp_sb[:, 0:2 * LQ], in_=AP(dwin, off, [[E, 128],
                                                         [1, 2 * LQ]])),
            waits=[h1_cast.get(i - 1)], inc="s_gd", amt=16)
        evc = S.op("gpsimd", lambda e: e.tensor_copy(
            rep_sb[:, 0:2 * LQ], etmp_sb[:, 0:2 * LQ]),
            waits=[evl, h1_store.get(i - 1)], inc="s_g")
        h1_cast[i] = evc
        evs = S.op("gpsimd", lambda e, off=off: e.dma_start(
            out=AP(dwin_bf, off, [[E, 128], [1, 2 * LQ]]),
            in_=rep_sb[:, 0:2 * LQ]),
            waits=[evc], inc="s_gd", amt=16)
        h1_store[i] = evs
        return evs

    for c in (4, 6):
        for j in range(KD):
            ev = stage_h1(j * 128 * E + c * LQ)
            win_piece_ev[(j, c)] = ev
            win_piece_ev[(j, c + 1)] = ev
    wo_ev = {}
    wo_store = {}
    fw_cast = {}
    fw_store = {}
    late_done = [None]

    def emit_late_staging(e_inv1, e_xln, e_t1_free):
        land = [lnt_sb[:, 0:LQ], inv_sb[:, 0:LQ]]
        lgu = [e_inv1, e_xln]
        lc = {}
        ls = {}
        li = [0]

        def st2(off):
            i = li[0]
            li[0] += 1
            sl = i % 2
            cs = i % 4
            evl = S.op("gpsimd", lambda e, sl=sl, off=off: e.dma_start(
                out=land[sl], in_=AP(dwout, off, [[D, 128], [1, LQ]])),
                waits=[lc.get(i - 2), lgu[sl] if i < 2 else None],
                inc="s_gd", amt=16)
            evc = S.op("gpsimd", lambda e, cs=cs, sl=sl: e.tensor_copy(
                t1_sb[:, cs * LQ:(cs + 1) * LQ], land[sl]),
                waits=[evl, ls.get(i - 4),
                       e_t1_free if i < 4 else None], inc="s_g")
            lc[i] = evc
            evs = S.op("gpsimd", lambda e, cs=cs, off=off: e.dma_start(
                out=AP(dwout_bf, off, [[D, 128], [1, LQ]]),
                in_=t1_sb[:, cs * LQ:(cs + 1) * LQ]),
                waits=[evc], inc="s_gd", amt=16)
            ls[i] = evs
            return evs

        for j in range(16):
            for c in range(2):
                wo_ev[(j, c)] = st2(j * 128 * D + c * LQ)
        wo_store[15] = wo_ev[(15, 1)]
        for j in range(8):
            evl = S.op("gpsimd", lambda e, j=j: e.dma_start(
                out=i8t_sb[:, (j % 2) * 1024:(j % 2 + 1) * 1024],
                in_=AP(dfw8, j * 128 * D, [[D, 128], [1, D]])),
                waits=[fw_cast.get(j - 2)], inc="s_gd", amt=16)
            fw_cast[j] = S.op("gpsimd", lambda e, j=j: e.tensor_copy(
                t1_sb[:, 2048 + (j % 2) * 1024:2048 + (j % 2 + 1) * 1024],
                i8t_sb[:, (j % 2) * 1024:(j % 2 + 1) * 1024]),
                waits=[evl, fw_store.get(j - 2),
                       e_t1_free if j < 2 else None],
                inc="s_g")
            fw_store[j] = S.op("gpsimd", lambda e, j=j: e.dma_start(
                out=AP(dfw_bf, j * 128 * D, [[D, 128], [1, D]]),
                in_=t1_sb[:, 2048 + (j % 2) * 1024:
                          2048 + (j % 2 + 1) * 1024]),
                waits=[fw_cast[j]], inc="s_gd", amt=16)
        late_done[0] = fw_store[7]

    startup_x_free = []
    ev_fstage_done = h1_cast[h1_idx[0] - 1]
    ev_stores_done = h1_store[h1_idx[0] - 1]
    rep_startup_guard = [ev_stores_done]
    startup_done = [e_sel, e_idf, e_a2, e_cols, e_fcols, e_ms1]

    last_win_rd = {}
    last_wot_rd = {}
    last_cv_rd = {}
    prev_conv_done = None
    prev_scan_slot = {}
    prev_rep_reader = None
    prev_da_last = None
    ev_xres_store = {}
    prev_xguard = startup_x_free + [e_ms2, e_ms3] + startup_done
    prev_rowdma = None
    prev_inproj_last = None
    qrow_tmp = [None]

    for q in range(NQ):
        qrow_tmp[0] = qrow_sb[0:1, 0:LQ]
        S.stage = 1
        if q == 0:
            e_x = e_x0
        else:
            e_x = S.op("gpsimd", lambda e, q=q: e.dma_start(
                out=AP(x_sb, 0, [[KD * LQ, 128], [LQ, KD], [1, LQ]]),
                in_=AP(dx, q * LQ, [[L, 128], [128 * L, KD], [1, LQ]])),
                waits=prev_xguard, inc="s_gd", amt=16)
        e_cast = None
        for k in range(KD):
            e_cast = S.op("vector", lambda e, k=k: e.tensor_copy(
                dtg(k), xk(k)), waits=[e_x, prev_da_last], inc="s_v")
        e_m = None
        for k in range(KD):
            e_m = S.op("tensor", lambda e, k=k: e.matmul(
                ps[0][:], onesd_sb[:], dtg(k),
                start=(k == 0), stop=(k == KD - 1)),
                waits=[e_cast, bank_free[0], e_ms1], inc="s_t")
        e_t1 = None
        for k in range(KD):
            e_t1 = S.op("vector", lambda e, k=k: e.tensor_tensor(
                out=t1k(k), in0=xk(k), in1=ps[0][:], op=ALU.subtract),
                waits=[e_m, late_done[0] if q == 1 else None], inc="s_v")
        e_sq = None
        for k in range(KD):
            e_sq = S.op("scalar", lambda e, k=k: e.activation(
                out=dtg(k), in_=t1k(k), func=AF.Square),
                waits=[e_t1, e_m], inc="s_s")
        e_v = None
        for k in range(KD):
            e_v = S.op("tensor", lambda e, k=k: e.matmul(
                ps[1][:], onesd_sb[:], dtg(k),
                start=(k == 0), stop=(k == KD - 1)),
                waits=[e_sq, bank_free[1]], inc="s_t")
        e_ln1 = S.op("scalar", lambda e: e.activation(
            out=lnt_sb[:], in_=ps[1][:], func=AF.Ln, bias=cc_sb[:, 0:1]),
            waits=[e_v, e_cc0, e_cc2], inc="s_s")
        e_inv1 = S.op("scalar", lambda e: e.activation(
            out=inv_sb[:], in_=lnt_sb[:], func=AF.Exp, scale=cc_sb[:, 2:3]),
            inc="s_s")
        bank_free[1] = e_ln1
        e_xln = None
        for k in range(KD):
            e_xln = S.op("vector", lambda e, k=k: e.tensor_tensor(
                out=t1k(k), in0=t1k(k), in1=inv_sb[:], op=ALU.mult),
                waits=[e_inv1], inc="s_v")
        bank_free[0] = e_t1

        S.stage = 2
        xc_rdy = {}
        cv_last_ld = [None]
        cv_mm_ev = {}
        g_done = {}
        xconv_done = {}
        ip_free = {2: bank_free[2], 3: bank_free[3]}
        cv_free = {4: bank_free[4], 5: bank_free[5]}
        e_conv_mm_last = None

        def emit_conv(g, q=q):
            nonlocal e_conv_mm_last
            bk = 4 + g % 2
            if g % 2 == 0:
                cv_last_ld[0] = S.op("sync", lambda e, g=g: e.dma_start(
                    out=cvrg_sb[:, ((g // 2) % 2) * 1024:
                                ((g // 2) % 2 + 1) * 1024],
                    in_=AP(dcvd_bf, g * 4 * 128,
                           [[GI * 4 * 128, 128], [1, 1024]])),
                    waits=[last_cv_rd.get((g // 2) % 2),
                           e_cvd_st[1] if q == 0 and g < 2 else None],
                    inc="s_in", amt=16)
            e_cl = cv_last_ld[0]
            e_cm = None
            for kk in range(4):
                e_cm = S.op("tensor", lambda e, g=g, kk=kk, bk=bk: e.matmul(
                    ps[bk][:], cvslot(g)[:, kk * 128:(kk + 1) * 128],
                    xcs(g, kk, kk + LQ),
                    start=(kk == 0), stop=(kk == 3)),
                    waits=[xc_rdy[g], e_cl, cv_free.get(bk)], inc="s_t")
            last_cv_rd[(g // 2) % 2] = e_cm
            cv_mm_ev[g] = e_cm
            e_cs = S.op("scalar", lambda e, g=g, bk=bk: e.activation(
                out=xcvg(g), in_=ps[bk][:], func=AF.Silu,
                bias=convb_c(g)), waits=[e_cm, e_cols], inc="s_s")
            cv_free[bk] = e_cs
            xconv_done[g] = e_cs
            e_conv_mm_last = e_cm

        last_ld = [None]
        for eb in range(NEB):
            if eb >= 4 and (eb - 4) < GI:
                emit_conv(eb - 4)
            if eb % 2 == 0:
                last_ld[0] = S.op("sync", lambda e, eb=eb: e.dma_start(
                    out=AP(winrg_sb, ((eb // 2) % 3) * 2048,
                           [[3 * 2048, 128], [256, KD], [1, 256]]),
                    in_=AP(dwin_bf, eb * 128,
                           [[E, 128], [128 * E, KD], [1, 256]])),
                    waits=[last_win_rd.get((eb // 2) % 3),
                           win_piece_ev[(KD - 1, (eb + 1) // 4)]
                           if q == 0 else None],
                    inc="s_in", amt=16)
            e_ld = last_ld[0]
            bk = 2 + eb % 2
            e_mm = None
            for k in range(KD):
                e_mm = S.op("tensor", lambda e, eb=eb, k=k, bk=bk: e.matmul(
                    ps[bk][:], wintile(eb, k), t1k(k),
                    start=(k == 0), stop=(k == KD - 1)),
                    waits=[e_ld, e_xln, ip_free.get(bk)], inc="s_t")
            last_win_rd[(eb // 2) % 3] = e_mm
            if eb < 16:
                g = eb
                slot_guard = cv_mm_ev.get(g - 4)
                if q > 0 and g < 4:
                    slot_guard = prev_conv_done
                e_e = S.op("vector", lambda e, g=g, bk=bk: e.tensor_copy(
                    xcs(g, 3, LQ + 3), ps[bk][:]),
                    waits=[e_mm, slot_guard], inc="s_v")
                S.op("vector", lambda e, g=g: e.tensor_copy(
                    xcs(g, 0, 3), halo_sb[:, g * 3:g * 3 + 3]),
                    waits=[e_ms2], inc="s_v")
                if q < NQ - 1:
                    e_sv = S.op("vector", lambda e, g=g: e.tensor_copy(
                        halo_sb[:, g * 3:g * 3 + 3], xcs(g, LQ, LQ + 3)),
                        inc="s_v")
                else:
                    e_sv = S.op("vector", lambda e: e.engine_nop(),
                                inc="s_v")
                xc_rdy[g] = e_sv
                ip_free[bk] = e_e
            else:
                e_e = S.op("scalar", lambda e, eb=eb, bk=bk: e.activation(
                    out=gg(eb - 16), in_=ps[bk][:], func=AF.Silu),
                    waits=[e_mm], inc="s_s")
                g_done[eb - 16] = e_e
                ip_free[bk] = e_e
        for g in range(GI - 4, GI):
            emit_conv(g)
        bank_free[2] = ip_free[2]
        bank_free[3] = ip_free[3]
        bank_free[4] = cv_free[4]
        bank_free[5] = cv_free[5]
        prev_conv_done = e_conv_mm_last
        prev_inproj_last = last_win_rd[((NEB - 1) // 2) % 3]

        e_wxm = None
        for g in range(GI):
            e_wxm = S.op("tensor", lambda e, g=g: e.matmul(
                ps[6][0:96, :], wx_sb[:, g * 96:(g + 1) * 96], xcvg(g),
                start=(g == 0), stop=(g == GI - 1)),
                waits=[xconv_done[g], e_wxc, bank_free[6]], inc="s_t")
        e_xdbl = S.op("vector", lambda e: e.tensor_copy(
            xdbl_sb[0:96, :], ps[6][0:96, :]),
            waits=[e_wxm, e_ms3], inc="s_v")
        bank_free[6] = e_xdbl

        dt_done = {}
        dt_free = {7: bank_free[7], 0: bank_free[0]}
        for g in range(GI):
            bk = 7 if g % 2 == 0 else 0
            e_dm = S.op("tensor", lambda e, g=g, bk=bk: e.matmul(
                ps[bk][:], wdt_sb[:, g * 128:(g + 1) * 128],
                xdbl_sb[:], start=True, stop=True),
                waits=[e_xdbl, e_wdtc, dt_free.get(bk),
                       e_inv1 if bk == 0 else None], inc="s_t")
            e_de = S.op("scalar", lambda e, g=g, bk=bk: e.activation(
                out=et_s(g), in_=ps[bk][:], func=AF.Exp, bias=bdt_c(g)),
                waits=[e_dm, ev_fstage_done if q == 0 and g < 2 else None],
                inc="s_s")
            e_dl = S.op("scalar", lambda e, g=g: e.activation(
                out=dtg(g), in_=et_s(g), func=AF.Ln, bias=cc_sb[:, 1:2]),
                waits=[e_cc1], inc="s_s")
            dt_free[bk] = e_de
            dt_done[g] = e_dl
        bank_free[7] = dt_free[7]
        bank_free[0] = dt_free[0]

        xcd_done = {}
        u_done = {}
        for g in range(GI):
            xcd_done[g] = S.op("vector", lambda e, g=g: e.tensor_scalar(
                out=yg(g), in0=xcvg(g), scalar1=dp_c(g), scalar2=None,
                op0=ALU.mult), waits=[xconv_done[g], e_cols], inc="s_v")
            u_done[g] = S.op("vector", lambda e, g=g: e.tensor_tensor(
                out=xcvg(g), in0=dtg(g), in1=xcvg(g), op=ALU.mult),
                waits=[dt_done[g]], inc="s_v")

        repB_ev = {}
        repC_ev = {}
        rb_free = bank_free[6]
        rc_free = bank_free[7]
        for n in range(NST):
            e_rb = S.op("tensor", lambda e, n=n: e.matmul(
                ps[6][:], selB(n), xdbl_sb[:], start=True, stop=True),
                waits=[e_xdbl, e_sel, rb_free], inc="s_t")
            e_rbe = S.op("scalar", lambda e, n=n: e.copy(
                out=repB(n), in_=ps[6][:]),
                waits=[e_rb, prev_rep_reader] +
                      (rep_startup_guard if q == 0 else []), inc="s_s")
            e_rc = S.op("tensor", lambda e, n=n: e.matmul(
                ps[7][:], selC(n), xdbl_sb[:], start=True, stop=True),
                waits=[rc_free], inc="s_t")
            e_rce = S.op("scalar", lambda e, n=n: e.copy(
                out=repC(n), in_=ps[7][:]),
                waits=[e_rc, prev_rep_reader] +
                      (rep_startup_guard if q == 0 else []), inc="s_s")
            rb_free, rc_free = e_rbe, e_rce
            repB_ev[n] = e_rbe
            repC_ev[n] = e_rce
        bank_free[6] = rb_free
        bank_free[7] = rc_free

        S.stage = 4
        gate_ev = {}
        scan_ev = {}
        zmm_ev = {}
        da_ev = {}
        y_free = {2: bank_free[2], 3: bank_free[3]}

        def emit_da(j, q=q):
            if j >= GI * NST:
                return
            gj, nj = j // NST, j % NST
            da_ev[j] = S.op("scalar", lambda e, gj=gj, nj=nj, j=j:
                            e.activation(out=da_s(j), in_=dtg(gj),
                                         func=AF.Exp, scale=acol(gj, nj)),
                            waits=[dt_done[gj], e_a2,
                                   scan_ev.get(j - 2,
                                               prev_scan_slot.get(j % 2))],
                            inc="s_s")

        emit_da(0)
        emit_da(1)
        ii = 0
        for g in range(GI):
            bk = 2 + g % 2
            S.op("tensor", lambda e, g=g, bk=bk: e.matmul(
                ps[bk][:], idb(), yg(g), start=True, stop=False),
                waits=[xcd_done[g], y_free.get(bk)], inc="s_t")
            for n in range(NST):
                i = ii
                ii += 1
                e_da = da_ev[i]
                S.stage = 64
                S.op("vector", lambda e, g=g, n=n, i=i: e.tensor_tensor(
                    out=dbx_s(i), in0=xcvg(g), in1=repB(n), op=ALU.mult),
                    waits=[u_done[g], repB_ev[n]], inc="s_v")
                S.stage = 128
                if q == 0:
                    e_h = S.op("vector", lambda e, i=i:
                               e.tensor_tensor_scan(
                                   out=h_s(i), data0=da_s(i), data1=dbx_s(i),
                                   initial=0.0, op0=ALU.mult, op1=ALU.add),
                               waits=[e_da], inc="s_v")
                else:
                    cr = ((q - 1) % 2) * GI * NST + g * NST + n
                    e_h = S.op("vector", lambda e, cr=cr, i=i:
                               e.tensor_tensor_scan(
                                   out=h_s(i), data0=da_s(i), data1=dbx_s(i),
                                   initial=carry_sb[:, cr:cr + 1],
                                   op0=ALU.mult, op1=ALU.add),
                               waits=[e_da], inc="s_v")
                scan_ev[i] = e_h
                emit_da(i + 2)
                S.stage = 256
                if q < NQ - 1:
                    cw = (q % 2) * GI * NST + g * NST + n
                    S.op("scalar", lambda e, cw=cw, i=i: e.copy(
                        out=carry_sb[:, cw:cw + 1],
                        in_=h_s(i)[:, LQ - 1:LQ]),
                        waits=[scan_ev[i]], inc="s_s")
                S.stage = 512
                e_z = S.op("vector", lambda e, n=n, i=i: e.tensor_tensor(
                    out=z_s(i), in0=h_s(i), in1=repC(n), op=ALU.mult),
                    waits=[repC_ev[n], zmm_ev.get(i - 2)], inc="s_v")
                zmm_ev[i] = S.op("tensor",
                                 lambda e, i=i, n=n, bk=bk: e.matmul(
                                     ps[bk][:], idb(), z_s(i),
                                     start=False, stop=(n == NST - 1)),
                                 waits=[e_z], inc="s_t")
            S.stage = 1024
            e_gate = S.op("vector", lambda e, g=g, bk=bk: e.tensor_tensor(
                out=yg(g), in0=ps[bk][:], in1=gg(g), op=ALU.mult),
                waits=[zmm_ev[ii - 1], g_done[g]], inc="s_v")
            S.stage = 4
            y_free[bk] = e_gate
            gate_ev[g] = e_gate
        bank_free[2] = y_free[2]
        bank_free[3] = y_free[3]
        prev_scan_slot = {0: scan_ev[ii - 2], 1: scan_ev[ii - 1]}
        prev_rep_reader = gate_ev[GI - 1]
        if q == 0:
            emit_late_staging(e_inv1, e_xln, prev_inproj_last)
        prev_da_last = da_ev[GI * NST - 1]

        S.stage = 8
        op_evac = {}
        op_free = {4: bank_free[4], 5: bank_free[5]}
        for dm in range(KD):
            e_wol = S.op("gpsimd", lambda e, dm=dm: e.dma_start(
                out=AP(wotrg_sb, (dm % 2) * 2048,
                       [[2 * 2048, 128], [128, GI], [1, 128]]),
                in_=AP(dwout_bf, dm * 128,
                       [[D, 128], [128 * D, GI], [1, 128]])),
                waits=[last_wot_rd.get(dm % 2)] +
                      ([wo_store[15]] if q == 0 and dm < 2 else []),
                inc="s_gd", amt=16)
            bk = 4 + dm % 2
            e_om = None
            for g in range(GI):
                e_om = S.op("tensor", lambda e, dm=dm, g=g, bk=bk: e.matmul(
                    ps[bk][:], wotslot(dm)[:, g * 128:(g + 1) * 128], yg(g),
                    start=(g == 0), stop=(g == GI - 1)),
                    waits=[e_wol, gate_ev[g], op_free.get(bk)], inc="s_t")
            last_wot_rd[dm % 2] = e_om
            e_oe = S.op("vector", lambda e, dm=dm, bk=bk: e.tensor_tensor(
                out=xk(dm), in0=ps[bk][:], in1=xk(dm), op=ALU.add),
                waits=[e_om], inc="s_v")
            op_evac[dm] = e_oe
            op_free[bk] = e_oe
        bank_free[4] = op_free[4]
        bank_free[5] = op_free[5]

        e_cast2 = None
        for k in range(KD):
            e_cast2 = S.op("vector", lambda e, k=k: e.tensor_copy(
                dtg(k), xk(k)), waits=[op_evac[k], prev_da_last], inc="s_v")
        e_m2 = None
        for k in range(KD):
            e_m2 = S.op("tensor", lambda e, k=k: e.matmul(
                ps[0][:], onesd_sb[:], dtg(k),
                start=(k == 0), stop=(k == KD - 1)),
                waits=[e_cast2, bank_free[0]], inc="s_t")
        e_sq2 = None
        for k in range(KD):
            e_sq2 = S.op("scalar", lambda e, k=k: e.activation(
                out=xcvg(k), in_=dtg(k), func=AF.Square),
                waits=[e_cast2, prev_rep_reader], inc="s_s")
        e_v2 = None
        for k in range(KD):
            e_v2 = S.op("tensor", lambda e, k=k: e.matmul(
                ps[1][:], onesd_sb[:], xcvg(k),
                start=(k == 0), stop=(k == KD - 1)),
                waits=[e_sq2, bank_free[1]], inc="s_t")
        e_c12 = S.op("scalar", lambda e: e.activation(
            out=lnt_sb[:], in_=ps[0][:], func=AF.Square),
            waits=[e_m2, late_done[0] if q == 0 else None], inc="s_s")
        e_c22 = S.op("vector", lambda e: e.tensor_tensor(
            out=lnt_sb[:], in0=ps[1][:], in1=lnt_sb[:], op=ALU.subtract),
            waits=[e_v2, e_c12], inc="s_v")
        e_ln2 = S.op("scalar", lambda e: e.activation(
            out=ps[1][:], in_=lnt_sb[:], func=AF.Ln, bias=cc_sb[:, 0:1]),
            waits=[e_c22], inc="s_s")
        e_inv2 = S.op("scalar", lambda e: e.activation(
            out=inv_sb[:], in_=ps[1][:], func=AF.Exp, scale=cc_sb[:, 2:3]),
            waits=[late_done[0] if q == 0 else None], inc="s_s")
        bank_free[1] = e_inv2
        e_r1 = S.op("vector", lambda e, q=q: e.tensor_copy(
            m2pad_sb[0:1, q * LQ:(q + 1) * LQ], ps[0][0:1, :]),
            waits=[e_m2, e_ms6, prev_rowdma], inc="s_v")
        e_r2 = S.op("vector", lambda e, q=q: e.tensor_tensor(
            out=padtmp_sb[0:1, 0:LQ], in0=ps[0][0:1, :],
            in1=m2pad_sb[0:1, q * LQ:(q + 1) * LQ], op=ALU.subtract),
            inc="s_v")
        e_r3 = S.op("vector", lambda e, q=q: e.tensor_copy(
            padtmp_sb[0:1, LQ:2 * LQ], inv_sb[0:1, :]),
            waits=[e_inv2], inc="s_v")
        e_r4 = S.op("vector", lambda e, q=q: e.tensor_tensor(
            out=padtmp_sb[0:1, 0:LQ] if False else qrow_tmp[0],
            in0=inv_sb[0:1, :], in1=padtmp_sb[0:1, LQ:2 * LQ],
            op=ALU.subtract), inc="s_v")
        e_d1 = S.op("gpsimd", lambda e, q=q: e.dma_start(
            out=AP(m2pad_sb, 32 * L + q * LQ, [[L, 1], [1, LQ]]),
            in_=padtmp_sb[0:1, 0:LQ]), waits=[e_r2], inc="s_gd", amt=16)
        e_d2 = S.op("gpsimd", lambda e, q=q: e.dma_start(
            out=AP(m2pad_sb, 64 * L + q * LQ, [[L, 1], [1, LQ]]),
            in_=padtmp_sb[0:1, LQ:2 * LQ]), waits=[e_r3], inc="s_gd", amt=16)
        e_d3 = S.op("gpsimd", lambda e, q=q: e.dma_start(
            out=AP(m2pad_sb, 96 * L + q * LQ, [[L, 1], [1, LQ]]),
            in_=qrow_tmp[0]), waits=[e_r4], inc="s_gd", amt=16)
        prev_rowdma = e_d3
        e_qp = None
        for k in range(KD):
            S.op("vector", lambda e, k=k: e.tensor_tensor(
                out=et_s(k), in0=xk(k), in1=ps[0][:], op=ALU.subtract),
                waits=[e_inv2], inc="s_v")
            S.op("vector", lambda e, k=k: e.tensor_tensor(
                out=et_s(k), in0=et_s(k), in1=inv_sb[:], op=ALU.mult),
                inc="s_v")
            S.op("vector", lambda e, k=k: e.tensor_reduce(
                out=qcol_sb[:, 1:2], in_=et_s(k), axis=mybir.AxisListType.X,
                op=ALU.max, apply_absolute_value=True),
                waits=[e_ms4], inc="s_v")
            e_qp = S.op("vector", lambda e: e.tensor_tensor(
                out=qcol_sb[:, 0:1], in0=qcol_sb[:, 0:1],
                in1=qcol_sb[:, 1:2], op=ALU.max), inc="s_v")
        bank_free[0] = e_qp
        ev_xres_store[q] = S.op("gpsimd", lambda e, q=q: e.dma_start(
            out=AP(dxres, q * LQ, [[L, 128], [128 * L, KD], [1, LQ]]),
            in_=AP(x_sb, 0, [[KD * LQ, 128], [LQ, KD], [1, LQ]])),
            waits=[op_evac[KD - 1]], inc="s_gd", amt=16)
        prev_xguard = [ev_xres_store[q], e_qp, e_r1, e_d3]
        prev_da_last = e_sq2

    S.stage = 2048
    e_tr = S.op("tensor", lambda e: e.transpose(
        ps[6][0:1, 0:128], qcol_sb[:, 0:1], idf_sb[:]),
        waits=[e_qp, e_idf, bank_free[6]], inc="s_t")
    e_gm = S.op("vector", lambda e: e.tensor_reduce(
        out=smal_sb[0:1, 0:1], in_=ps[6][0:1, 0:128],
        axis=mybir.AxisListType.X, op=ALU.max), waits=[e_tr], inc="s_v")
    bank_free[6] = e_gm
    e_st = S.op("gpsimd", lambda e: e.dma_start(
        out=AP(dccin, 0, [[1, 1], [1, 1]]), in_=smal_sb[0:1, 0:1]),
        waits=[e_gm], inc="s_gd", amt=16)
    e_cc = S.op("gpsimd", lambda e: e.collective_compute(
        "AllReduce", ALU.max, replica_groups=[list(range(8))],
        ins=[dccin.ap().opt()], outs=[dccout.ap().opt()]),
        waits=[e_st], inc="s_cc")
    e_ccl = S.op("gpsimd", lambda e: e.dma_start(
        out=smal_sb[0:1, 2:3], in_=dccout.ap()), waits=[e_cc],
        inc="s_gd", amt=16)
    e_xs = S.op("vector", lambda e: e.tensor_scalar(
        out=smal_sb[0:1, 1:2], in0=smal_sb[0:1, 2:3], scalar1=1.0 / 127.0,
        scalar2=EPS_FFN, op0=ALU.mult, op1=ALU.max),
        waits=[e_ccl], inc="s_v")
    e_is = S.op("vector", lambda e: e.reciprocal(
        out=smal_sb[0:1, 3:4], in_=smal_sb[0:1, 1:2]), inc="s_v")
    e_gp1 = S.op("vector", lambda e: e.tensor_copy(
        gpad_sb[0:1, 0:1], smal_sb[0:1, 1:2]),
        waits=[e_ms5, e_xs], inc="s_v")
    e_gp2 = S.op("vector", lambda e: e.tensor_tensor(
        out=gpad_sb[0:1, 1:2], in0=smal_sb[0:1, 1:2], in1=gpad_sb[0:1, 0:1],
        op=ALU.subtract), inc="s_v")
    e_gp3 = S.op("vector", lambda e: e.tensor_copy(
        gpad_sb[0:1, 2:3], smal_sb[0:1, 3:4]), waits=[e_is], inc="s_v")
    e_gp4 = S.op("vector", lambda e: e.tensor_tensor(
        out=gpad_sb[0:1, 3:4], in0=smal_sb[0:1, 3:4], in1=gpad_sb[0:1, 2:3],
        op=ALU.subtract), inc="s_v")
    e_xsc = None
    for j in range(2):
        e_xsc = S.op("tensor", lambda e, j=j: e.matmul(
            ps[7][:, 0:1], selR(0), gpad_sb[:, j:j + 1],
            start=(j == 0), stop=(j == 1)),
            waits=[e_gp2, e_sel, bank_free[7]], inc="s_t")
    e_wsx = S.op("vector", lambda e: e.tensor_scalar(
        out=wsx_sb[:], in0=fc_sb[:, 0:8], scalar1=ps[7][:, 0:1],
        scalar2=None, op0=ALU.mult), waits=[e_xsc, e_fcols], inc="s_v")
    bank_free[7] = e_wsx
    e_isc = None
    for j in range(2):
        e_isc = S.op("tensor", lambda e, j=j: e.matmul(
            ps[6][:, 0:1], selR(0), gpad_sb[:, 2 + j:3 + j],
            start=(j == 0), stop=(j == 1)),
            waits=[e_gp4, bank_free[6]], inc="s_t")
    e_iscol = S.op("vector", lambda e: e.tensor_copy(
        iscol_sb[:], ps[6][:, 0:1]), waits=[e_isc], inc="s_v")
    bank_free[6] = e_iscol
    e_fw = S.op("sync", lambda e: e.dma_start(
        out=rep_sb[:, 0:8192],
        in_=AP(dfw_bf, 0, [[D, 128], [128 * D, 8], [1, D]])),
        waits=[prev_rep_reader, fw_store[7]], inc="s_in", amt=16)
    fwt = lambda k, nb: rep_sb[:, k * 1024 + nb * 128:
                               k * 1024 + (nb + 1) * 128]

    S.stage = 16
    prev_ffn_done = None
    ps0_free = bank_free[0]
    ps1_free = bank_free[1]
    ffn_free = {2: bank_free[2], 3: bank_free[3]}
    for q in range(NQ):
        e_xl = S.op("gpsimd", lambda e, q=q: e.dma_start(
            out=AP(x_sb, 0, [[KD * LQ, 128], [LQ, KD], [1, LQ]]),
            in_=AP(dxres, q * LQ, [[L, 128], [128 * L, KD], [1, LQ]])),
            waits=[e_iscol, e_wsx, prev_ffn_done, e_qp if q == 0 else None],
            inc="s_gd", amt=16)
        e_m2r = None
        for j in range(2):
            e_m2r = S.op("tensor", lambda e, q=q, j=j: e.matmul(
                ps[0][:], selR(j), m2pad_sb[:, q * LQ:(q + 1) * LQ],
                start=(j == 0), stop=(j == 1)),
                waits=[prev_rowdma, ps0_free], inc="s_t")
        e_sir = None
        for j in range(2):
            e_sir = S.op("tensor", lambda e, q=q, j=j: e.matmul(
                ps[1][:], selR(2 + j), m2pad_sb[:, q * LQ:(q + 1) * LQ],
                start=(j == 0), stop=(j == 1)),
                waits=[prev_rowdma, ps1_free], inc="s_t")
        e_sirs = S.op("vector", lambda e: e.tensor_scalar(
            out=inv_sb[:], in0=ps[1][:], scalar1=iscol_sb[:, 0:1],
            scalar2=None, op0=ALU.mult), waits=[e_sir, e_iscol], inc="s_v")
        e_xq = {}
        for k in range(KD):
            S.op("vector", lambda e, k=k: e.tensor_tensor(
                out=et_s(k), in0=xk(k), in1=ps[0][:], op=ALU.subtract),
                waits=[e_xl, e_m2r, prev_ffn_done], inc="s_v")
            S.op("vector", lambda e, k=k: e.tensor_tensor(
                out=et_s(k), in0=et_s(k), in1=inv_sb[:], op=ALU.mult),
                waits=[e_sirs], inc="s_v")
            e_xq[k] = S.op("vector", lambda e, k=k: e.tensor_scalar(
                out=dtg(k), in0=et_s(k), scalar1=MAGIC, scalar2=MAGIC,
                op0=ALU.add, op1=ALU.subtract), inc="s_v")
        ps0_free = e_xq[KD - 1]
        ps1_free = e_sirs
        e_fin = {}
        for nb in range(KD):
            bk = 2 + nb % 2
            e_fm = None
            for k in range(KD):
                e_fm = S.op("tensor", lambda e, nb=nb, k=k, bk=bk: e.matmul(
                    ps[bk][:], fwt(k, nb), dtg(k),
                    start=(k == 0), stop=(k == KD - 1)),
                    waits=[e_fw, e_xq[k], ffn_free.get(bk)], inc="s_t")
            e_fe = S.op("scalar", lambda e, nb=nb, bk=bk: e.activation(
                out=et_s(nb), in_=ps[bk][:], func=AF.Copy,
                scale=wsx_sb[:, nb:nb + 1]),
                waits=[e_fm, e_wsx, e_xq[KD - 1], e_fin.get(nb - 2)],
                inc="s_s")
            e_fa = S.op("vector", lambda e, nb=nb: e.scalar_tensor_tensor(
                out=xk(nb), in0=et_s(nb), scalar=fc_sb[:, 8 + nb:9 + nb],
                in1=xk(nb), op0=ALU.add, op1=ALU.add),
                waits=[e_fe, e_fcols], inc="s_v")
            ffn_free[bk] = e_fa
            e_fin[nb] = e_fa
        prev_ffn_done = e_fin[KD - 1]
        S.op("gpsimd", lambda e, q=q: e.dma_start(
            out=AP(dout, q * LQ, [[L, 128], [128 * L, KD], [1, LQ]]),
            in_=AP(x_sb, 0, [[KD * LQ, 128], [LQ, KD], [1, LQ]])),
            waits=[e_fin[KD - 1]], inc="s_gd", amt=16)

    S.op("vector", lambda e: e.tensor_copy(
        etmp_sb[:, 0:2 * GI * NST], carry_sb[:]), inc="s_v")
    S.op("gpsimd", lambda e: e.dma_start(
        out=ddbg.ap(), in_=etmp_sb[:, 0:2 * GI * NST]),
        waits=[("s_v", S.cnt["s_v"])], inc="s_gd", amt=16)
    final_gd = S.cnt["s_gd"]
    S.op("gpsimd", lambda e: e.nop(), waits=[("s_gd", final_gd)], inc=None)

    with es:
        block = es.enter_context(nc.Block())

        def replay(eng_name):
            def _(eng):
                hw = {}
                for (en, fn, waits, inc, amt) in S.ops:
                    if en != eng_name:
                        continue
                    for (sn, val) in waits:
                        if hw.get(sn, 0) < val:
                            eng.wait_ge(sems[sn], val)
                            hw[sn] = val
                    instr = fn(eng)
                    if inc is not None:
                        instr.then_inc(sems[inc], amt)
                    if KDBG and eng_name != "sync":
                        try:
                            eng.drain()
                        except Exception:
                            pass
            return _

        block.sync(replay("sync"))
        block.vector(replay("vector"))
        block.scalar(replay("scalar"))
        block.tensor(replay("tensor"))
        block.gpsimd(replay("gpsimd"))
    return nc


def kernel(L0_attn_x, L0_attn_W_in, L0_attn_conv_w, L0_attn_conv_b,
           L0_attn_W_x, L0_attn_W_dt, L0_attn_b_dt, L0_attn_A_log,
           L0_attn_Dp, L0_attn_W_out, L0_ffn_weight_int8,
           L0_ffn_weight_scale, L0_ffn_bias):
    import ml_dtypes
    x = np.asarray(L0_attn_x, dtype=np.float32)
    W_in = np.asarray(L0_attn_W_in, dtype=np.float32)
    conv_w = np.asarray(L0_attn_conv_w, dtype=np.float32)
    conv_b = np.asarray(L0_attn_conv_b, dtype=np.float32)
    W_x = np.asarray(L0_attn_W_x, dtype=np.float32)
    W_dt = np.asarray(L0_attn_W_dt, dtype=np.float32)
    b_dt = np.asarray(L0_attn_b_dt, dtype=np.float32)
    A_log = np.asarray(L0_attn_A_log, dtype=np.float32)
    Dp = np.asarray(L0_attn_Dp, dtype=np.float32)
    W_out = np.asarray(L0_attn_W_out, dtype=np.float32)
    fw8 = np.asarray(L0_ffn_weight_int8)
    fws = np.asarray(L0_ffn_weight_scale, dtype=np.float32)
    fb = np.asarray(L0_ffn_bias, dtype=np.float32)

    x_fm = np.ascontiguousarray(np.swapaxes(x, 1, 2))
    winT = np.ascontiguousarray(W_in.T)
    woutT = np.ascontiguousarray(W_out.T)
    wxT_t = np.ascontiguousarray(
        W_x.T.reshape(GI, 128, 96).transpose(1, 0, 2).reshape(128, GI * 96))
    wdtT = np.zeros((128, DI), np.float32)
    wdtT[0:RNK, :] = W_dt.T
    fwT8 = np.ascontiguousarray(fw8.T)
    convd = np.zeros((128, GI * 4 * 128), np.float32)
    for g in range(GI):
        for kk in range(4):
            blk = convd[:, (g * 4 + kk) * 128:(g * 4 + kk + 1) * 128]
            np.fill_diagonal(blk, conv_w[g * 128:(g + 1) * 128, kk])
    sel = np.zeros((128, 37 * 128), ml_dtypes.bfloat16)
    for n in range(NST):
        sel[64 + n, n * 128:(n + 1) * 128] = 1
        sel[80 + n, (16 + n) * 128:(17 + n) * 128] = 1
    sel[:, 32 * 128:33 * 128] = np.eye(128, dtype=ml_dtypes.bfloat16)
    for j, row in enumerate((0, 32, 64, 96)):
        sel[row, (33 + j) * 128:(34 + j) * 128] = 1
    identf = np.eye(128, dtype=np.float32)
    alog_fm = np.ascontiguousarray(
        A_log.reshape(GI, 128, NST).transpose(1, 0, 2).reshape(128, GI * NST))
    colify = lambda v: np.ascontiguousarray(v.reshape(GI, 128).T)
    cols = np.concatenate([colify(conv_b), colify(b_dt), colify(Dp)], axis=1)
    colify8 = lambda v: np.ascontiguousarray(v.reshape(8, 128).T)
    fcols = np.concatenate([colify8(fws[:, 0]), colify8(fb)], axis=1)

    nc = build_nc()
    in_maps = []
    for c in range(8):
        in_maps.append({
            "x_fm": np.ascontiguousarray(x_fm[c]),
            "winT": winT, "woutT": woutT, "wxT_t": wxT_t, "wdtT": wdtT,
            "convd": convd, "sel_bf": sel,
            "ident_f32": identf, "alog_fm": alog_fm, "cols": cols,
            "fcols": fcols, "fwT8": fwT8,
        })
    res = run_bass_kernel_spmd(nc, in_maps, core_ids=list(range(8)))
    out_fm = np.stack([np.asarray(r["out"]) for r in res.results])
    return np.ascontiguousarray(np.swapaxes(out_fm, 1, 2))


if __name__ == "__main__":
    pass
